# version 27
# baseline (speedup 1.0000x reference)
import numpy as np
import ml_dtypes
import concourse.bass as bass
import concourse.mybir as mybir
from concourse.bass_utils import run_bass_kernel_spmd

F32 = mybir.dt.float32
BF16 = mybir.dt.bfloat16
AF = mybir.ActivationFunctionType
ALU = mybir.AluOpType
AX = mybir.AxisListType
BF = ml_dtypes.bfloat16

D = 1024
S = 2048
NSEQ = 4
NTOK = NSEQ * S
NT = 512
NTILES = NTOK // NT
DC = D // 128
H = 16
HD = 64


class Prog:

    def __init__(self, nc, es, n_dma_sems=48):
        self.nc = nc
        self.es = es
        self.q_eng = {
            "pe": nc.tensor, "act": nc.scalar, "dve": nc.vector,
            "pool": nc.gpsimd, "sp": nc.sync,
            "sp_dma": nc.sync, "act_dma": nc.scalar, "pool_dma": nc.gpsimd,
        }
        self.qsem = {q: es.enter_context(nc.semaphore(f"q_{q}"))
                     for q in ("pe", "act", "dve", "pool", "sp")}
        self.cnt = {q: 0 for q in self.qsem}
        self.dma_pool = [es.enter_context(nc.semaphore(f"dsem{i}"))
                         for i in range(n_dma_sems)]
        self.dma_free = list(self.dma_pool)
        self.semval = {id(s): 0 for s in self.dma_pool}
        self.buf_sem = {}
        self.state = {}
        self.waited = {}

    def _key(self, buf):
        if isinstance(buf, tuple):
            return (id(buf[0]), buf[1])
        return id(buf)

    def _tid(self, buf):
        return id(buf[0]) if isinstance(buf, tuple) else id(buf)

    def _bufstate(self, buf):
        k = self._key(buf)
        if k not in self.state:
            self.state[k] = {"w": None, "r": {}}
        return self.state[k]

    def _dma_sem(self, buf, q):
        t = (self._tid(buf), "sw" if q == "pool_dma" else "hw")
        if t not in self.buf_sem:
            if not self.dma_free:
                raise RuntimeError("out of DMA semaphores; call recycle_dma_sems()")
            self.buf_sem[t] = self.dma_free.pop()
        return self.buf_sem[t]

    def _wait(self, q, sem, val, dep_q):
        if val <= 0:
            return
        if dep_q is None:
            val = max(val, self.semval[id(sem)])
        if self.waited.get((q, id(sem)), 0) >= val:
            return
        self.q_eng[q].wait_ge(sem, val)
        self.waited[(q, id(sem))] = val

    SAME_Q_SKIP = {"pe", "act"}

    def _dep_wait(self, q, dep):
        sem, val, dep_q = dep
        if dep_q == q and q in self.SAME_Q_SKIP:
            return
        self._wait(q, sem, val, dep_q if not (dep_q and dep_q.endswith("_dma")) else None)

    def do(self, q, fn, reads=(), writes=()):
        for b in reads:
            st = self._bufstate(b)
            if st["w"] is not None:
                self._dep_wait(q, st["w"])
        for b in writes:
            st = self._bufstate(b)
            if st["w"] is not None:
                self._dep_wait(q, st["w"])
            for dep in st["r"].values():
                self._dep_wait(q, dep)
        inst = fn(self.q_eng[q])
        if q.endswith("_dma"):
            sem = self._dma_sem(writes[0], q)
            self.semval[id(sem)] += 16
            val = self.semval[id(sem)]
            rec = (sem, val, q)
        else:
            self.cnt[q] += 1
            val = self.cnt[q]
            sem = self.qsem[q]
            rec = (sem, val, q)
        inst.then_inc(sem, 16 if q.endswith("_dma") else 1)
        for b in reads:
            st = self._bufstate(b)
            st["r"][id(sem)] = rec
        for b in writes:
            st = self._bufstate(b)
            st["w"] = rec
            st["r"] = {}
        return inst

    def barrier(self):
        for q in self.qsem:
            for q2, c in self.cnt.items():
                if q2 != q:
                    self._wait(q, self.qsem[q2], c, q2)
            for s in self.dma_pool:
                self._wait(q, s, self.semval[id(s)], None)

    def recycle_dma_sems(self):
        self.buf_sem = {}
        self.dma_free = list(self.dma_pool)

    def finish(self, q="pe"):
        for q2, c in self.cnt.items():
            if q2 != q:
                self._wait(q, self.qsem[q2], c, q2)
        for s in self.dma_pool:
            self._wait(q, s, self.semval[id(s)], None)


_NAME_SEQ = [0]


def _uname(name):
    _NAME_SEQ[0] += 1
    return f"{name}_{_NAME_SEQ[0]}"


def sb(P, name, shape, dtype):
    return P.es.enter_context(P.nc.sbuf_tensor(_uname(name), shape, dtype))


def ps(P, name, shape=(128, 512), dtype=F32):
    return P.es.enter_context(P.nc.psum_tensor(_uname(name), list(shape), dtype))


def emit_linear(P, out_psum_fn, lhsT_fn, rhs_fn, n_k, reads, tag=""):
    def fn(t):
        last = None
        for k in range(n_k):
            last = t.matmul(out_psum_fn(), lhsT_fn(k), rhs_fn(k),
                            start=(k == 0), stop=(k == n_k - 1))
        return last
    return fn


class Consts:

    def __init__(self, P):
        nc = P.nc
        self.onesc = sb(P, "c_onesc", [128, 1], BF16)
        self.onesr = sb(P, "c_onesr", [1, 128], F32)
        self.iota8 = sb(P, "c_iota8", [128, 8], F32)
        self.ident = sb(P, "c_ident", [128, 128], F32)
        h_onesc = nc.inline_tensor(np.ones((128, 1), dtype=BF), name=_uname("h_onesc"))
        h_onesr = nc.inline_tensor(np.ones((1, 128), dtype=np.float32), name=_uname("h_onesr"))
        self.onesrb = sb(P, "c_onesrb", [1, 128], BF16)
        h_onesrb = nc.inline_tensor(np.ones((1, 128), dtype=BF), name=_uname("h_onesrb"))
        h_iota8 = nc.inline_tensor(
            np.tile(np.arange(8, dtype=np.float32), (128, 1)).copy(), name=_uname("h_iota8"))
        h_ident = nc.inline_tensor(np.eye(128, dtype=np.float32), name=_uname("h_ident"))
        self.sel8 = sb(P, "c_sel8", [8, 8 * 128], BF16)
        sel_np = np.zeros((8, 8 * 128), dtype=BF)
        for e in range(8):
            sel_np[e, e * 128:(e + 1) * 128] = 1.0
        h_sel8 = nc.inline_tensor(sel_np, name=_uname("h_sel8"))
        for dst, src in ((self.onesc, h_onesc), (self.onesr, h_onesr),
                         (self.onesrb, h_onesrb),
                         (self.iota8, h_iota8), (self.ident, h_ident),
                         (self.sel8, h_sel8)):
            P.do("sp_dma", lambda e, d=dst, s=src: e.dma_start(out=d[:], in_=s.ap()),
                 writes=[d for d in (dst,)])
        P._ln_rows = sb(P, "ln_rows", [1, 3 * NT], F32)


def emit_stats_ln(P, C, xb, sq, stat_ps, brd_ps, aB, bB, eps=1e-5, n_chunks=DC,
                  rms_only=False, denom=None):
    nt = NT
    dn = denom if denom is not None else n_chunks * 128
    P.do("dve", lambda v: v.tensor_tensor(sq[:], xb[:], xb[:], ALU.mult),
         reads=[xb], writes=[sq])
    if not rms_only:
        P.do("pe", emit_linear(P, lambda: stat_ps[0:1, 0:nt],
                               lambda k: C.onesc[:],
                               lambda k: xb[:, k * nt:(k + 1) * nt], n_chunks, None),
             reads=[xb, C.onesc], writes=[stat_ps])
    P.do("pe", emit_linear(P, lambda: stat_ps[64:65, 0:nt],
                           lambda k: C.onesc[:],
                           lambda k: sq[:, k * nt:(k + 1) * nt], n_chunks, None),
         reads=[sq, C.onesc], writes=[stat_ps])
    row = getattr(P, "_ln_rows", None)
    if row is None:
        row = sb(P, "ln_rows", [1, 3 * NT], F32)
        P._ln_rows = row
    m = row[0:1, 0:nt]
    v2 = row[0:1, nt:2 * nt]
    inv = row[0:1, 2 * nt:3 * nt]
    if rms_only:
        P.do("dve", lambda v: v.tensor_scalar(v2, stat_ps[64:65, 0:nt],
                                              1.0 / dn, eps, ALU.mult, ALU.add),
             reads=[stat_ps], writes=[row])
        P.do("act", lambda s: s.activation(inv, v2, AF.Ln), reads=[row], writes=[row])
        P.do("act", lambda s: s.activation(inv, inv, AF.Exp, scale=-0.5),
             reads=[row], writes=[row])
    else:
        P.do("dve", lambda v: v.tensor_scalar_mul(m, stat_ps[0:1, 0:nt], 1.0 / dn),
             reads=[stat_ps], writes=[row])
        P.do("dve", lambda v: v.tensor_scalar_mul(v2, stat_ps[64:65, 0:nt], 1.0 / dn),
             reads=[stat_ps], writes=[row])
        P.do("dve", lambda v: v.tensor_mul(inv, m, m), reads=[row], writes=[row])
        P.do("dve", lambda v: v.tensor_sub(v2, v2, inv), reads=[row], writes=[row])
        P.do("dve", lambda v: v.tensor_scalar_add(v2, v2, eps), reads=[row], writes=[row])
        P.do("act", lambda s: s.activation(inv, v2, AF.Ln), reads=[row], writes=[row])
        P.do("act", lambda s: s.activation(inv, inv, AF.Exp, scale=-0.5),
             reads=[row], writes=[row])
        P.do("dve", lambda v: v.tensor_mul(m, m, inv), reads=[row], writes=[row])
        P.do("dve", lambda v: v.tensor_scalar_mul(m, m, -1.0), reads=[row], writes=[row])
    P.do("pe", lambda t: t.matmul(brd_ps[:, 0:nt], C.onesr[:], inv,
                                  start=True, stop=True),
         reads=[row, C.onesr], writes=[brd_ps])
    P.do("act", lambda s: s.copy(aB[:], brd_ps[:, 0:nt]), reads=[brd_ps], writes=[aB])
    if not rms_only:
        P.do("pe", lambda t: t.matmul(brd_ps[:, 0:nt], C.onesr[:], m,
                                      start=True, stop=True),
             reads=[row, C.onesr], writes=[brd_ps])
        P.do("act", lambda s: s.copy(bB[:], brd_ps[:, 0:nt]), reads=[brd_ps], writes=[bB])


def emit_ln_apply(P, xb, aB, bB, xn, n_chunks=DC):
    for c in range(n_chunks):
        sl = slice(c * NT, (c + 1) * NT)
        P.do("dve", lambda v, sl=sl: v.tensor_tensor(xn[:, sl], xb[:, sl], aB[:], ALU.mult),
             reads=[xb, aB], writes=[xn])
        if bB is not None:
            P.do("dve", lambda v, sl=sl: v.tensor_tensor(xn[:, sl], xn[:, sl], bB[:], ALU.add),
                 reads=[xn, bB], writes=[xn])


def load_wT(P, dst_sb, hbm_ap, q="pool_dma", col0=0):
    r, c = hbm_ap.shape
    if r == 128:
        P.do("sp_dma", lambda e: e.dma_start(out=dst_sb[:, col0:col0 + c], in_=hbm_ap),
             writes=[dst_sb])
        return
    n = r // 128
    src = hbm_ap.rearrange("(n p) c -> p n c", p=128)
    dst = dst_sb[:, col0:col0 + n * c].rearrange("p (n c) -> p n c", c=c)
    P.do(q, lambda e: e.dma_start(out=dst, in_=src), writes=[dst_sb])


def emit_dma_tile(P, q, sbuf, hbm, t, write=False, subkeys=None):
    src = hbm.rearrange("(c p) n -> p c n", p=128)[:, :, t * NT:(t + 1) * NT]
    keys = [sbuf] if subkeys is None else [(sbuf, k) for k in subkeys]
    if write:
        P.do(q, lambda e: e.dma_start(out=src, in_=sbuf[:].rearrange("p (c n) -> p c n", n=NT)),
             reads=keys, writes=[(hbm.tensor, t)])
    else:
        P.do(q, lambda e: e.dma_start(out=sbuf[:].rearrange("p (c n) -> p c n", n=NT), in_=src),
             reads=[(hbm.tensor, t)], writes=keys)


class L0FFNBufs:
    def __init__(self, P):
        self.gateT = sb(P, "b_gateT", [128, DC * 8], BF16)
        self.corr = sb(P, "b_corr", [128, 32], F32)
        self.iota32 = sb(P, "b_iota32", [128, 32], F32)
        self.fc1T = sb(P, "b_fc1T", [128, DC * 512], BF16)
        self.w1T = sb(P, "b_w1T", [128, 8 * 2048], BF16)
        self.w2T = sb(P, "b_w2T", [128, 8 * 2048], BF16)
        self.fc2T = sb(P, "b_fc2T", [128, 4 * 1024], BF16)
        self.xf = [sb(P, f"b_xf{i}", [128, DC * NT], F32) for i in range(2)]
        self.xb = sb(P, "b_xb", [128, DC * NT], BF16)
        self.aB = sb(P, "b_aB", [128, NT], BF16)
        self.bB = sb(P, "b_bB", [128, NT], BF16)
        self.xn = sb(P, "b_xn", [128, DC * NT], BF16)
        self.s_sc = sb(P, "b_ssc", [128, 32], F32)
        self.r1 = sb(P, "b_r1", [128, 32], F32)
        self.r2 = sb(P, "b_r2", [128, 32], F32)
        self.r3 = sb(P, "b_r3", [128, 32], F32)
        self.rs = sb(P, "b_rs", [128, 6 * 4], F32)
        self.cw = sb(P, "b_cw", [128, 32], F32)
        self.cwT = sb(P, "b_cwT", [8, NT], BF16)
        self.cwB = sb(P, "b_cwB", [128, NT], BF16)
        self.latT = sb(P, "b_latT", [128, 4 * NT], BF16)
        self.hr2 = sb(P, "b_hr2", [128, 2 * NT], BF16)
        self.hs = sb(P, "b_hs", [128, 4 * NT], BF16)
        self.combT = sb(P, "b_combT", [128, 4 * NT], BF16)
        self.xo = sb(P, "b_xo", [128, DC * NT], F32)
        self.pb_comb = ps(P, "pb_comb", (128, 4 * 512))
        self.pb_h = ps(P, "pb_h01", (128, 2 * 512))
        self.pb_bc = ps(P, "pb_bc")
        self.pb_s = ps(P, "pb_s")


def load_l0ffn_weights(P, B, hb):
    load_wT(P, B.gateT, hb["gateT"].ap())
    load_wT(P, B.fc1T, hb["fc1T"].ap())
    for e in range(8):
        load_wT(P, B.w1T, hb["w1T"].ap()[e], col0=e * 2048)
        load_wT(P, B.w2T, hb["w2T"].ap()[e], col0=e * 2048)
    load_wT(P, B.fc2T, hb["fc2T"].ap())
    P.do("sp_dma", lambda e: e.dma_start(out=B.corr[:], in_=hb["corr4"].ap()),
         writes=[B.corr])
    P.do("sp_dma", lambda e: e.dma_start(out=B.iota32[:], in_=hb["iota32"].ap()),
         writes=[B.iota32])


def emit_routing_l0(P, C, B):
    v = "dve"
    sc, r1, r2, r3, cw = B.s_sc, B.r1, B.r2, B.r3, B.cw
    rs = B.rs
    def grp(buf, g):
        return buf[:].rearrange("p (tc e) -> p tc e", e=8)[:, :, g * 4:(g + 1) * 4]
    def full(buf):
        return buf[:].rearrange("p (tc e) -> p tc e", e=8)
    def col(buf, j):
        return buf[:, j:32:8]
    def rsv(i):
        return rs[:, i * 4:(i + 1) * 4]
    def rsb(i, n=8):
        return rs[:, i * 4:(i + 1) * 4].unsqueeze(2).broadcast_to([128, 4, n])

    P.do("act", lambda s: s.activation(sc[:], B.pb_s[:, 0:32], AF.Sigmoid),
         reads=[B.pb_s], writes=[sc])
    P.do(v, lambda e: e.tensor_tensor(r1[:], sc[:], B.corr[:], ALU.add),
         reads=[sc, B.corr], writes=[r1])
    for g in range(2):
        first = True
        for i in range(4):
            for j in range(i + 1, 4):
                pair = rsv(5)
                P.do(v, lambda e, i=i, j=j, g=g, pair=pair: e.tensor_tensor(
                    pair, col(r1, g * 4 + i), col(r1, g * 4 + j), ALU.add),
                    reads=[r1], writes=[rs])
                if first:
                    P.do(v, lambda e, g=g, pair=pair: e.tensor_copy(rsv(g), pair),
                         reads=[rs], writes=[rs])
                    first = False
                else:
                    P.do(v, lambda e, g=g, pair=pair: e.tensor_tensor(
                        rsv(g), rsv(g), pair, ALU.max), reads=[rs], writes=[rs])
    P.do(v, lambda e: e.tensor_tensor(rsv(2), rsv(0), rsv(1), ALU.is_ge),
         reads=[rs], writes=[rs])
    P.do(v, lambda e: e.tensor_tensor(grp(r2, 0), grp(r1, 0), rsb(2, 4), ALU.mult),
         reads=[r1, rs], writes=[r2])
    P.do(v, lambda e: e.tensor_scalar(rsv(3), rsv(2), -1.0, 1.0, ALU.mult, ALU.add),
         reads=[rs], writes=[rs])
    P.do(v, lambda e: e.tensor_tensor(grp(r2, 1), grp(r1, 1), rsb(3, 4), ALU.mult),
         reads=[r1, rs], writes=[r2])
    P.do(v, lambda e: e.tensor_reduce(rsv(0), full(r2), axis=AX.X, op=ALU.max),
         reads=[r2], writes=[rs])
    P.do(v, lambda e: e.tensor_tensor(full(r3), full(r2), rsb(0), ALU.is_ge),
         reads=[r2, rs], writes=[r3])
    P.do(v, lambda e: e.scalar_tensor_tensor(r3[:], r3[:], -999.0, B.iota32[:],
                                             ALU.mult, ALU.add),
         reads=[r3, B.iota32], writes=[r3])
    P.do(v, lambda e: e.tensor_reduce(rsv(1), full(r3), axis=AX.X, op=ALU.min),
         reads=[r3], writes=[rs])
    P.do(v, lambda e: e.tensor_scalar_add(rsv(1), rsv(1), 999.0), reads=[rs], writes=[rs])
    P.do(v, lambda e: e.tensor_tensor(full(r3), full(B.iota32), rsb(1), ALU.is_equal),
         reads=[B.iota32, rs], writes=[r3])
    P.do(v, lambda e: e.scalar_tensor_tensor(cw[:], r3[:], -1e30, r2[:], ALU.mult, ALU.add),
         reads=[r3, r2], writes=[cw])
    P.do(v, lambda e: e.tensor_reduce(rsv(2), full(cw), axis=AX.X, op=ALU.max),
         reads=[cw], writes=[rs])
    P.do(v, lambda e: e.tensor_tensor(full(r2), full(cw), rsb(2), ALU.is_ge),
         reads=[cw, rs], writes=[r2])
    P.do(v, lambda e: e.scalar_tensor_tensor(r2[:], r2[:], -999.0, B.iota32[:],
                                             ALU.mult, ALU.add),
         reads=[r2, B.iota32], writes=[r2])
    P.do(v, lambda e: e.tensor_reduce(rsv(3), full(r2), axis=AX.X, op=ALU.min),
         reads=[r2], writes=[rs])
    P.do(v, lambda e: e.tensor_scalar_add(rsv(3), rsv(3), 999.0), reads=[rs], writes=[rs])
    P.do(v, lambda e: e.tensor_tensor(full(r2), full(B.iota32), rsb(3), ALU.is_equal),
         reads=[B.iota32, rs], writes=[r2])
    P.do(v, lambda e: e.tensor_tensor(cw[:], r3[:], sc[:], ALU.mult),
         reads=[r3, sc], writes=[cw])
    P.do(v, lambda e: e.tensor_reduce(rsv(0), full(cw), axis=AX.X, op=ALU.add),
         reads=[cw], writes=[rs])
    P.do(v, lambda e: e.tensor_tensor(cw[:], r2[:], sc[:], ALU.mult),
         reads=[r2, sc], writes=[cw])
    P.do(v, lambda e: e.tensor_reduce(rsv(2), full(cw), axis=AX.X, op=ALU.add),
         reads=[cw], writes=[rs])
    P.do(v, lambda e: e.tensor_tensor(rsv(4), rsv(0), rsv(2), ALU.add),
         reads=[rs], writes=[rs])
    P.do(v, lambda e: e.tensor_scalar_add(rsv(4), rsv(4), 1e-20), reads=[rs], writes=[rs])
    P.do(v, lambda e: e.reciprocal(rsv(5), rsv(4)), reads=[rs], writes=[rs])
    P.do(v, lambda e: e.tensor_tensor(rsv(4), rsv(0), rsv(5), ALU.mult),
         reads=[rs], writes=[rs])
    P.do(v, lambda e: e.tensor_tensor(rsv(5), rsv(2), rsv(5), ALU.mult),
         reads=[rs], writes=[rs])
    P.do(v, lambda e: e.tensor_tensor(full(r3), full(r3), rsb(4), ALU.mult),
         reads=[r3, rs], writes=[r3])
    P.do(v, lambda e: e.tensor_tensor(full(r2), full(r2), rsb(5), ALU.mult),
         reads=[r2, rs], writes=[r2])
    P.do(v, lambda e: e.tensor_tensor(cw[:], r3[:], r2[:], ALU.add),
         reads=[r3, r2], writes=[cw])
    for tc in range(4):
        P.do("pe", lambda t, tc=tc: t.transpose(B.pb_s[0:8, 128:256], cw[:, tc * 8:tc * 8 + 8],
                                                C.ident[:]),
             reads=[cw, C.ident], writes=[B.pb_s])
        P.do(v, lambda e, tc=tc: e.tensor_copy(B.cwT[0:8, tc * 128:(tc + 1) * 128],
                                               B.pb_s[0:8, 128:256]),
             reads=[B.pb_s], writes=[B.cwT])


def emit_phase_l0ffn_routed(P, C, B, x_in, x_out, xn_spill, tiles=range(NTILES)):
    for t in tiles:
        xf = B.xf[t % 2]
        emit_dma_tile(P, "sp_dma", xf, x_in, t)
        P.do("dve", lambda v: v.tensor_copy(B.xb[:], xf[:]), reads=[xf], writes=[B.xb])
        emit_stats_ln(P, C, B.xb, B.xn, B.pb_s, B.pb_bc, B.aB, B.bB)
        emit_ln_apply(P, B.xb, B.aB, B.bB, B.xn)
        emit_dma_tile(P, "pool_dma", B.xn, xn_spill, t, write=True)

        def routing():
            for tc in range(4):
                P.do("pe", emit_linear(
                    P, lambda tc=tc: B.pb_s[:, tc * 8:(tc + 1) * 8],
                    lambda k, tc=tc: B.xn[:, k * NT + tc * 128:k * NT + tc * 128 + 128],
                    lambda k: B.gateT[:, k * 8:(k + 1) * 8], DC, None),
                    reads=[B.xn, B.gateT], writes=[B.pb_s])
            emit_routing_l0(P, C, B)
        for lc in range(4):
            pb = B.pb_h[:, (lc % 2) * 512:(lc % 2) * 512 + 512]
            hk = (B.pb_h, lc % 2)
            P.do("pe", emit_linear(
                P, lambda pb=pb: pb,
                lambda k, lc=lc: B.fc1T[:, k * 512 + lc * 128:k * 512 + lc * 128 + 128],
                lambda k: B.xn[:, k * NT:(k + 1) * NT], DC, None),
                reads=[B.xn, B.fc1T], writes=[hk])
            P.do("act", lambda s, lc=lc, pb=pb: s.copy(B.latT[:, lc * NT:(lc + 1) * NT], pb),
                 reads=[hk], writes=[B.latT])
            if lc == 0:
                routing()
        for e in range(8):
            P.do("pe", lambda t_, e=e: t_.matmul(B.pb_bc[:], C.sel8[:, e * 128:(e + 1) * 128],
                                                 B.cwT[:], start=True, stop=True),
                 reads=[B.cwT, C.sel8], writes=[B.pb_bc])
            P.do("act", lambda s: s.copy(B.cwB[:], B.pb_bc[:]),
                 reads=[B.pb_bc], writes=[B.cwB])
            def he1pair(p, e=e):
                def f(t_):
                    last = None
                    for icx in range(2):
                        ic = 2 * p + icx
                        for k in range(4):
                            last = t_.matmul(
                                B.pb_h[:, icx * 512:(icx + 1) * 512],
                                B.w1T[:, e * 2048 + k * 512 + ic * 128:
                                      e * 2048 + k * 512 + ic * 128 + 128],
                                B.latT[:, k * NT:(k + 1) * NT],
                                start=(k == 0), stop=(k == 3))
                    return last
                P.do("pe", f, reads=[B.latT, B.w1T],
                     writes=[(B.pb_h, 0), (B.pb_h, 1)])
                P.do("act", lambda s: s.activation(B.hr2[:], B.pb_h[:], AF.Relu),
                     reads=[(B.pb_h, 0), (B.pb_h, 1)], writes=[B.hr2])
                sl = slice(2 * p * NT, (2 * p + 2) * NT)
                P.do("dve", lambda v, sl=sl: v.tensor_tensor(
                    B.hs[:, sl], B.hr2[:], B.hr2[:], ALU.mult),
                    reads=[B.hr2], writes=[(B.hs, p)])
                P.do("dve", lambda v, sl=sl: v.tensor_tensor(
                    B.hs[:, sl].rearrange("p (a n) -> p a n", n=NT),
                    B.hs[:, sl].rearrange("p (a n) -> p a n", n=NT),
                    B.cwB[:].unsqueeze(1).broadcast_to([128, 2, NT]), ALU.mult),
                    reads=[(B.hs, p), B.cwB], writes=[(B.hs, p)])

            def he2pair(p, e=e):
                def f(t_):
                    last = None
                    for icx in range(2):
                        ic = 2 * p + icx
                        for lc in range(4):
                            last = t_.matmul(
                                B.pb_comb[:, lc * 512:(lc + 1) * 512],
                                B.w2T[:, e * 2048 + ic * 512 + lc * 128:
                                      e * 2048 + ic * 512 + lc * 128 + 128],
                                B.hs[:, ic * NT:(ic + 1) * NT],
                                start=(e == 0 and ic == 0), stop=(e == 7 and ic == 3))
                    return last
                P.do("pe", f, reads=[(B.hs, p), B.w2T], writes=[B.pb_comb])

            he1pair(0)
            he1pair(1)
            he2pair(0)
            he2pair(1)
        for lc in range(4):
            P.do("act", lambda s, lc=lc: s.copy(B.combT[:, lc * NT:(lc + 1) * NT],
                                                B.pb_comb[:, lc * 512:(lc + 1) * 512]),
                 reads=[B.pb_comb], writes=[B.combT])
        for co in range(8):
            pb = B.pb_h[:, (co % 2) * 512:(co % 2) * 512 + 512]
            hk = (B.pb_h, co % 2)
            P.do("pe", emit_linear(
                P, lambda pb=pb: pb,
                lambda k, co=co: B.fc2T[:, k * 1024 + co * 128:k * 1024 + co * 128 + 128],
                lambda k: B.combT[:, k * NT:(k + 1) * NT], 4, None),
                reads=[B.combT, B.fc2T], writes=[hk])
            P.do("dve", lambda v, co=co, pb=pb, xf=xf: v.tensor_tensor(
                B.xo[:, co * NT:(co + 1) * NT], pb, xf[:, co * NT:(co + 1) * NT], ALU.add),
                reads=[hk, xf], writes=[B.xo])
        emit_dma_tile(P, "sp_dma", B.xo, x_out, t, write=True)


class SharedBufs:
    def __init__(self, P):
        self.upT = sb(P, "s_upT", [128, DC * 2048], BF16)
        self.downT = sb(P, "s_downT", [128, 16 * 1024], BF16)
        self.xn = [sb(P, f"s_xn{i}", [128, DC * NT], BF16) for i in range(2)]
        self.xf = [sb(P, f"s_xf{i}", [128, DC * NT], F32) for i in range(2)]
        self.ur = sb(P, "s_ur", [128, NT], BF16)
        self.us = sb(P, "s_us", [128, 16 * NT], BF16)
        self.xo = sb(P, "s_xo", [128, DC * NT], F32)
        self.pb = [ps(P, f"s_pb{i}") for i in range(4)]


def load_shared_weights(P, B, hb, half):
    load_wT(P, B.upT, hb["upT"].ap()[:, half * 2048:(half + 1) * 2048])
    load_wT(P, B.downT, hb["downT"].ap()[half * 2048:(half + 1) * 2048, :])


def emit_phase_l0ffn_shared(P, C, B, x_in, x_out, xn_spill, half, tiles=range(NTILES)):
    for t in tiles:
        xn = B.xn[t % 2]
        xf = B.xf[t % 2]
        emit_dma_tile(P, "pool_dma", xn, xn_spill, t)
        emit_dma_tile(P, "sp_dma", xf, x_in, t)
        for fc in range(16):
            pb = B.pb[fc % 2]
            P.do("pe", emit_linear(
                P, lambda pb=pb: pb[:],
                lambda k, fc=fc: B.upT[:, k * 2048 + fc * 128:k * 2048 + fc * 128 + 128],
                lambda k, xn=xn: xn[:, k * NT:(k + 1) * NT], DC, None),
                reads=[xn, B.upT], writes=[pb])
            P.do("act", lambda s, pb=pb: s.activation(B.ur[:], pb[:], AF.Relu),
                 reads=[pb], writes=[B.ur])
            P.do("dve", lambda v, fc=fc: v.tensor_tensor(
                B.us[:, fc * NT:(fc + 1) * NT], B.ur[:], B.ur[:], ALU.mult),
                reads=[B.ur], writes=[B.us])
        for co in range(8):
            pb = B.pb[2 + co % 2]
            P.do("pe", emit_linear(
                P, lambda pb=pb: pb[:],
                lambda k, co=co: B.downT[:, k * 1024 + co * 128:k * 1024 + co * 128 + 128],
                lambda k: B.us[:, k * NT:(k + 1) * NT], 16, None),
                reads=[B.us, B.downT], writes=[pb])
            P.do("dve", lambda v, co=co, pb=pb, xf=xf: v.tensor_tensor(
                B.xo[:, co * NT:(co + 1) * NT], pb[:], xf[:, co * NT:(co + 1) * NT], ALU.add),
                reads=[pb, xf], writes=[B.xo])
        emit_dma_tile(P, "sp_dma", B.xo, x_out, t, write=True)


class AttnBufs:
    def __init__(self, P, rope: bool):
        self.Wa = sb(P, "a_Wa", [128, DC * 1024], BF16)
        self.Wb = sb(P, "a_Wb", [128, DC * 1024], BF16)
        self.xf = [sb(P, f"a_xf{i}", [128, DC * NT], F32) for i in range(2)]
        self.aB = sb(P, "a_aB", [128, NT], BF16)
        self.bB = sb(P, "a_bB", [128, NT], BF16)
        self.xn = sb(P, "a_xn", [128, DC * NT], BF16)
        self.KT = sb(P, "a_KT", [128, DC * S], BF16)
        self.Vaug = sb(P, "a_Vaug", [128, 16 * (16 * 65)], BF16)
        self.QT = sb(P, "a_QT", [128, DC * NT], BF16)
        self.ET = [sb(P, f"a_ET{i}", [128, 4 * NT], BF16) for i in range(2)]
        self.ctxT = sb(P, "a_ctxT", [128, DC * NT], BF16)
        self.rrow = sb(P, "a_rrow", [1, NT], F32)
        self.rBs = sb(P, "a_rBs", [64, NT], F32)
        self.masks = sb(P, "a_masks", [128, 4 * NT], BF16)
        self.xo = sb(P, "a_xo", [128, DC * NT], F32)
        self.rope = rope
        if rope:
            self.cosT = sb(P, "a_cosT", [128, S], BF16)
            self.sinA = sb(P, "a_sinA", [128, S], BF16)
            self.perm = sb(P, "a_perm", [128, 128], BF16)
            self.rtmp = sb(P, "a_rtmp", [128, NT], BF16)
        else:
            self.bias = sb(P, "a_bias", [128, 4 * DC], F32)
        self.pb_q = ps(P, "a_pbq", (128, 4 * 512))
        self.pb_c = [ps(P, f"a_pbc{i}") for i in range(2)]
        self.pb_st = ps(P, "a_pbst")
        self.pb_bc = ps(P, "a_pbbc")


def attn_masks_np():
    m = np.zeros((128, 4 * NT), dtype=BF)
    for j in range(4):
        kk = np.arange(128)[:, None]
        qq = np.arange(NT)[None, :]
        m[:, j * NT:(j + 1) * NT] = (kk <= qq - 128 * j).astype(np.float32)
    return m


def rope_tables_np():
    inv = 1.0 / (10000.0 ** (np.arange(0, HD, 2, dtype=np.float64) / HD))
    t = np.arange(S, dtype=np.float64)
    fr = np.outer(inv, t)
    cos = np.cos(fr)
    sin = np.sin(fr)
    cosT = np.concatenate([cos, cos], 0)
    sinA = np.concatenate([-sin, sin], 0)
    cos128 = np.concatenate([cosT, cosT], 0).astype(np.float32)
    sin128 = np.concatenate([sinA, sinA], 0).astype(np.float32)
    perm = np.zeros((128, 128), dtype=np.float32)
    for b in range(2):
        for j in range(32):
            perm[b * 64 + j + 32, b * 64 + j] = 1.0
            perm[b * 64 + j, b * 64 + j + 32] = 1.0
    return cos128.astype(BF), sin128.astype(BF), perm.astype(BF)


def emit_rope(P, B, dst, dst_cols, pos0, pb, pbkey=None):
    if pbkey is None:
        pbkey = pb
    sl = slice(dst_cols, dst_cols + NT)
    tsl = slice(pos0, pos0 + NT)
    P.do("pe", lambda t: t.matmul(pb[:], B.perm[:], dst[:, sl], start=True, stop=True),
         reads=[dst, B.perm], writes=[pbkey])
    P.do("dve", lambda v: v.tensor_tensor(B.rtmp[:], pb[:], B.sinA[:, tsl], ALU.mult),
         reads=[pbkey, B.sinA], writes=[B.rtmp])
    P.do("dve", lambda v: v.tensor_tensor(dst[:, sl], dst[:, sl], B.cosT[:, tsl], ALU.mult),
         reads=[dst, B.cosT], writes=[dst])
    P.do("dve", lambda v: v.tensor_tensor(dst[:, sl], dst[:, sl], B.rtmp[:], ALU.add),
         reads=[dst, B.rtmp], writes=[dst])


def emit_proj(P, B, W, xn, out_sb, co_range=range(DC), bias=None, bias_idx=0,
              evac_dtype_note="bf16"):
    for co in co_range:
        pb = B.pb_q[:, (co % 4) * 512:(co % 4) * 512 + 512]
        qk = (B.pb_q, co % 4)
        P.do("pe", emit_linear(
            P, lambda pb=pb: pb,
            lambda k, co=co, W=W: W[:, k * 1024 + co * 128:k * 1024 + co * 128 + 128],
            lambda k, xn=xn: xn[:, k * NT:(k + 1) * NT], DC, None),
            reads=[(xn, c2) for c2 in range(DC)] + [W], writes=[qk])
        if bias is not None:
            P.do("act", lambda s, co=co, pb=pb: s.activation(
                out_sb[:, co * NT:(co + 1) * NT], pb, AF.Identity,
                bias=B.bias[:, bias_idx * DC + co:bias_idx * DC + co + 1]),
                reads=[qk, B.bias], writes=[out_sb])
        else:
            P.do("act", lambda s, co=co, pb=pb: s.copy(
                out_sb[:, co * NT:(co + 1) * NT], pb),
                reads=[qk], writes=[out_sb])


def emit_ln_tile(P, C, B, xf, src_keys=None):
    nt = NT
    if src_keys is None:
        src_keys = [xf]
    ALLXN = [(B.xn, c) for c in range(DC)]
    P.do("dve", lambda v: v.tensor_copy(B.xn[:], xf[:]), reads=src_keys, writes=ALLXN)
    P.do("pe", emit_linear(P, lambda: B.pb_st[0:1, 0:nt], lambda k: C.onesc[:],
                           lambda k: B.xn[:, k * nt:(k + 1) * nt], DC, None),
         reads=ALLXN + [C.onesc], writes=[B.pb_st])
    P.do("dve", lambda v: v.tensor_tensor(B.xn[:], B.xn[:], B.xn[:], ALU.mult),
         reads=ALLXN, writes=ALLXN)
    P.do("pe", emit_linear(P, lambda: B.pb_st[64:65, 0:nt], lambda k: C.onesc[:],
                           lambda k: B.xn[:, k * nt:(k + 1) * nt], DC, None),
         reads=ALLXN + [C.onesc], writes=[B.pb_st])
    row = P._ln_rows
    m = row[0:1, 0:nt]
    v2 = row[0:1, nt:2 * nt]
    inv = row[0:1, 2 * nt:3 * nt]
    P.do("dve", lambda v: v.tensor_scalar_mul(m, B.pb_st[0:1, 0:nt], 1.0 / D),
         reads=[B.pb_st], writes=[row])
    P.do("dve", lambda v: v.tensor_scalar_mul(v2, B.pb_st[64:65, 0:nt], 1.0 / D),
         reads=[B.pb_st], writes=[row])
    P.do("dve", lambda v: v.tensor_mul(inv, m, m), reads=[row], writes=[row])
    P.do("dve", lambda v: v.tensor_sub(v2, v2, inv), reads=[row], writes=[row])
    P.do("dve", lambda v: v.tensor_scalar_add(v2, v2, 1e-5), reads=[row], writes=[row])
    P.do("act", lambda s: s.activation(inv, v2, AF.Ln), reads=[row], writes=[row])
    P.do("act", lambda s: s.activation(inv, inv, AF.Exp, scale=-0.5),
         reads=[row], writes=[row])
    P.do("dve", lambda v: v.tensor_mul(m, m, inv), reads=[row], writes=[row])
    P.do("dve", lambda v: v.tensor_scalar_mul(m, m, -1.0), reads=[row], writes=[row])
    P.do("pe", lambda t: t.matmul(B.pb_bc[:, 0:nt], C.onesr[:], inv, start=True, stop=True),
         reads=[row, C.onesr], writes=[B.pb_bc])
    P.do("act", lambda s: s.copy(B.aB[:], B.pb_bc[:, 0:nt]), reads=[B.pb_bc], writes=[B.aB])
    P.do("pe", lambda t: t.matmul(B.pb_bc[:, 0:nt], C.onesr[:], m, start=True, stop=True),
         reads=[row, C.onesr], writes=[B.pb_bc])
    P.do("act", lambda s: s.copy(B.bB[:], B.pb_bc[:, 0:nt]), reads=[B.pb_bc], writes=[B.bB])
    for c in range(DC):
        sl = slice(c * NT, (c + 1) * NT)
        P.do("dve", lambda v, sl=sl: v.tensor_tensor(B.xn[:, sl], xf[:, sl], B.aB[:],
                                                     ALU.mult),
             reads=src_keys + [B.aB], writes=[(B.xn, c)])
        P.do("dve", lambda v, sl=sl: v.tensor_tensor(B.xn[:, sl], B.xn[:, sl], B.bB[:],
                                                     ALU.add),
             reads=[(B.xn, c), B.bB], writes=[(B.xn, c)])


def emit_attn_zprep(P, C, B, t, i, zT, znT, nw):
    nt = NT
    xo = B.xo
    xokeys = [(B.xo, c) for c in range(DC)]
    xf = B.xf[i % 2]
    emit_ln_tile(P, C, B, xo, src_keys=xokeys)
    row = P._ln_rows
    inv = row[0:1, 2 * nt:3 * nt]
    r2 = row[0:1, nt:2 * nt]
    P.do("dve", lambda v: v.tensor_mul(r2, inv, inv), reads=[row], writes=[row])
    P.do("dve", lambda v: v.tensor_scalar(r2, r2, -1e-5, 1.0 + 1e-6, ALU.mult, ALU.add),
         reads=[row], writes=[row])
    P.do("act", lambda s: s.activation(r2, r2, AF.Ln), reads=[row], writes=[row])
    P.do("act", lambda s: s.activation(r2, r2, AF.Exp, scale=-0.5), reads=[row], writes=[row])
    for c in range(DC):
        sl = slice(c * NT, (c + 1) * NT)
        P.do("dve", lambda v, sl=sl: v.tensor_tensor(xf[:, sl], xo[:, sl], B.aB[:], ALU.mult),
             reads=[(B.xo, c), B.aB], writes=[xf])
        P.do("dve", lambda v, sl=sl: v.tensor_tensor(xf[:, sl], xf[:, sl], B.bB[:], ALU.add),
             reads=[xf, B.bB], writes=[xf])
    emit_dma_tile(P, "pool_dma", xf, zT, t, write=True)
    P.do("pe", lambda t_: t_.matmul(B.pb_bc[:, 0:nt], C.onesr[:], r2, start=True, stop=True),
         reads=[row, C.onesr], writes=[B.pb_bc])
    P.do("act", lambda s: s.copy(B.aB[:], B.pb_bc[:, 0:nt]), reads=[B.pb_bc], writes=[B.aB])
    for c in range(DC):
        sl = slice(c * NT, (c + 1) * NT)
        P.do("dve", lambda v, sl=sl: v.tensor_tensor(B.xn[:, sl], B.xn[:, sl], B.aB[:],
                                                     ALU.mult),
             reads=[(B.xn, c), B.aB], writes=[(B.xn, c)])
        P.do("act", lambda s, c=c, sl=sl: s.activation(B.xn[:, sl], B.xn[:, sl],
             AF.Identity, scale=nw[:, c:c + 1]),
             reads=[(B.xn, c), nw], writes=[(B.xn, c)])
    P.do("sp_dma", lambda e, t=t: e.dma_start(
        out=znT.rearrange("(c p) n -> p c n", p=128)[:, :, t * NT:(t + 1) * NT],
        in_=B.xn[:].rearrange("p (c n) -> p c n", n=NT)),
        reads=[(B.xn, c) for c in range(DC)], writes=[(znT.tensor, t)])


def emit_phase_attn(P, C, B, x_in, x_out, WqT, WkT, WvT, WoT, tiles_per_seq=4, seqs=None,
                    zprep=None):
    rope = B.rope
    P.do("dve", lambda v: v.memset(
        B.Vaug[:].rearrange("p (g x) -> p g x", x=16 * 65)[:, :, 64:16 * 65:65], 1.0),
        writes=[B.Vaug])
    if seqs is None:
        seqs = range(NSEQ)
    for s in seqs:
        load_wT(P, B.Wa, WkT)
        load_wT(P, B.Wb, WvT)
        for i in range(tiles_per_seq):
            t = s * tiles_per_seq + i
            xf = B.xf[i % 2]
            emit_dma_tile(P, "sp_dma", xf, x_in, t)
            emit_ln_tile(P, C, B, xf)
            for co in range(DC):
                pb = B.pb_q[:, (co % 4) * 512:(co % 4) * 512 + 512]
                qk = (B.pb_q, co % 4)
                P.do("pe", emit_linear(
                    P, lambda pb=pb: pb,
                    lambda k, co=co: B.Wa[:, k * 1024 + co * 128:k * 1024 + co * 128 + 128],
                    lambda k: B.xn[:, k * NT:(k + 1) * NT], DC, None),
                    reads=[(B.xn, c2) for c2 in range(DC)] + [B.Wa], writes=[qk])
                if rope:
                    P.do("act", lambda sc, co=co, pb=pb, i=i: sc.copy(
                        B.KT[:, co * S + i * NT:co * S + (i + 1) * NT], pb),
                        reads=[qk], writes=[B.KT])
                else:
                    P.do("act", lambda sc, co=co, pb=pb, i=i: sc.activation(
                        B.KT[:, co * S + i * NT:co * S + (i + 1) * NT], pb, AF.Identity,
                        bias=B.bias[:, 1 * DC + co:1 * DC + co + 1]),
                        reads=[qk, B.bias], writes=[B.KT])
            if rope:
                for co in range(DC):
                    emit_rope(P, B, B.KT, co * S + i * NT, i * NT,
                              B.pb_q[:, (co % 4) * 512:(co % 4) * 512 + 512],
                              (B.pb_q, co % 4))
            for tc in range(4):
                for dvh in range(2):
                    qd = (tc * 2 + dvh) % 4
                    pb = B.pb_q[:, qd * 512:qd * 512 + 512]
                    qk = (B.pb_q, qd)
                    P.do("pe", emit_linear(
                        P, lambda pb=pb: pb,
                        lambda k, tc=tc: B.xn[:, k * NT + tc * 128:k * NT + tc * 128 + 128],
                        lambda k, dvh=dvh: B.Wb[:, k * 1024 + dvh * 512:k * 1024 + (dvh + 1) * 512],
                        DC, None),
                        reads=[(B.xn, c2) for c2 in range(DC)] + [B.Wb], writes=[qk])
                    gtc = i * 4 + tc
                    h0 = dvh * 8
                    dst = B.Vaug[:, gtc * 1040 + h0 * 65:gtc * 1040 + h0 * 65 + 65 * 8]\
                        .rearrange("p (hh x) -> p hh x", x=65)[:, :, 0:64]
                    P.do("act", lambda sc, pb=pb, dst=dst: sc.copy(
                        dst, pb.rearrange("p (hh x) -> p hh x", x=64)),
                        reads=[qk], writes=[B.Vaug])
        load_wT(P, B.Wa, WqT)
        load_wT(P, B.Wb, WoT)
        for i in range(tiles_per_seq):
            t = s * tiles_per_seq + i
            xf = B.xf[i % 2]
            emit_dma_tile(P, "sp_dma", xf, x_in, t)
            emit_ln_tile(P, C, B, xf)
            if rope:
                emit_proj(P, B, B.Wa, B.xn, B.QT)
                for co in range(DC):
                    emit_rope(P, B, B.QT, co * NT, i * NT,
                              B.pb_q[:, (co % 4) * 512:(co % 4) * 512 + 512],
                              (B.pb_q, co % 4))
            else:
                emit_proj(P, B, B.Wa, B.xn, B.QT, bias=True, bias_idx=0)
            kc = 4 * i + 4

            def normalize_head(h):
                hrow = (h % 2) * 64
                P.do("act", lambda s_, h=h: s_.activation(
                    B.rBs[0:1, :], B.pb_c[h % 2][64:65, :], AF.Ln),
                    reads=[B.pb_c[h % 2]], writes=[B.rBs])
                P.do("act", lambda s_: s_.activation(
                    B.rrow[:], B.rBs[0:1, :], AF.Exp, scale=-1.0),
                    reads=[B.rBs], writes=[B.rrow])
                P.do("pe", lambda t_, h=h: t_.matmul(B.pb_bc[0:64, :], C.onesr[0:1, 0:64],
                                                     B.rrow[:], start=True, stop=True),
                     reads=[B.rrow, C.onesr], writes=[B.pb_bc])
                P.do("act", lambda sc: sc.copy(B.rBs[:], B.pb_bc[0:64, :]),
                     reads=[B.pb_bc], writes=[B.rBs])
                P.do("dve", lambda v, h=h, hrow=hrow: v.tensor_tensor(
                    B.ctxT[hrow:hrow + 64, (h // 2) * NT:(h // 2 + 1) * NT],
                    B.pb_c[h % 2][0:64, :], B.rBs[:], ALU.mult),
                    reads=[B.pb_c[h % 2], B.rBs], writes=[(B.ctxT, h)])
                if not rope:
                    P.do("act", lambda sc, h=h, hrow=hrow: sc.activation(
                        B.ctxT[hrow:hrow + 64, (h // 2) * NT:(h // 2 + 1) * NT],
                        B.ctxT[hrow:hrow + 64, (h // 2) * NT:(h // 2 + 1) * NT],
                        AF.Identity,
                        bias=B.bias[hrow:hrow + 64, 2 * DC + h // 2:2 * DC + h // 2 + 1]),
                        reads=[(B.ctxT, h), B.bias], writes=[(B.ctxT, h)])

            ng2 = kc // 2
            for hp in range(H // 2):
                h0, h1 = 2 * hp, 2 * hp + 1

                def sg(h, w, g):
                    hrow = (h % 2) * 64
                    def f(t_):
                        last = None
                        for jj in range(2):
                            j = 2 * g + jj
                            last = t_.matmul(
                                B.pb_q[:, (2 * w + jj) * 512:(2 * w + jj) * 512 + 512],
                                B.KT[hrow:hrow + 64,
                                     (h // 2) * S + j * 128:(h // 2) * S + j * 128 + 128],
                                B.QT[hrow:hrow + 64, (h // 2) * NT:(h // 2 + 1) * NT],
                                start=True, stop=True)
                        return last
                    return f

                def eg(w, g):
                    et = B.ET[w][:, (g % 2) * 1024:(g % 2) * 1024 + 1024]
                    P.do("act", lambda sc, et=et, w=w: sc.activation(
                        et, B.pb_q[:, 2 * w * 512:2 * w * 512 + 1024],
                        AF.Exp, bias=0.0, scale=0.125),
                        reads=[(B.pb_q, 2 * w), (B.pb_q, 2 * w + 1)],
                        writes=[(B.ET[w], g % 2)])
                    j0 = 2 * g
                    if j0 >= 4 * i:
                        m = j0 - 4 * i
                        P.do("dve", lambda v, et=et, m=m: v.tensor_tensor(
                            et, et, B.masks[:, m * NT:(m + 2) * NT], ALU.mult),
                            reads=[(B.ET[w], g % 2), B.masks], writes=[(B.ET[w], g % 2)])

                def cg(h, w, g):
                    def f(t_):
                        last = None
                        for jj in range(2):
                            j = 2 * g + jj
                            last = t_.matmul(
                                B.pb_c[w][0:65, :],
                                B.Vaug[:, j * 1040 + h * 65:j * 1040 + (h + 1) * 65],
                                B.ET[w][:, ((g % 2) * 2 + jj) * 512:((g % 2) * 2 + jj) * 512 + 512],
                                start=(j == 0), stop=(j == kc - 1))
                        return last
                    return f
                P.do("pe", sg(h0, 0, 0), reads=[B.KT, B.QT],
                     writes=[(B.pb_q, 0), (B.pb_q, 1)])
                P.do("pe", sg(h1, 1, 0), reads=[B.KT, B.QT],
                     writes=[(B.pb_q, 2), (B.pb_q, 3)])
                if hp > 0:
                    normalize_head(2 * hp - 2)
                    normalize_head(2 * hp - 1)
                eg(0, 0)
                eg(1, 0)
                for g in range(1, ng2):
                    P.do("pe", sg(h0, 0, g), reads=[B.KT, B.QT],
                         writes=[(B.pb_q, 0), (B.pb_q, 1)])
                    P.do("pe", sg(h1, 1, g), reads=[B.KT, B.QT],
                         writes=[(B.pb_q, 2), (B.pb_q, 3)])
                    eg(0, g)
                    eg(1, g)
                    P.do("pe", cg(h0, 0, g - 1), reads=[B.Vaug, (B.ET[0], (g - 1) % 2)],
                         writes=[B.pb_c[0]])
                    P.do("pe", cg(h1, 1, g - 1), reads=[B.Vaug, (B.ET[1], (g - 1) % 2)],
                         writes=[B.pb_c[1]])
                P.do("pe", cg(h0, 0, ng2 - 1), reads=[B.Vaug, (B.ET[0], (ng2 - 1) % 2)],
                     writes=[B.pb_c[0]])
                P.do("pe", cg(h1, 1, ng2 - 1), reads=[B.Vaug, (B.ET[1], (ng2 - 1) % 2)],
                     writes=[B.pb_c[1]])
            normalize_head(H - 2)
            normalize_head(H - 1)
            for co in range(DC):
                pb = B.pb_q[:, (co % 4) * 512:(co % 4) * 512 + 512]
                qk = (B.pb_q, co % 4)
                P.do("pe", emit_linear(
                    P, lambda pb=pb: pb,
                    lambda k, co=co: B.Wb[:, k * 1024 + co * 128:k * 1024 + co * 128 + 128],
                    lambda k: B.ctxT[:, k * NT:(k + 1) * NT], DC, None),
                    reads=[(B.ctxT, hh) for hh in range(H)] + [B.Wb], writes=[qk])
                bias_s = 0.0 if rope else B.bias[:, 3 * DC + co:3 * DC + co + 1]
                brd = [] if rope else [B.bias]
                P.do("dve", lambda v, co=co, pb=pb, xf=xf, bias_s=bias_s:
                     v.scalar_tensor_tensor(
                         B.xo[:, co * NT:(co + 1) * NT], pb, bias_s,
                         xf[:, co * NT:(co + 1) * NT], ALU.add, ALU.add),
                     reads=[qk, xf] + brd, writes=[(B.xo, co)])
            emit_dma_tile(P, "sp_dma", B.xo, x_out, t, write=True, subkeys=range(8))
            if zprep is not None:
                emit_attn_zprep(P, C, B, t, i, zprep[0], zprep[1], zprep[2])


class L1FFNBufs:
    def __init__(self, P):
        self.wmacT = sb(P, "e_wmacT", [128, DC * 4], BF16)
        self.wmicT = sb(P, "e_wmicT", [128, DC * 8], BF16)
        self.nw = sb(P, "e_nw", [128, DC], F32)
        self.w1sb = sb(P, "e_w1", [128, 8 * 2048], BF16)
        self.w2sb = sb(P, "e_w2", [128, 16 * 1024], BF16)
        self.z = [sb(P, f"e_z{i}", [128, DC * NT], F32) for i in range(2)]
        self.zb = [sb(P, f"e_zb{i}", [128, DC * NT], BF16) for i in range(2)]
        self.zn = [sb(P, f"e_zn{i}", [128, DC * NT], BF16) for i in range(2)]
        self.y = [sb(P, f"e_y{i}", [128, DC * NT], BF16) for i in range(2)]
        self.hs = sb(P, "e_hs", [128, 16 * NT], BF16)
        self.xr = sb(P, "e_xr", [128, DC * NT], F32)
        self.aB = sb(P, "e_aB", [128, NT], BF16)
        self.bB2 = sb(P, "e_bB2", [128, NT], BF16)
        self.sm = sb(P, "e_sm", [128, 16], F32)
        self.mi = sb(P, "e_mi", [128, 32], F32)
        self.r1 = sb(P, "e_r1", [128, 16], F32)
        self.r2 = sb(P, "e_r2", [128, 16], F32)
        self.r3 = sb(P, "e_r3", [128, 16], F32)
        self.r4 = sb(P, "e_r4", [128, 16], F32)
        self.rs = sb(P, "e_rs", [128, 4 * 4], F32)
        self.rows = sb(P, "e_rows", [1, 6 * NT], F32)
        self.cw = sb(P, "e_cw", [128, 32], F32)
        self.cwT = [sb(P, f"e_cwT{i}", [8, NT], BF16) for i in range(2)]
        self.cwB = sb(P, "e_cwB", [128, NT], BF16)
        self.pb_h = [ps(P, f"e_pbh{i}") for i in range(2)]
        self.pb_y = [ps(P, f"e_pby{i}") for i in range(2)]
        self.pb_r = ps(P, "e_pbr")
        self.pb_bc = ps(P, "e_pbbc")
        self.pb_t = ps(P, "e_pbt")


def emit_stage_bf16(P, buf, src_ap, dst_ap, key):
    load_wT(P, buf, src_ap)
    w = dst_ap.shape[-1]
    P.do("sp_dma", lambda e: e.dma_start(out=dst_ap, in_=buf[:, 0:w]),
         reads=[buf], writes=[key])


def emit_weight_convert_l1(P, buf1, buf2, w1T_hbm, w2T_hbm, w1bf, w2bf):
    for e in range(8):
        b = buf1 if e % 2 == 0 else buf2
        load_wT(P, b, w1T_hbm[e])
        P.do("sp_dma", lambda en, e=e, b=b: en.dma_start(out=w1bf[e], in_=b[:]),
             reads=[b], writes=[(w1bf.tensor, e)])
        load_wT(P, b, w2T_hbm[e])
        P.do("act_dma", lambda en, e=e, b=b: en.dma_start(out=w2bf[e], in_=b[:]),
             reads=[b], writes=[(w2bf.tensor, e)])


def emit_routing_l1(P, C, E, cwT):
    v = "dve"
    sm, r1, r2, r3, r4, rs, cw = E.sm, E.r1, E.r2, E.r3, E.r4, E.rs, E.cw

    def v4(buf):
        return buf[:].rearrange("p (tc e) -> p tc e", e=4)

    def v8(buf):
        return buf[:].rearrange("p (tc e) -> p tc e", e=8)

    def rsv(i):
        return rs[:, i * 4:(i + 1) * 4]

    def rsb(i, n=4):
        return rs[:, i * 4:(i + 1) * 4].unsqueeze(2).broadcast_to([128, 4, n])

    mac = E.pb_r[:, 0:16].rearrange("p (tc e) -> p tc e", e=4)
    iota4b = C.iota8[:, 0:4].unsqueeze(1).broadcast_to([128, 4, 4])
    P.do(v, lambda e: e.tensor_reduce(rsv(0), mac, axis=AX.X, op=ALU.max),
         reads=[E.pb_r], writes=[rs])
    P.do(v, lambda e: e.tensor_tensor(v4(sm), mac, rsb(0), ALU.subtract),
         reads=[E.pb_r, rs], writes=[sm])
    P.do("act", lambda s: s.activation(sm[:], sm[:], AF.Exp), reads=[sm], writes=[sm])
    P.do(v, lambda e: e.tensor_reduce(rsv(0), v4(sm), axis=AX.X, op=ALU.add),
         reads=[sm], writes=[rs])
    P.do(v, lambda e: e.reciprocal(rsv(1), rsv(0)), reads=[rs], writes=[rs])
    P.do(v, lambda e: e.tensor_tensor(v4(sm), v4(sm), rsb(1), ALU.mult),
         reads=[sm, rs], writes=[sm])
    P.do(v, lambda e: e.tensor_reduce(rsv(0), v4(sm), axis=AX.X, op=ALU.max),
         reads=[sm], writes=[rs])
    P.do(v, lambda e: e.tensor_tensor(v4(r1), v4(sm), rsb(0), ALU.is_ge),
         reads=[sm, rs], writes=[r1])
    P.do(v, lambda e: e.scalar_tensor_tensor(v4(r1), v4(r1), -999.0, iota4b,
                                             ALU.mult, ALU.add),
         reads=[r1, C.iota8], writes=[r1])
    P.do(v, lambda e: e.tensor_reduce(rsv(1), v4(r1), axis=AX.X, op=ALU.min),
         reads=[r1], writes=[rs])
    P.do(v, lambda e: e.tensor_scalar_add(rsv(1), rsv(1), 999.0), reads=[rs], writes=[rs])
    P.do(v, lambda e: e.tensor_tensor(v4(r1), iota4b, rsb(1), ALU.is_equal),
         reads=[C.iota8, rs], writes=[r1])
    P.do(v, lambda e: e.scalar_tensor_tensor(v4(r2), v4(r1), -1e30, v4(sm),
                                             ALU.mult, ALU.add),
         reads=[r1, sm], writes=[r2])
    P.do(v, lambda e: e.tensor_reduce(rsv(2), v4(r2), axis=AX.X, op=ALU.max),
         reads=[r2], writes=[rs])
    P.do(v, lambda e: e.tensor_tensor(v4(r2), v4(r2), rsb(2), ALU.is_ge),
         reads=[r2, rs], writes=[r2])
    P.do(v, lambda e: e.scalar_tensor_tensor(v4(r2), v4(r2), -999.0, iota4b,
                                             ALU.mult, ALU.add),
         reads=[r2, C.iota8], writes=[r2])
    P.do(v, lambda e: e.tensor_reduce(rsv(3), v4(r2), axis=AX.X, op=ALU.min),
         reads=[r2], writes=[rs])
    P.do(v, lambda e: e.tensor_scalar_add(rsv(3), rsv(3), 999.0), reads=[rs], writes=[rs])
    P.do(v, lambda e: e.tensor_tensor(v4(r2), iota4b, rsb(3), ALU.is_equal),
         reads=[C.iota8, rs], writes=[r2])
    P.do(v, lambda e: e.tensor_tensor(rsv(3), rsv(0), rsv(2), ALU.add),
         reads=[rs], writes=[rs])
    P.do(v, lambda e: e.tensor_scalar_add(rsv(3), rsv(3), 1e-9), reads=[rs], writes=[rs])
    P.do(v, lambda e: e.reciprocal(rsv(3), rsv(3)), reads=[rs], writes=[rs])
    P.do(v, lambda e: e.tensor_tensor(rsv(0), rsv(0), rsv(3), ALU.mult),
         reads=[rs], writes=[rs])
    P.do(v, lambda e: e.tensor_tensor(rsv(2), rsv(2), rsv(3), ALU.mult),
         reads=[rs], writes=[rs])
    P.do(v, lambda e: e.tensor_tensor(v4(r1), v4(r1), rsb(0), ALU.mult),
         reads=[r1, rs], writes=[r1])
    P.do(v, lambda e: e.tensor_tensor(v4(r2), v4(r2), rsb(2), ALU.mult),
         reads=[r2, rs], writes=[r2])
    P.do(v, lambda e: e.tensor_tensor(r1[:], r1[:], r2[:], ALU.add),
         reads=[r1, r2], writes=[r1])
    P.do("act", lambda s: s.copy(E.mi[:], E.pb_r[:, 16:48]), reads=[E.pb_r], writes=[E.mi])
    mic = E.mi[:].rearrange("p (tc e) -> p tc e", e=8)
    P.do(v, lambda e: e.tensor_tensor(v4(r2), mic[:, :, 0:8:2], mic[:, :, 1:8:2],
                                      ALU.subtract),
         reads=[E.mi], writes=[r2])
    P.do("act", lambda s: s.activation(r2[:], r2[:], AF.Sigmoid), reads=[r2], writes=[r2])
    P.do(v, lambda e: e.tensor_scalar(r3[:], r2[:], -1.0, 1.0, ALU.mult, ALU.add),
         reads=[r2], writes=[r3])
    P.do(v, lambda e: e.tensor_tensor(r4[:], r2[:], r3[:], ALU.max),
         reads=[r2, r3], writes=[r4])
    P.do(v, lambda e: e.tensor_tensor(r3[:], r2[:], r3[:], ALU.is_ge),
         reads=[r2, r3], writes=[r3])
    P.do(v, lambda e: e.tensor_scalar_add(r2[:], r4[:], 1e-9), reads=[r4], writes=[r2])
    P.do(v, lambda e: e.reciprocal(r2[:], r2[:]), reads=[r2], writes=[r2])
    P.do(v, lambda e: e.tensor_tensor(r4[:], r4[:], r2[:], ALU.mult),
         reads=[r4, r2], writes=[r4])
    P.do(v, lambda e: e.tensor_tensor(r4[:], r1[:], r4[:], ALU.mult),
         reads=[r1, r4], writes=[r4])
    P.do(v, lambda e: e.tensor_scalar(r2[:], r3[:], -1.0, 1.0, ALU.mult, ALU.add),
         reads=[r3], writes=[r2])
    P.do(v, lambda e: e.tensor_tensor(v8(cw)[:, :, 0::2], v4(r4), v4(r3), ALU.mult),
         reads=[r4, r3], writes=[cw])
    P.do(v, lambda e: e.tensor_tensor(v8(cw)[:, :, 1::2], v4(r4), v4(r2), ALU.mult),
         reads=[r4, r2], writes=[cw])
    for tc in range(4):
        P.do("pe", lambda t, tc=tc: t.transpose(E.pb_t[0:8, 0:128], cw[:, tc * 8:tc * 8 + 8],
                                                C.ident[:]),
             reads=[cw, C.ident], writes=[E.pb_t])
        P.do(v, lambda e, tc=tc: e.tensor_copy(cwT[0:8, tc * 128:(tc + 1) * 128],
                                               E.pb_t[0:8, 0:128]),
             reads=[E.pb_t], writes=[cwT])


def emit_phase_l1ffn_loop(P, C, E, loop, x_in, zT, znT, x_out, w1bf, w2bf,
                          tiles=range(NTILES)):
    tiles = list(tiles)

    def preamble_dma(t, s):
        emit_dma_tile(P, "pool_dma", E.z[s], zT, t)
        P.do("sp_dma", lambda e, t=t, s=s: e.dma_start(
            out=E.zn[s][:].rearrange("p (c n) -> p c n", n=NT),
            in_=znT.rearrange("(c p) n -> p c n", p=128)[:, :, t * NT:(t + 1) * NT]),
            reads=[(znT.tensor, t)], writes=[(E.zn[s], c) for c in range(DC)])

    def preamble_route(t, s):
        zn, cwT = E.zn[s], E.cwT[s]
        for tc in range(4):
            P.do("pe", emit_linear(
                P, lambda tc=tc: E.pb_r[:, tc * 4:(tc + 1) * 4],
                lambda k, tc=tc: zn[:, k * NT + tc * 128:k * NT + tc * 128 + 128],
                lambda k: E.wmacT[:, k * 4:(k + 1) * 4], DC, None),
                reads=[(zn, c2) for c2 in range(DC)] + [E.wmacT], writes=[E.pb_r])
            P.do("pe", emit_linear(
                P, lambda tc=tc: E.pb_r[:, 16 + tc * 8:16 + (tc + 1) * 8],
                lambda k, tc=tc: zn[:, k * NT + tc * 128:k * NT + tc * 128 + 128],
                lambda k: E.wmicT[:, k * 8:(k + 1) * 8], DC, None),
                reads=[(zn, c2) for c2 in range(DC)] + [E.wmicT], writes=[E.pb_r])
        emit_routing_l1(P, C, E, cwT)

    def experts_pair(pair):
        w1 = E.w1sb
        for e in range(8):
            for wh in range(2):
                P.do("sp_dma", lambda en, e=e, wh=wh: en.dma_start(
                    out=w1[:, wh * 8192:(wh + 1) * 8192],
                    in_=w1bf[e][:, wh * 8192:(wh + 1) * 8192]),
                    reads=[(w1bf.tensor, e)], writes=[(w1, wh)])
            P.do("act_dma", lambda en, e=e: en.dma_start(out=E.w2sb[:], in_=w2bf[e]),
                 reads=[(w2bf.tensor, e)], writes=[E.w2sb])
            for pi, (t, s) in enumerate(pair):
                zn, cwT, y = E.zn[s], E.cwT[s], E.y[s]
                first = (e == 0)

                def bcast(e=e, cwT=cwT):
                    P.do("pe", lambda t_: t_.matmul(
                        E.pb_bc[:], C.sel8[:, e * 128:(e + 1) * 128],
                        cwT[:], start=True, stop=True),
                        reads=[cwT, C.sel8], writes=[E.pb_bc])
                    P.do("act", lambda s_: s_.copy(E.cwB[:], E.pb_bc[:]),
                         reads=[E.pb_bc], writes=[E.cwB])

                def scale(fc):
                    P.do("dve", lambda v, fc=fc: v.tensor_tensor(
                        E.hs[:, fc * NT:(fc + 1) * NT], E.hs[:, fc * NT:(fc + 1) * NT],
                        E.cwB[:], ALU.mult),
                        reads=[(E.hs, fc), E.cwB], writes=[(E.hs, fc)])
                if not first:
                    bcast()
                for fc in range(16):
                    pb = E.pb_h[fc % 2]

                    def hmm(k0, k1, fc=fc, pb=pb, zn=zn):
                        def f(t_):
                            last = None
                            for k in range(k0, k1):
                                last = t_.matmul(
                                    pb[:], w1[:, k * 2048 + fc * 128:k * 2048 + fc * 128 + 128],
                                    zn[:, k * NT:(k + 1) * NT],
                                    start=(k == 0), stop=(k == DC - 1))
                            return last
                        return f
                    P.do("pe", hmm(0, 4),
                         reads=[(zn, c2) for c2 in range(4)] + [(w1, 0)], writes=[pb])
                    P.do("pe", hmm(4, 8),
                         reads=[(zn, c2) for c2 in range(4, 8)] + [(w1, 1)], writes=[pb])
                    if first and pi == 0 and fc == 0:
                        for t2, s2 in pair:
                            preamble_route(t2, s2)
                    if first and fc == 8:
                        bcast()
                    P.do("act", lambda s_, fc=fc, pb=pb: s_.activation(
                        E.hs[:, fc * NT:(fc + 1) * NT], pb[:], AF.Silu),
                        reads=[pb], writes=[(E.hs, fc)])
                    if not first or fc >= 8:
                        scale(fc)
                        if first and fc >= 8:
                            scale(fc - 8)
                if first:
                    pass
                for co in range(8):
                    pb = E.pb_y[co % 2]

                    def ymm(k0, k1, co=co, pb=pb):
                        def f(t_):
                            last = None
                            for k in range(k0, k1):
                                last = t_.matmul(
                                    pb[:],
                                    E.w2sb[:, k * 1024 + co * 128:k * 1024 + co * 128 + 128],
                                    E.hs[:, k * NT:(k + 1) * NT],
                                    start=(k == 0), stop=(k == 15))
                            return last
                        return f
                    P.do("pe", ymm(0, 14),
                         reads=[(E.hs, k) for k in range(14)] + [E.w2sb], writes=[pb])
                    P.do("pe", ymm(14, 16),
                         reads=[(E.hs, 14), (E.hs, 15), E.w2sb], writes=[pb])
                    if e == 0:
                        P.do("act", lambda s_, co=co, pb=pb, y=y: s_.copy(
                            y[:, co * NT:(co + 1) * NT], pb[:]),
                            reads=[pb], writes=[(y, co)])
                    else:
                        P.do("dve", lambda v, co=co, pb=pb, y=y: v.tensor_tensor(
                            y[:, co * NT:(co + 1) * NT], y[:, co * NT:(co + 1) * NT],
                            pb[:], ALU.add),
                            reads=[(y, co), pb], writes=[(y, co)])

    def finals(t, s):
        z = E.z[s]
        y = E.y[s]
        if loop == 0:
            zn = E.zn[s]
            zbk = y
            nt = NT
            rw = E.rows
            v2 = rw[0:1, (3 * s + 1) * nt:(3 * s + 1) * nt + nt]
            inv = rw[0:1, (3 * s + 2) * nt:(3 * s + 2) * nt + nt]
            P.do("dve", lambda v, y=y: v.tensor_tensor(z[:], z[:], y[:], ALU.add),
                 reads=[z] + [(z, c) for c in range(DC)] + [(y, co) for co in range(8)],
                 writes=[z] + [(z, c) for c in range(DC)])
            emit_dma_tile(P, "sp_dma", z, zT, t, write=True)
            P.do("dve", lambda v, zbk=zbk: v.tensor_tensor(zbk[:], z[:], z[:], ALU.mult),
                 reads=[z], writes=[(zbk, co) for co in range(8)])
            P.do("pe", emit_linear(P, lambda: E.pb_r[64:65, 0:nt], lambda k: C.onesc[:],
                                   lambda k, zbk=zbk: zbk[:, k * nt:(k + 1) * nt], DC, None),
                 reads=[(zbk, co) for co in range(8)] + [C.onesc], writes=[E.pb_r])
            P.do("dve", lambda v: v.tensor_scalar(v2, E.pb_r[64:65, 0:nt],
                                                  1.0 / D, 1e-6, ALU.mult, ALU.add),
                 reads=[E.pb_r], writes=[rw])
            P.do("act", lambda s_: s_.activation(inv, v2, AF.Ln), reads=[rw], writes=[rw])
            P.do("act", lambda s_: s_.activation(inv, inv, AF.Exp, scale=-0.5),
                 reads=[rw], writes=[rw])
            P.do("pe", lambda t_: t_.matmul(E.pb_bc[:, 0:nt], C.onesr[:], inv,
                                            start=True, stop=True),
                 reads=[rw, C.onesr], writes=[E.pb_bc])
            P.do("act", lambda s_: s_.copy(E.aB[:], E.pb_bc[:, 0:nt]),
                 reads=[E.pb_bc], writes=[E.aB])
            for c in range(DC):
                sl = slice(c * NT, (c + 1) * NT)
                P.do("dve", lambda v, sl=sl, zn=zn: v.tensor_tensor(
                    zn[:, sl], z[:, sl], E.aB[:], ALU.mult),
                    reads=[z, E.aB], writes=[(zn, c)])
                P.do("act", lambda s_, c=c, sl=sl, zn=zn: s_.activation(
                    zn[:, sl], zn[:, sl], AF.Identity, scale=E.nw[:, c:c + 1]),
                    reads=[(zn, c), E.nw], writes=[(zn, c)])
            P.do("sp_dma", lambda e, t=t, zn=zn: e.dma_start(
                out=znT.rearrange("(c p) n -> p c n", p=128)[:, :, t * NT:(t + 1) * NT],
                in_=zn[:].rearrange("p (c n) -> p c n", n=NT)),
                reads=[(zn, c) for c in range(DC)], writes=[(znT.tensor, t)])
        else:
            emit_dma_tile(P, "pool_dma", E.xr, x_in, t)
            P.do("dve", lambda v, y=y: v.tensor_tensor(z[:], z[:], y[:], ALU.add),
                 reads=[z] + [(z, c) for c in range(DC)] + [(y, co) for co in range(8)],
                 writes=[z] + [(z, c) for c in range(DC)])
            P.do("dve", lambda v: v.tensor_tensor(z[:], z[:], E.xr[:], ALU.add),
                 reads=[z, E.xr], writes=[z] + [(z, c) for c in range(DC)])
            emit_dma_tile(P, "sp_dma", z, x_out, t, write=True)

    assert len(tiles) % 2 == 0
    for i in range(0, len(tiles), 2):
        pair = [(tiles[i], 0), (tiles[i + 1], 1)]
        for t, s in pair:
            preamble_dma(t, s)
        experts_pair(pair)
        for t, s in pair:
            finals(t, s)

FUSED = True


def host_prep(inputs):
    g = {}
    ac = np.ascontiguousarray
    g["WqT0"], g["WkT0"], g["WvT0"], g["WoT0"] = (
        inputs["L0_attn_Wq"], inputs["L0_attn_Wk"],
        inputs["L0_attn_Wv"], inputs["L0_attn_Wo"])
    ab = np.zeros((128, 32), np.float32)
    for i, n in enumerate(("bq", "bk", "bv", "bo")):
        ab[:, i * 8:(i + 1) * 8] = inputs[f"L0_attn_{n}"].reshape(8, 128).T
    g["abias"] = ab
    g["gateT"] = ac(inputs["L0_ffn_gate"].T)
    g["corr4"] = np.tile(inputs["L0_ffn_corr_bias"], (128, 4)).astype(np.float32)
    g["iota32"] = np.tile(np.arange(8, dtype=np.float32), (128, 4))
    g["fc1T"] = ac(inputs["L0_ffn_fc1_w"].T)
    g["fc2T"] = ac(inputs["L0_ffn_fc2_w"].T)
    g["w1Tl0"] = ac(inputs["L0_ffn_w1"].transpose(0, 2, 1))
    g["w2Tl0"] = ac(inputs["L0_ffn_w2"].transpose(0, 2, 1))
    g["upT"] = ac(inputs["L0_ffn_up_w"].T)
    g["downT"] = ac(inputs["L0_ffn_down_w"].T)
    g["WqT1"] = ac(inputs["L1_attn_Wq"].T)
    g["WkT1"] = ac(inputs["L1_attn_Wk"].T)
    g["WvT1"] = ac(inputs["L1_attn_Wv"].T)
    g["WoT1"] = ac(inputs["L1_attn_Wo"].T)
    g["wmacT"] = ac(inputs["L1_ffn_w_macro"].T)
    g["wmicT"] = ac(inputs["L1_ffn_w_micro"].reshape(8, 1024).T)
    g["nw"] = ac(inputs["L1_ffn_norm_w"].reshape(8, 128).T)
    g["w1Tl1"] = ac(inputs["L1_ffn_w1"].transpose(0, 2, 1))
    g["w2Tl1"] = ac(inputs["L1_ffn_w2"].transpose(0, 2, 1))
    return g


_W_SPECS = {
    "WqT0": [1024, 1024], "WkT0": [1024, 1024], "WvT0": [1024, 1024],
    "WoT0": [1024, 1024], "abias": [128, 32],
    "gateT": [1024, 8], "corr4": [128, 32], "iota32": [128, 32],
    "fc1T": [1024, 512], "fc2T": [512, 1024],
    "w1Tl0": [8, 512, 512], "w2Tl0": [8, 512, 512],
    "upT": [1024, 4096], "downT": [4096, 1024],
    "WqT1": [1024, 1024], "WkT1": [1024, 1024], "WvT1": [1024, 1024],
    "WoT1": [1024, 1024],
    "wmacT": [1024, 4], "wmicT": [1024, 8], "nw": [128, 8],
    "w1Tl1": [8, 1024, 2048], "w2Tl1": [8, 2048, 1024],
}


class _Scoped:

    def __init__(self, P):
        self.P = P

    def __enter__(self):
        import contextlib
        self.outer = self.P.es
        self.inner = contextlib.ExitStack()
        self.inner.__enter__()
        self.P.es = self.inner
        return self

    def __exit__(self, *a):
        self.P.es = self.outer
        return self.inner.__exit__(*a)


def _decl_inputs(nc, names):
    hb = {}
    for n in names:
        hb[n] = nc.dram_tensor(n, _W_SPECS[n], F32, kind="ExternalInput")
    return hb


def _emit_attn_consts(P, nc, B, hb, rope):
    h_masks = nc.inline_tensor(attn_masks_np(), name=_uname("h_masks"))
    P.do("sp_dma", lambda e: e.dma_start(out=B.masks[:], in_=h_masks.ap()),
         writes=[B.masks])
    if rope:
        cosn, sinn, permn = rope_tables_np()
        hc = nc.inline_tensor(cosn, name="h_cos")
        hs = nc.inline_tensor(sinn, name="h_sin")
        hp = nc.inline_tensor(permn, name="h_perm")
        P.do("sp_dma", lambda e: e.dma_start(out=B.cosT[:], in_=hc.ap()), writes=[B.cosT])
        P.do("sp_dma", lambda e: e.dma_start(out=B.sinA[:], in_=hs.ap()), writes=[B.sinA])
        P.do("sp_dma", lambda e: e.dma_start(out=B.perm[:], in_=hp.ap()), writes=[B.perm])
    else:
        P.do("sp_dma", lambda e: e.dma_start(out=B.bias[:], in_=hb["abias"].ap()),
             writes=[B.bias])


def build_full(n_launch_phases=None):
    import contextlib
    progs = []

    def new_nc():
        return bass.Bass("TRN2", target_bir_lowering=False, debug=False)

    if FUSED:
        nc = new_nc()
        hb = _decl_inputs(nc, list(_W_SPECS))
        xin = nc.dram_tensor("xT", [1024, NTOK], F32, kind="ExternalInput")
        yout = nc.dram_tensor("yT", [1024, NTOK], F32, kind="ExternalOutput")
        xr = [nc.dram_tensor(f"xr{i}", [1024, NTOK], F32, kind="Internal")
              for i in range(5)]
        xnsp = nc.dram_tensor("xnsp", [1024, NTOK], BF16, kind="Internal")
        zT = nc.dram_tensor("zT", [1024, NTOK], F32, kind="Internal")
        znT = nc.dram_tensor("znT", [1024, NTOK], BF16, kind="Internal")
        w1bf = nc.dram_tensor("w1bf", [8, 128, 16384], BF16, kind="Internal")
        w2bf = nc.dram_tensor("w2bf", [8, 128, 16384], BF16, kind="Internal")
        upbf = nc.dram_tensor("upbf", [2, 128, 16384], BF16, kind="Internal")
        dnbf = nc.dram_tensor("dnbf", [2, 128, 16384], BF16, kind="Internal")
        w4bf = nc.dram_tensor("w4bf", [4, 128, 8192], BF16, kind="Internal")
        es = contextlib.ExitStack()
        with es:
            P = Prog(nc, es)
            C = Consts(P)
            with _Scoped(P):
                B = AttnBufs(P, False)
                _emit_attn_consts(P, nc, B, hb, False)
                emit_phase_attn(P, C, B, xin.ap(), xr[0].ap(), hb["WqT0"].ap(),
                                hb["WkT0"].ap(), hb["WvT0"].ap(), hb["WoT0"].ap())
                P.barrier()
            P.recycle_dma_sems()
            with _Scoped(P):
                B = L0FFNBufs(P)
                cb1 = sb(P, "cvt1", [128, 16384], BF16)
                cb2 = cb1
                load_l0ffn_weights(P, B, {"gateT": hb["gateT"], "fc1T": hb["fc1T"],
                                          "w1T": hb["w1Tl0"], "w2T": hb["w2Tl0"],
                                          "fc2T": hb["fc2T"], "corr4": hb["corr4"],
                                          "iota32": hb["iota32"]})
                emit_weight_convert_l1(P, cb1, cb2, hb["w1Tl1"].ap(), hb["w2Tl1"].ap(),
                                       w1bf.ap(), w2bf.ap())
                for half in range(2):
                    emit_stage_bf16(P, cb1, hb["upT"].ap()[:, half * 2048:(half + 1) * 2048],
                                    upbf.ap()[half], (upbf, half))
                    emit_stage_bf16(P, cb1, hb["downT"].ap()[half * 2048:(half + 1) * 2048, :],
                                    dnbf.ap()[half], (dnbf, half))
                for wi, wn in enumerate(("WqT1", "WkT1", "WvT1", "WoT1")):
                    emit_stage_bf16(P, cb1, hb[wn].ap(),
                                    w4bf.ap()[wi], (w4bf, wi))
                emit_phase_l0ffn_routed(P, C, B, xr[0].ap(), xr[1].ap(), xnsp.ap())
                P.barrier()
            P.recycle_dma_sems()
            for half in (0, 1):
                with _Scoped(P):
                    B = SharedBufs(P)
                    load_wT(P, B.upT, upbf.ap()[half])
                    load_wT(P, B.downT, dnbf.ap()[half])
                    emit_phase_l0ffn_shared(P, C, B, xr[1 + half].ap(),
                                            xr[2 + half].ap(), xnsp.ap(), half)
                    P.barrier()
                P.recycle_dma_sems()
            with _Scoped(P):
                B = AttnBufs(P, True)
                _emit_attn_consts(P, nc, B, hb, True)
                nw_d = sb(P, "d_nw", [128, DC], F32)
                P.do("sp_dma", lambda e: e.dma_start(out=nw_d[:], in_=hb["nw"].ap()),
                     writes=[nw_d])
                emit_phase_attn(P, C, B, xr[3].ap(), xr[4].ap(), w4bf.ap()[0],
                                w4bf.ap()[1], w4bf.ap()[2], w4bf.ap()[3],
                                zprep=(zT.ap(), znT.ap(), nw_d))
                P.barrier()
            P.recycle_dma_sems()
            with _Scoped(P):
                E = L1FFNBufs(P)
                load_wT(P, E.wmacT, hb["wmacT"].ap())
                load_wT(P, E.wmicT, hb["wmicT"].ap())
                P.do("sp_dma", lambda e: e.dma_start(out=E.nw[:], in_=hb["nw"].ap()),
                     writes=[E.nw])
                emit_phase_l1ffn_loop(P, C, E, 0, xr[4].ap(), zT.ap(), znT.ap(), None,
                                      w1bf.ap(), w2bf.ap())
                emit_phase_l1ffn_loop(P, C, E, 1, xr[4].ap(), zT.ap(), znT.ap(), yout.ap(),
                                      w1bf.ap(), w2bf.ap())
            P.finish()
        ins = list(_W_SPECS) + ["xT"]
        progs.append((nc, ins, ["yT"]))
        return progs

    def mk_io(nc, xname="xT", xdt=F32):
        xin = nc.dram_tensor(xname, [1024, NTOK], xdt, kind="ExternalInput")
        xo = nc.dram_tensor("xo", [1024, NTOK], F32, kind="ExternalOutput")
        return xin, xo

    nc = new_nc()
    hb = _decl_inputs(nc, ["WqT0", "WkT0", "WvT0", "WoT0", "abias"])
    xin, xo = mk_io(nc)
    es = contextlib.ExitStack()
    with es:
        P = Prog(nc, es)
        C = Consts(P)
        B = AttnBufs(P, False)
        _emit_attn_consts(P, nc, B, hb, False)
        emit_phase_attn(P, C, B, xin.ap(), xo.ap(), hb["WqT0"].ap(), hb["WkT0"].ap(),
                        hb["WvT0"].ap(), hb["WoT0"].ap())
        P.finish()
    progs.append((nc, ["WqT0", "WkT0", "WvT0", "WoT0", "abias", "xT"], ["xo"]))

    nc = new_nc()
    hb = _decl_inputs(nc, ["gateT", "fc1T", "w1Tl0", "w2Tl0", "fc2T", "corr4", "iota32"])
    xin, xo = mk_io(nc)
    xnsp = nc.dram_tensor("xnsp", [1024, NTOK], BF16, kind="ExternalOutput")
    es = contextlib.ExitStack()
    with es:
        P = Prog(nc, es)
        C = Consts(P)
        B = L0FFNBufs(P)
        load_l0ffn_weights(P, B, {"gateT": hb["gateT"], "fc1T": hb["fc1T"],
                                  "w1T": hb["w1Tl0"], "w2T": hb["w2Tl0"],
                                  "fc2T": hb["fc2T"], "corr4": hb["corr4"],
                                  "iota32": hb["iota32"]})
        emit_phase_l0ffn_routed(P, C, B, xin.ap(), xo.ap(), xnsp.ap())
        P.finish()
    progs.append((nc, ["gateT", "fc1T", "w1Tl0", "w2Tl0", "fc2T", "corr4", "iota32",
                       "xT"], ["xo", "xnsp"]))

    for half in (0, 1):
        nc = new_nc()
        hb = _decl_inputs(nc, ["upT", "downT"])
        xin, xo = mk_io(nc)
        xnsp = nc.dram_tensor("xnsp", [1024, NTOK], BF16, kind="ExternalInput")
        es = contextlib.ExitStack()
        with es:
            P = Prog(nc, es)
            C = Consts(P)
            B = SharedBufs(P)
            load_shared_weights(P, B, hb, half)
            emit_phase_l0ffn_shared(P, C, B, xin.ap(), xo.ap(), xnsp.ap(), half)
            P.finish()
        progs.append((nc, ["upT", "downT", "xT", "xnsp"], ["xo"]))

    nc = new_nc()
    hb = _decl_inputs(nc, ["WqT1", "WkT1", "WvT1", "WoT1"])
    xin, xo = mk_io(nc)
    es = contextlib.ExitStack()
    with es:
        P = Prog(nc, es)
        C = Consts(P)
        B = AttnBufs(P, True)
        _emit_attn_consts(P, nc, B, hb, True)
        emit_phase_attn(P, C, B, xin.ap(), xo.ap(), hb["WqT1"].ap(), hb["WkT1"].ap(),
                        hb["WvT1"].ap(), hb["WoT1"].ap())
        P.finish()
    progs.append((nc, ["WqT1", "WkT1", "WvT1", "WoT1", "xT"], ["xo"]))

    nc = new_nc()
    hb = _decl_inputs(nc, ["wmacT", "wmicT", "nw", "w1Tl1", "w2Tl1"])
    xin, xo = mk_io(nc)
    zT = nc.dram_tensor("zT", [1024, NTOK], F32, kind="Internal")
    w1bf = nc.dram_tensor("w1bf", [8, 128, 16384], BF16, kind="Internal")
    w2bf = nc.dram_tensor("w2bf", [8, 128, 16384], BF16, kind="Internal")
    es = contextlib.ExitStack()
    with es:
        P = Prog(nc, es)
        C = Consts(P)
        E = L1FFNBufs(P)
        load_wT(P, E.wmacT, hb["wmacT"].ap())
        load_wT(P, E.wmicT, hb["wmicT"].ap())
        P.do("sp_dma", lambda e: e.dma_start(out=E.nw[:], in_=hb["nw"].ap()),
             writes=[E.nw])
        emit_weight_convert_l1(P, E.w1sb, E.w2sb, hb["w1Tl1"].ap(), hb["w2Tl1"].ap(),
                               w1bf.ap(), w2bf.ap())
        znT2 = nc.dram_tensor("znT", [1024, NTOK], BF16, kind="Internal")
        emit_phase_l1ffn_loop(P, C, E, 0, xin.ap(), zT.ap(), znT2.ap(), None,
                              w1bf.ap(), w2bf.ap())
        emit_phase_l1ffn_loop(P, C, E, 1, xin.ap(), zT.ap(), znT2.ap(), xo.ap(),
                              w1bf.ap(), w2bf.ap())
        P.finish()
    progs.append((nc, ["wmacT", "wmicT", "nw", "w1Tl1", "w2Tl1", "xT"], ["xo"]))
    return progs


_CACHED = {"progs": None, "fused": None}


def kernel(**inputs):
    g = host_prep(inputs)
    x = inputs["L0_attn_x"]
    xT = [np.ascontiguousarray(x[4 * c:4 * c + 4].transpose(2, 0, 1).reshape(1024, NTOK))
          for c in range(8)]
    if _CACHED["progs"] is None or _CACHED["fused"] != FUSED:
        _CACHED["progs"] = build_full()
        _CACHED["fused"] = FUSED
    progs = _CACHED["progs"]
    state = {c: {"xT": xT[c]} for c in range(8)}
    for nc, ins, outs in progs:
        in_maps = []
        for c in range(8):
            m = {}
            for n in ins:
                if n in g:
                    m[n] = g[n]
                else:
                    m[n] = state[c][n]
            in_maps.append(m)
        res = run_bass_kernel_spmd(nc, in_maps, core_ids=list(range(8)))
        for c in range(8):
            r = res.results[c]
            if "yT" in r:
                state[c]["xT"] = r["yT"]
            elif "xo" in r:
                state[c]["xT"] = r["xo"]
            if "xnsp" in r:
                state[c]["xnsp"] = r["xnsp"]
    out = np.empty((32, 2048, 1024), dtype=np.float32)
    for c in range(8):
        out[4 * c:4 * c + 4] = np.asarray(state[c]["xT"], dtype=np.float32).reshape(
            1024, 4, 2048).transpose(1, 2, 0)
    return out


# version 28
# speedup vs baseline: 1.1043x; 1.1043x over previous
import numpy as np
import ml_dtypes
import concourse.bass as bass
import concourse.mybir as mybir
from concourse.bass_utils import run_bass_kernel_spmd

F32 = mybir.dt.float32
BF16 = mybir.dt.bfloat16
AF = mybir.ActivationFunctionType
ALU = mybir.AluOpType
AX = mybir.AxisListType
BF = ml_dtypes.bfloat16

D = 1024
S = 2048
NSEQ = 4
NTOK = NSEQ * S
NT = 512
NTILES = NTOK // NT
DC = D // 128
H = 16
HD = 64


class Prog:

    def __init__(self, nc, es, n_dma_sems=48):
        self.nc = nc
        self.es = es
        self.q_eng = {
            "pe": nc.tensor, "act": nc.scalar, "dve": nc.vector,
            "pool": nc.gpsimd, "sp": nc.sync,
            "sp_dma": nc.sync, "act_dma": nc.scalar, "pool_dma": nc.gpsimd,
        }
        self.qsem = {q: es.enter_context(nc.semaphore(f"q_{q}"))
                     for q in ("pe", "act", "dve", "pool", "sp")}
        self.cnt = {q: 0 for q in self.qsem}
        self.dma_pool = [es.enter_context(nc.semaphore(f"dsem{i}"))
                         for i in range(n_dma_sems)]
        self.dma_free = list(self.dma_pool)
        self.semval = {id(s): 0 for s in self.dma_pool}
        self.buf_sem = {}
        self.state = {}
        self.waited = {}

    def _key(self, buf):
        if isinstance(buf, tuple):
            return (id(buf[0]), buf[1])
        return id(buf)

    def _tid(self, buf):
        return id(buf[0]) if isinstance(buf, tuple) else id(buf)

    def _bufstate(self, buf):
        k = self._key(buf)
        if k not in self.state:
            self.state[k] = {"w": None, "r": {}}
        return self.state[k]

    def _dma_sem(self, buf, q):
        t = (self._tid(buf), "sw" if q == "pool_dma" else "hw")
        if t not in self.buf_sem:
            if not self.dma_free:
                raise RuntimeError("out of DMA semaphores; call recycle_dma_sems()")
            self.buf_sem[t] = self.dma_free.pop()
        return self.buf_sem[t]

    def _wait(self, q, sem, val, dep_q):
        if val <= 0:
            return
        if dep_q is None:
            val = max(val, self.semval[id(sem)])
        if self.waited.get((q, id(sem)), 0) >= val:
            return
        self.q_eng[q].wait_ge(sem, val)
        self.waited[(q, id(sem))] = val

    SAME_Q_SKIP = {"pe", "act"}

    def _dep_wait(self, q, dep):
        sem, val, dep_q = dep
        if dep_q == q and q in self.SAME_Q_SKIP:
            return
        self._wait(q, sem, val, dep_q if not (dep_q and dep_q.endswith("_dma")) else None)

    def do(self, q, fn, reads=(), writes=()):
        for b in reads:
            st = self._bufstate(b)
            if st["w"] is not None:
                self._dep_wait(q, st["w"])
        for b in writes:
            st = self._bufstate(b)
            if st["w"] is not None:
                self._dep_wait(q, st["w"])
            for dep in st["r"].values():
                self._dep_wait(q, dep)
        inst = fn(self.q_eng[q])
        if q.endswith("_dma"):
            sem = self._dma_sem(writes[0], q)
            self.semval[id(sem)] += 16
            val = self.semval[id(sem)]
            rec = (sem, val, q)
        else:
            self.cnt[q] += 1
            val = self.cnt[q]
            sem = self.qsem[q]
            rec = (sem, val, q)
        inst.then_inc(sem, 16 if q.endswith("_dma") else 1)
        for b in reads:
            st = self._bufstate(b)
            st["r"][id(sem)] = rec
        for b in writes:
            st = self._bufstate(b)
            st["w"] = rec
            st["r"] = {}
        return inst

    def barrier(self):
        for q in self.qsem:
            for q2, c in self.cnt.items():
                if q2 != q:
                    self._wait(q, self.qsem[q2], c, q2)
            for s in self.dma_pool:
                self._wait(q, s, self.semval[id(s)], None)

    def recycle_dma_sems(self):
        self.buf_sem = {}
        self.dma_free = list(self.dma_pool)

    def finish(self, q="pe"):
        for q2, c in self.cnt.items():
            if q2 != q:
                self._wait(q, self.qsem[q2], c, q2)
        for s in self.dma_pool:
            self._wait(q, s, self.semval[id(s)], None)


_NAME_SEQ = [0]


def _uname(name):
    _NAME_SEQ[0] += 1
    return f"{name}_{_NAME_SEQ[0]}"


def sb(P, name, shape, dtype):
    return P.es.enter_context(P.nc.sbuf_tensor(_uname(name), shape, dtype))


def ps(P, name, shape=(128, 512), dtype=F32):
    return P.es.enter_context(P.nc.psum_tensor(_uname(name), list(shape), dtype))


def emit_linear(P, out_psum_fn, lhsT_fn, rhs_fn, n_k, reads, tag=""):
    def fn(t):
        last = None
        for k in range(n_k):
            last = t.matmul(out_psum_fn(), lhsT_fn(k), rhs_fn(k),
                            start=(k == 0), stop=(k == n_k - 1))
        return last
    return fn


class Consts:

    def __init__(self, P):
        nc = P.nc
        self.onesc = sb(P, "c_onesc", [128, 1], BF16)
        self.onesr = sb(P, "c_onesr", [1, 128], F32)
        self.iota8 = sb(P, "c_iota8", [128, 8], F32)
        self.ident = sb(P, "c_ident", [128, 128], F32)
        h_onesc = nc.inline_tensor(np.ones((128, 1), dtype=BF), name=_uname("h_onesc"))
        h_onesr = nc.inline_tensor(np.ones((1, 128), dtype=np.float32), name=_uname("h_onesr"))
        self.onesrb = sb(P, "c_onesrb", [1, 128], BF16)
        h_onesrb = nc.inline_tensor(np.ones((1, 128), dtype=BF), name=_uname("h_onesrb"))
        h_iota8 = nc.inline_tensor(
            np.tile(np.arange(8, dtype=np.float32), (128, 1)).copy(), name=_uname("h_iota8"))
        h_ident = nc.inline_tensor(np.eye(128, dtype=np.float32), name=_uname("h_ident"))
        self.sel8 = sb(P, "c_sel8", [8, 8 * 128], BF16)
        sel_np = np.zeros((8, 8 * 128), dtype=BF)
        for e in range(8):
            sel_np[e, e * 128:(e + 1) * 128] = 1.0
        h_sel8 = nc.inline_tensor(sel_np, name=_uname("h_sel8"))
        for dst, src in ((self.onesc, h_onesc), (self.onesr, h_onesr),
                         (self.onesrb, h_onesrb),
                         (self.iota8, h_iota8), (self.ident, h_ident),
                         (self.sel8, h_sel8)):
            P.do("sp_dma", lambda e, d=dst, s=src: e.dma_start(out=d[:], in_=s.ap()),
                 writes=[d for d in (dst,)])
        P._ln_rows = sb(P, "ln_rows", [1, 3 * NT], F32)


def emit_stats_ln(P, C, xb, sq, stat_ps, brd_ps, aB, bB, eps=1e-5, n_chunks=DC,
                  rms_only=False, denom=None):
    nt = NT
    dn = denom if denom is not None else n_chunks * 128
    P.do("dve", lambda v: v.tensor_tensor(sq[:], xb[:], xb[:], ALU.mult),
         reads=[xb], writes=[sq])
    if not rms_only:
        P.do("pe", emit_linear(P, lambda: stat_ps[0:1, 0:nt],
                               lambda k: C.onesc[:],
                               lambda k: xb[:, k * nt:(k + 1) * nt], n_chunks, None),
             reads=[xb, C.onesc], writes=[stat_ps])
    P.do("pe", emit_linear(P, lambda: stat_ps[64:65, 0:nt],
                           lambda k: C.onesc[:],
                           lambda k: sq[:, k * nt:(k + 1) * nt], n_chunks, None),
         reads=[sq, C.onesc], writes=[stat_ps])
    row = getattr(P, "_ln_rows", None)
    if row is None:
        row = sb(P, "ln_rows", [1, 3 * NT], F32)
        P._ln_rows = row
    m = row[0:1, 0:nt]
    v2 = row[0:1, nt:2 * nt]
    inv = row[0:1, 2 * nt:3 * nt]
    if rms_only:
        P.do("dve", lambda v: v.tensor_scalar(v2, stat_ps[64:65, 0:nt],
                                              1.0 / dn, eps, ALU.mult, ALU.add),
             reads=[stat_ps], writes=[row])
        P.do("act", lambda s: s.activation(inv, v2, AF.Ln), reads=[row], writes=[row])
        P.do("act", lambda s: s.activation(inv, inv, AF.Exp, scale=-0.5),
             reads=[row], writes=[row])
    else:
        P.do("dve", lambda v: v.tensor_scalar_mul(m, stat_ps[0:1, 0:nt], 1.0 / dn),
             reads=[stat_ps], writes=[row])
        P.do("dve", lambda v: v.tensor_scalar_mul(v2, stat_ps[64:65, 0:nt], 1.0 / dn),
             reads=[stat_ps], writes=[row])
        P.do("dve", lambda v: v.tensor_mul(inv, m, m), reads=[row], writes=[row])
        P.do("dve", lambda v: v.tensor_sub(v2, v2, inv), reads=[row], writes=[row])
        P.do("dve", lambda v: v.tensor_scalar_add(v2, v2, eps), reads=[row], writes=[row])
        P.do("act", lambda s: s.activation(inv, v2, AF.Ln), reads=[row], writes=[row])
        P.do("act", lambda s: s.activation(inv, inv, AF.Exp, scale=-0.5),
             reads=[row], writes=[row])
        P.do("dve", lambda v: v.tensor_mul(m, m, inv), reads=[row], writes=[row])
        P.do("dve", lambda v: v.tensor_scalar_mul(m, m, -1.0), reads=[row], writes=[row])
    P.do("pe", lambda t: t.matmul(brd_ps[:, 0:nt], C.onesr[:], inv,
                                  start=True, stop=True),
         reads=[row, C.onesr], writes=[brd_ps])
    P.do("act", lambda s: s.copy(aB[:], brd_ps[:, 0:nt]), reads=[brd_ps], writes=[aB])
    if not rms_only:
        P.do("pe", lambda t: t.matmul(brd_ps[:, 0:nt], C.onesr[:], m,
                                      start=True, stop=True),
             reads=[row, C.onesr], writes=[brd_ps])
        P.do("act", lambda s: s.copy(bB[:], brd_ps[:, 0:nt]), reads=[brd_ps], writes=[bB])


def emit_ln_apply(P, xb, aB, bB, xn, n_chunks=DC):
    for c in range(n_chunks):
        sl = slice(c * NT, (c + 1) * NT)
        P.do("dve", lambda v, sl=sl: v.tensor_tensor(xn[:, sl], xb[:, sl], aB[:], ALU.mult),
             reads=[xb, aB], writes=[xn])
        if bB is not None:
            P.do("dve", lambda v, sl=sl: v.tensor_tensor(xn[:, sl], xn[:, sl], bB[:], ALU.add),
                 reads=[xn, bB], writes=[xn])


def load_wT(P, dst_sb, hbm_ap, q="pool_dma", col0=0):
    r, c = hbm_ap.shape
    if r == 128:
        P.do("sp_dma", lambda e: e.dma_start(out=dst_sb[:, col0:col0 + c], in_=hbm_ap),
             writes=[dst_sb])
        return
    n = r // 128
    src = hbm_ap.rearrange("(n p) c -> p n c", p=128)
    dst = dst_sb[:, col0:col0 + n * c].rearrange("p (n c) -> p n c", c=c)
    P.do(q, lambda e: e.dma_start(out=dst, in_=src), writes=[dst_sb])


def emit_dma_tile(P, q, sbuf, hbm, t, write=False, subkeys=None):
    src = hbm.rearrange("(c p) n -> p c n", p=128)[:, :, t * NT:(t + 1) * NT]
    keys = [sbuf] if subkeys is None else [(sbuf, k) for k in subkeys]
    if write:
        P.do(q, lambda e: e.dma_start(out=src, in_=sbuf[:].rearrange("p (c n) -> p c n", n=NT)),
             reads=keys, writes=[(hbm.tensor, t)])
    else:
        P.do(q, lambda e: e.dma_start(out=sbuf[:].rearrange("p (c n) -> p c n", n=NT), in_=src),
             reads=[(hbm.tensor, t)], writes=keys)


class L0FFNBufs:
    def __init__(self, P):
        self.gateT = sb(P, "b_gateT", [128, DC * 8], BF16)
        self.corr = sb(P, "b_corr", [128, 32], F32)
        self.iota32 = sb(P, "b_iota32", [128, 32], F32)
        self.fc1T = sb(P, "b_fc1T", [128, DC * 512], BF16)
        self.w1T = sb(P, "b_w1T", [128, 8 * 2048], BF16)
        self.w2T = sb(P, "b_w2T", [128, 8 * 2048], BF16)
        self.fc2T = sb(P, "b_fc2T", [128, 4 * 1024], BF16)
        self.xf = [sb(P, f"b_xf{i}", [128, DC * NT], F32) for i in range(2)]
        self.xb = sb(P, "b_xb", [128, DC * NT], BF16)
        self.aB = sb(P, "b_aB", [128, NT], BF16)
        self.bB = sb(P, "b_bB", [128, NT], BF16)
        self.xn = sb(P, "b_xn", [128, DC * NT], BF16)
        self.s_sc = sb(P, "b_ssc", [128, 32], F32)
        self.r1 = sb(P, "b_r1", [128, 32], F32)
        self.r2 = sb(P, "b_r2", [128, 32], F32)
        self.r3 = sb(P, "b_r3", [128, 32], F32)
        self.rs = sb(P, "b_rs", [128, 6 * 4], F32)
        self.cw = sb(P, "b_cw", [128, 32], F32)
        self.cwT = sb(P, "b_cwT", [8, NT], BF16)
        self.cwB = sb(P, "b_cwB", [128, NT], BF16)
        self.latT = sb(P, "b_latT", [128, 4 * NT], BF16)
        self.hr2 = sb(P, "b_hr2", [128, 2 * NT], BF16)
        self.hs = sb(P, "b_hs", [128, 4 * NT], BF16)
        self.combT = sb(P, "b_combT", [128, 4 * NT], BF16)
        self.xo = sb(P, "b_xo", [128, DC * NT], F32)
        self.pb_comb = ps(P, "pb_comb", (128, 4 * 512))
        self.pb_h = ps(P, "pb_h01", (128, 2 * 512))
        self.pb_bc = ps(P, "pb_bc")
        self.pb_s = ps(P, "pb_s")


def load_l0ffn_weights(P, B, hb):
    load_wT(P, B.gateT, hb["gateT"].ap())
    load_wT(P, B.fc1T, hb["fc1T"].ap())
    for e in range(8):
        load_wT(P, B.w1T, hb["w1T"].ap()[e], col0=e * 2048)
        load_wT(P, B.w2T, hb["w2T"].ap()[e], col0=e * 2048)
    load_wT(P, B.fc2T, hb["fc2T"].ap())
    P.do("sp_dma", lambda e: e.dma_start(out=B.corr[:], in_=hb["corr4"].ap()),
         writes=[B.corr])
    P.do("sp_dma", lambda e: e.dma_start(out=B.iota32[:], in_=hb["iota32"].ap()),
         writes=[B.iota32])


def emit_routing_l0(P, C, B):
    v = "dve"
    sc, r1, r2, r3, cw = B.s_sc, B.r1, B.r2, B.r3, B.cw
    rs = B.rs
    def grp(buf, g):
        return buf[:].rearrange("p (tc e) -> p tc e", e=8)[:, :, g * 4:(g + 1) * 4]
    def full(buf):
        return buf[:].rearrange("p (tc e) -> p tc e", e=8)
    def col(buf, j):
        return buf[:, j:32:8]
    def rsv(i):
        return rs[:, i * 4:(i + 1) * 4]
    def rsb(i, n=8):
        return rs[:, i * 4:(i + 1) * 4].unsqueeze(2).broadcast_to([128, 4, n])

    P.do("act", lambda s: s.activation(sc[:], B.pb_s[:, 0:32], AF.Sigmoid),
         reads=[B.pb_s], writes=[sc])
    P.do(v, lambda e: e.tensor_tensor(r1[:], sc[:], B.corr[:], ALU.add),
         reads=[sc, B.corr], writes=[r1])
    for g in range(2):
        first = True
        for i in range(4):
            for j in range(i + 1, 4):
                pair = rsv(5)
                P.do(v, lambda e, i=i, j=j, g=g, pair=pair: e.tensor_tensor(
                    pair, col(r1, g * 4 + i), col(r1, g * 4 + j), ALU.add),
                    reads=[r1], writes=[rs])
                if first:
                    P.do(v, lambda e, g=g, pair=pair: e.tensor_copy(rsv(g), pair),
                         reads=[rs], writes=[rs])
                    first = False
                else:
                    P.do(v, lambda e, g=g, pair=pair: e.tensor_tensor(
                        rsv(g), rsv(g), pair, ALU.max), reads=[rs], writes=[rs])
    P.do(v, lambda e: e.tensor_tensor(rsv(2), rsv(0), rsv(1), ALU.is_ge),
         reads=[rs], writes=[rs])
    P.do(v, lambda e: e.tensor_tensor(grp(r2, 0), grp(r1, 0), rsb(2, 4), ALU.mult),
         reads=[r1, rs], writes=[r2])
    P.do(v, lambda e: e.tensor_scalar(rsv(3), rsv(2), -1.0, 1.0, ALU.mult, ALU.add),
         reads=[rs], writes=[rs])
    P.do(v, lambda e: e.tensor_tensor(grp(r2, 1), grp(r1, 1), rsb(3, 4), ALU.mult),
         reads=[r1, rs], writes=[r2])
    P.do(v, lambda e: e.tensor_reduce(rsv(0), full(r2), axis=AX.X, op=ALU.max),
         reads=[r2], writes=[rs])
    P.do(v, lambda e: e.tensor_tensor(full(r3), full(r2), rsb(0), ALU.is_ge),
         reads=[r2, rs], writes=[r3])
    P.do(v, lambda e: e.scalar_tensor_tensor(r3[:], r3[:], -999.0, B.iota32[:],
                                             ALU.mult, ALU.add),
         reads=[r3, B.iota32], writes=[r3])
    P.do(v, lambda e: e.tensor_reduce(rsv(1), full(r3), axis=AX.X, op=ALU.min),
         reads=[r3], writes=[rs])
    P.do(v, lambda e: e.tensor_scalar_add(rsv(1), rsv(1), 999.0), reads=[rs], writes=[rs])
    P.do(v, lambda e: e.tensor_tensor(full(r3), full(B.iota32), rsb(1), ALU.is_equal),
         reads=[B.iota32, rs], writes=[r3])
    P.do(v, lambda e: e.scalar_tensor_tensor(cw[:], r3[:], -1e30, r2[:], ALU.mult, ALU.add),
         reads=[r3, r2], writes=[cw])
    P.do(v, lambda e: e.tensor_reduce(rsv(2), full(cw), axis=AX.X, op=ALU.max),
         reads=[cw], writes=[rs])
    P.do(v, lambda e: e.tensor_tensor(full(r2), full(cw), rsb(2), ALU.is_ge),
         reads=[cw, rs], writes=[r2])
    P.do(v, lambda e: e.scalar_tensor_tensor(r2[:], r2[:], -999.0, B.iota32[:],
                                             ALU.mult, ALU.add),
         reads=[r2, B.iota32], writes=[r2])
    P.do(v, lambda e: e.tensor_reduce(rsv(3), full(r2), axis=AX.X, op=ALU.min),
         reads=[r2], writes=[rs])
    P.do(v, lambda e: e.tensor_scalar_add(rsv(3), rsv(3), 999.0), reads=[rs], writes=[rs])
    P.do(v, lambda e: e.tensor_tensor(full(r2), full(B.iota32), rsb(3), ALU.is_equal),
         reads=[B.iota32, rs], writes=[r2])
    P.do(v, lambda e: e.tensor_tensor(cw[:], r3[:], sc[:], ALU.mult),
         reads=[r3, sc], writes=[cw])
    P.do(v, lambda e: e.tensor_reduce(rsv(0), full(cw), axis=AX.X, op=ALU.add),
         reads=[cw], writes=[rs])
    P.do(v, lambda e: e.tensor_tensor(cw[:], r2[:], sc[:], ALU.mult),
         reads=[r2, sc], writes=[cw])
    P.do(v, lambda e: e.tensor_reduce(rsv(2), full(cw), axis=AX.X, op=ALU.add),
         reads=[cw], writes=[rs])
    P.do(v, lambda e: e.tensor_tensor(rsv(4), rsv(0), rsv(2), ALU.add),
         reads=[rs], writes=[rs])
    P.do(v, lambda e: e.tensor_scalar_add(rsv(4), rsv(4), 1e-20), reads=[rs], writes=[rs])
    P.do(v, lambda e: e.reciprocal(rsv(5), rsv(4)), reads=[rs], writes=[rs])
    P.do(v, lambda e: e.tensor_tensor(rsv(4), rsv(0), rsv(5), ALU.mult),
         reads=[rs], writes=[rs])
    P.do(v, lambda e: e.tensor_tensor(rsv(5), rsv(2), rsv(5), ALU.mult),
         reads=[rs], writes=[rs])
    P.do(v, lambda e: e.tensor_tensor(full(r3), full(r3), rsb(4), ALU.mult),
         reads=[r3, rs], writes=[r3])
    P.do(v, lambda e: e.tensor_tensor(full(r2), full(r2), rsb(5), ALU.mult),
         reads=[r2, rs], writes=[r2])
    P.do(v, lambda e: e.tensor_tensor(cw[:], r3[:], r2[:], ALU.add),
         reads=[r3, r2], writes=[cw])
    for tc in range(4):
        P.do("pe", lambda t, tc=tc: t.transpose(B.pb_s[0:8, 128:256], cw[:, tc * 8:tc * 8 + 8],
                                                C.ident[:]),
             reads=[cw, C.ident], writes=[B.pb_s])
        P.do(v, lambda e, tc=tc: e.tensor_copy(B.cwT[0:8, tc * 128:(tc + 1) * 128],
                                               B.pb_s[0:8, 128:256]),
             reads=[B.pb_s], writes=[B.cwT])


def emit_phase_l0ffn_routed(P, C, B, x_in, x_out, xn_spill, tiles=range(NTILES)):
    for t in tiles:
        xf = B.xf[t % 2]
        emit_dma_tile(P, "sp_dma", xf, x_in, t)
        P.do("dve", lambda v: v.tensor_copy(B.xb[:], xf[:]), reads=[xf], writes=[B.xb])
        emit_stats_ln(P, C, B.xb, B.xn, B.pb_s, B.pb_bc, B.aB, B.bB)
        emit_ln_apply(P, B.xb, B.aB, B.bB, B.xn)
        emit_dma_tile(P, "pool_dma", B.xn, xn_spill, t, write=True)

        def routing():
            for tc in range(4):
                P.do("pe", emit_linear(
                    P, lambda tc=tc: B.pb_s[:, tc * 8:(tc + 1) * 8],
                    lambda k, tc=tc: B.xn[:, k * NT + tc * 128:k * NT + tc * 128 + 128],
                    lambda k: B.gateT[:, k * 8:(k + 1) * 8], DC, None),
                    reads=[B.xn, B.gateT], writes=[B.pb_s])
            emit_routing_l0(P, C, B)
        for lc in range(4):
            pb = B.pb_h[:, (lc % 2) * 512:(lc % 2) * 512 + 512]
            hk = (B.pb_h, lc % 2)
            P.do("pe", emit_linear(
                P, lambda pb=pb: pb,
                lambda k, lc=lc: B.fc1T[:, k * 512 + lc * 128:k * 512 + lc * 128 + 128],
                lambda k: B.xn[:, k * NT:(k + 1) * NT], DC, None),
                reads=[B.xn, B.fc1T], writes=[hk])
            P.do("act", lambda s, lc=lc, pb=pb: s.copy(B.latT[:, lc * NT:(lc + 1) * NT], pb),
                 reads=[hk], writes=[B.latT])
            if lc == 0:
                routing()
        for e in range(8):
            P.do("pe", lambda t_, e=e: t_.matmul(B.pb_bc[:], C.sel8[:, e * 128:(e + 1) * 128],
                                                 B.cwT[:], start=True, stop=True),
                 reads=[B.cwT, C.sel8], writes=[B.pb_bc])
            P.do("act", lambda s: s.copy(B.cwB[:], B.pb_bc[:]),
                 reads=[B.pb_bc], writes=[B.cwB])
            def he1pair(p, e=e):
                def f(t_):
                    last = None
                    for icx in range(2):
                        ic = 2 * p + icx
                        for k in range(4):
                            last = t_.matmul(
                                B.pb_h[:, icx * 512:(icx + 1) * 512],
                                B.w1T[:, e * 2048 + k * 512 + ic * 128:
                                      e * 2048 + k * 512 + ic * 128 + 128],
                                B.latT[:, k * NT:(k + 1) * NT],
                                start=(k == 0), stop=(k == 3))
                    return last
                P.do("pe", f, reads=[B.latT, B.w1T],
                     writes=[(B.pb_h, 0), (B.pb_h, 1)])
                P.do("act", lambda s: s.activation(B.hr2[:], B.pb_h[:], AF.Relu),
                     reads=[(B.pb_h, 0), (B.pb_h, 1)], writes=[B.hr2])
                sl = slice(2 * p * NT, (2 * p + 2) * NT)
                P.do("dve", lambda v, sl=sl: v.tensor_tensor(
                    B.hs[:, sl], B.hr2[:], B.hr2[:], ALU.mult),
                    reads=[B.hr2], writes=[(B.hs, p)])
                P.do("dve", lambda v, sl=sl: v.tensor_tensor(
                    B.hs[:, sl].rearrange("p (a n) -> p a n", n=NT),
                    B.hs[:, sl].rearrange("p (a n) -> p a n", n=NT),
                    B.cwB[:].unsqueeze(1).broadcast_to([128, 2, NT]), ALU.mult),
                    reads=[(B.hs, p), B.cwB], writes=[(B.hs, p)])

            def he2pair(p, e=e):
                def f(t_):
                    last = None
                    for icx in range(2):
                        ic = 2 * p + icx
                        for lc in range(4):
                            last = t_.matmul(
                                B.pb_comb[:, lc * 512:(lc + 1) * 512],
                                B.w2T[:, e * 2048 + ic * 512 + lc * 128:
                                      e * 2048 + ic * 512 + lc * 128 + 128],
                                B.hs[:, ic * NT:(ic + 1) * NT],
                                start=(e == 0 and ic == 0), stop=(e == 7 and ic == 3))
                    return last
                P.do("pe", f, reads=[(B.hs, p), B.w2T], writes=[B.pb_comb])

            he1pair(0)
            he1pair(1)
            he2pair(0)
            he2pair(1)
        for lc in range(4):
            P.do("act", lambda s, lc=lc: s.copy(B.combT[:, lc * NT:(lc + 1) * NT],
                                                B.pb_comb[:, lc * 512:(lc + 1) * 512]),
                 reads=[B.pb_comb], writes=[B.combT])
        for co in range(8):
            pb = B.pb_h[:, (co % 2) * 512:(co % 2) * 512 + 512]
            hk = (B.pb_h, co % 2)
            P.do("pe", emit_linear(
                P, lambda pb=pb: pb,
                lambda k, co=co: B.fc2T[:, k * 1024 + co * 128:k * 1024 + co * 128 + 128],
                lambda k: B.combT[:, k * NT:(k + 1) * NT], 4, None),
                reads=[B.combT, B.fc2T], writes=[hk])
            P.do("dve", lambda v, co=co, pb=pb, xf=xf: v.tensor_tensor(
                B.xo[:, co * NT:(co + 1) * NT], pb, xf[:, co * NT:(co + 1) * NT], ALU.add),
                reads=[hk, xf], writes=[B.xo])
        emit_dma_tile(P, "sp_dma", B.xo, x_out, t, write=True)


class SharedBufs:
    def __init__(self, P):
        self.upT = sb(P, "s_upT", [128, DC * 2048], BF16)
        self.downT = sb(P, "s_downT", [128, 16 * 1024], BF16)
        self.xn = [sb(P, f"s_xn{i}", [128, DC * NT], BF16) for i in range(2)]
        self.xf = [sb(P, f"s_xf{i}", [128, DC * NT], F32) for i in range(2)]
        self.ur = sb(P, "s_ur", [128, NT], BF16)
        self.us = sb(P, "s_us", [128, 16 * NT], BF16)
        self.xo = sb(P, "s_xo", [128, DC * NT], F32)
        self.pb = [ps(P, f"s_pb{i}") for i in range(4)]


def load_shared_weights(P, B, hb, half):
    load_wT(P, B.upT, hb["upT"].ap()[:, half * 2048:(half + 1) * 2048])
    load_wT(P, B.downT, hb["downT"].ap()[half * 2048:(half + 1) * 2048, :])


def emit_phase_l0ffn_shared(P, C, B, x_in, x_out, xn_spill, half, tiles=range(NTILES)):
    for t in tiles:
        xn = B.xn[t % 2]
        xf = B.xf[t % 2]
        emit_dma_tile(P, "pool_dma", xn, xn_spill, t)
        emit_dma_tile(P, "sp_dma", xf, x_in, t)
        for fc in range(16):
            pb = B.pb[fc % 2]
            P.do("pe", emit_linear(
                P, lambda pb=pb: pb[:],
                lambda k, fc=fc: B.upT[:, k * 2048 + fc * 128:k * 2048 + fc * 128 + 128],
                lambda k, xn=xn: xn[:, k * NT:(k + 1) * NT], DC, None),
                reads=[xn, B.upT], writes=[pb])
            P.do("act", lambda s, pb=pb: s.activation(B.ur[:], pb[:], AF.Relu),
                 reads=[pb], writes=[B.ur])
            P.do("dve", lambda v, fc=fc: v.tensor_tensor(
                B.us[:, fc * NT:(fc + 1) * NT], B.ur[:], B.ur[:], ALU.mult),
                reads=[B.ur], writes=[B.us])
        for co in range(8):
            pb = B.pb[2 + co % 2]
            P.do("pe", emit_linear(
                P, lambda pb=pb: pb[:],
                lambda k, co=co: B.downT[:, k * 1024 + co * 128:k * 1024 + co * 128 + 128],
                lambda k: B.us[:, k * NT:(k + 1) * NT], 16, None),
                reads=[B.us, B.downT], writes=[pb])
            P.do("dve", lambda v, co=co, pb=pb, xf=xf: v.tensor_tensor(
                B.xo[:, co * NT:(co + 1) * NT], pb[:], xf[:, co * NT:(co + 1) * NT], ALU.add),
                reads=[pb, xf], writes=[B.xo])
        emit_dma_tile(P, "sp_dma", B.xo, x_out, t, write=True)


class AttnBufs:
    def __init__(self, P, rope: bool):
        self.Wa = sb(P, "a_Wa", [128, DC * 1024], BF16)
        self.Wb = sb(P, "a_Wb", [128, DC * 1024], BF16)
        self.xf = [sb(P, f"a_xf{i}", [128, DC * NT], F32) for i in range(2)]
        self.aB = sb(P, "a_aB", [128, NT], BF16)
        self.bB = sb(P, "a_bB", [128, NT], BF16)
        self.xn = sb(P, "a_xn", [128, DC * NT], BF16)
        self.KT = sb(P, "a_KT", [128, DC * S], BF16)
        self.Vaug = sb(P, "a_Vaug", [128, 16 * (16 * 65)], BF16)
        self.QT = sb(P, "a_QT", [128, DC * NT], BF16)
        self.ET = [sb(P, f"a_ET{i}", [128, 4 * NT], BF16) for i in range(2)]
        self.ctxT = sb(P, "a_ctxT", [128, DC * NT], BF16)
        self.rrow = sb(P, "a_rrow", [1, NT], F32)
        self.rBs = sb(P, "a_rBs", [64, NT], F32)
        self.masks = sb(P, "a_masks", [128, 4 * NT], BF16)
        self.xo = sb(P, "a_xo", [128, DC * NT], F32)
        self.rope = rope
        if rope:
            self.cosT = sb(P, "a_cosT", [128, S], BF16)
            self.sinA = sb(P, "a_sinA", [128, S], BF16)
            self.perm = sb(P, "a_perm", [128, 128], BF16)
            self.rtmp = [sb(P, f"a_rtmp{i}", [128, NT], BF16) for i in range(2)]
        else:
            self.bias = sb(P, "a_bias", [128, 4 * DC], F32)
        self.pb_q = ps(P, "a_pbq", (128, 4 * 512))
        self.pb_c = [ps(P, f"a_pbc{i}") for i in range(2)]
        self.pb_st = ps(P, "a_pbst")
        self.pb_bc = ps(P, "a_pbbc")


def attn_masks_np():
    m = np.zeros((128, 4 * NT), dtype=BF)
    for j in range(4):
        kk = np.arange(128)[:, None]
        qq = np.arange(NT)[None, :]
        m[:, j * NT:(j + 1) * NT] = (kk <= qq - 128 * j).astype(np.float32)
    return m


def rope_tables_np():
    inv = 1.0 / (10000.0 ** (np.arange(0, HD, 2, dtype=np.float64) / HD))
    t = np.arange(S, dtype=np.float64)
    fr = np.outer(inv, t)
    cos = np.cos(fr)
    sin = np.sin(fr)
    cosT = np.concatenate([cos, cos], 0)
    sinA = np.concatenate([-sin, sin], 0)
    cos128 = np.concatenate([cosT, cosT], 0).astype(np.float32)
    sin128 = np.concatenate([sinA, sinA], 0).astype(np.float32)
    perm = np.zeros((128, 128), dtype=np.float32)
    for b in range(2):
        for j in range(32):
            perm[b * 64 + j + 32, b * 64 + j] = 1.0
            perm[b * 64 + j, b * 64 + j + 32] = 1.0
    return cos128.astype(BF), sin128.astype(BF), perm.astype(BF)


def emit_rope(P, B, dst, dst_cols, pos0, pb, pbkey=None, par=0):
    if pbkey is None:
        pbkey = pb
    rt = B.rtmp[par % 2]
    sl = slice(dst_cols, dst_cols + NT)
    tsl = slice(pos0, pos0 + NT)
    P.do("pe", lambda t: t.matmul(pb[:], B.perm[:], dst[:, sl], start=True, stop=True),
         reads=[dst, B.perm], writes=[pbkey])
    P.do("dve", lambda v: v.tensor_tensor(rt[:], pb[:], B.sinA[:, tsl], ALU.mult),
         reads=[pbkey, B.sinA], writes=[rt])
    P.do("dve", lambda v: v.tensor_tensor(dst[:, sl], dst[:, sl], B.cosT[:, tsl], ALU.mult),
         reads=[dst, B.cosT], writes=[dst])
    P.do("dve", lambda v: v.tensor_tensor(dst[:, sl], dst[:, sl], rt[:], ALU.add),
         reads=[dst, rt], writes=[dst])


def emit_proj(P, B, W, xn, out_sb, co_range=range(DC), bias=None, bias_idx=0,
              evac_dtype_note="bf16"):
    for co in co_range:
        pb = B.pb_q[:, (co % 4) * 512:(co % 4) * 512 + 512]
        qk = (B.pb_q, co % 4)
        P.do("pe", emit_linear(
            P, lambda pb=pb: pb,
            lambda k, co=co, W=W: W[:, k * 1024 + co * 128:k * 1024 + co * 128 + 128],
            lambda k, xn=xn: xn[:, k * NT:(k + 1) * NT], DC, None),
            reads=[(xn, c2) for c2 in range(DC)] + [W], writes=[qk])
        if bias is not None:
            P.do("act", lambda s, co=co, pb=pb: s.activation(
                out_sb[:, co * NT:(co + 1) * NT], pb, AF.Identity,
                bias=B.bias[:, bias_idx * DC + co:bias_idx * DC + co + 1]),
                reads=[qk, B.bias], writes=[out_sb])
        else:
            P.do("act", lambda s, co=co, pb=pb: s.copy(
                out_sb[:, co * NT:(co + 1) * NT], pb),
                reads=[qk], writes=[out_sb])


def emit_ln_tile(P, C, B, xf, src_keys=None):
    nt = NT
    if src_keys is None:
        src_keys = [xf]
    ALLXN = [(B.xn, c) for c in range(DC)]
    P.do("dve", lambda v: v.tensor_copy(B.xn[:], xf[:]), reads=src_keys, writes=ALLXN)
    P.do("pe", emit_linear(P, lambda: B.pb_st[0:1, 0:nt], lambda k: C.onesc[:],
                           lambda k: B.xn[:, k * nt:(k + 1) * nt], DC, None),
         reads=ALLXN + [C.onesc], writes=[B.pb_st])
    P.do("dve", lambda v: v.tensor_tensor(B.xn[:], B.xn[:], B.xn[:], ALU.mult),
         reads=ALLXN, writes=ALLXN)
    P.do("pe", emit_linear(P, lambda: B.pb_st[64:65, 0:nt], lambda k: C.onesc[:],
                           lambda k: B.xn[:, k * nt:(k + 1) * nt], DC, None),
         reads=ALLXN + [C.onesc], writes=[B.pb_st])
    row = P._ln_rows
    m = row[0:1, 0:nt]
    v2 = row[0:1, nt:2 * nt]
    inv = row[0:1, 2 * nt:3 * nt]
    P.do("dve", lambda v: v.tensor_scalar_mul(m, B.pb_st[0:1, 0:nt], 1.0 / D),
         reads=[B.pb_st], writes=[row])
    P.do("dve", lambda v: v.tensor_scalar_mul(v2, B.pb_st[64:65, 0:nt], 1.0 / D),
         reads=[B.pb_st], writes=[row])
    P.do("dve", lambda v: v.tensor_mul(inv, m, m), reads=[row], writes=[row])
    P.do("dve", lambda v: v.tensor_sub(v2, v2, inv), reads=[row], writes=[row])
    P.do("dve", lambda v: v.tensor_scalar_add(v2, v2, 1e-5), reads=[row], writes=[row])
    P.do("act", lambda s: s.activation(inv, v2, AF.Ln), reads=[row], writes=[row])
    P.do("act", lambda s: s.activation(inv, inv, AF.Exp, scale=-0.5),
         reads=[row], writes=[row])
    P.do("dve", lambda v: v.tensor_mul(m, m, inv), reads=[row], writes=[row])
    P.do("dve", lambda v: v.tensor_scalar_mul(m, m, -1.0), reads=[row], writes=[row])
    P.do("pe", lambda t: t.matmul(B.pb_bc[:, 0:nt], C.onesr[:], inv, start=True, stop=True),
         reads=[row, C.onesr], writes=[B.pb_bc])
    P.do("act", lambda s: s.copy(B.aB[:], B.pb_bc[:, 0:nt]), reads=[B.pb_bc], writes=[B.aB])
    P.do("pe", lambda t: t.matmul(B.pb_bc[:, 0:nt], C.onesr[:], m, start=True, stop=True),
         reads=[row, C.onesr], writes=[B.pb_bc])
    P.do("act", lambda s: s.copy(B.bB[:], B.pb_bc[:, 0:nt]), reads=[B.pb_bc], writes=[B.bB])
    for c in range(DC):
        sl = slice(c * NT, (c + 1) * NT)
        P.do("dve", lambda v, sl=sl: v.tensor_tensor(B.xn[:, sl], xf[:, sl], B.aB[:],
                                                     ALU.mult),
             reads=src_keys + [B.aB], writes=[(B.xn, c)])
        P.do("dve", lambda v, sl=sl: v.tensor_tensor(B.xn[:, sl], B.xn[:, sl], B.bB[:],
                                                     ALU.add),
             reads=[(B.xn, c), B.bB], writes=[(B.xn, c)])


def emit_attn_zprep(P, C, B, t, i, zT, znT, nw):
    nt = NT
    xo = B.xo
    xokeys = [(B.xo, c) for c in range(DC)]
    xf = B.xf[i % 2]
    emit_ln_tile(P, C, B, xo, src_keys=xokeys)
    row = P._ln_rows
    inv = row[0:1, 2 * nt:3 * nt]
    r2 = row[0:1, nt:2 * nt]
    P.do("dve", lambda v: v.tensor_mul(r2, inv, inv), reads=[row], writes=[row])
    P.do("dve", lambda v: v.tensor_scalar(r2, r2, -1e-5, 1.0 + 1e-6, ALU.mult, ALU.add),
         reads=[row], writes=[row])
    P.do("act", lambda s: s.activation(r2, r2, AF.Ln), reads=[row], writes=[row])
    P.do("act", lambda s: s.activation(r2, r2, AF.Exp, scale=-0.5), reads=[row], writes=[row])
    for c in range(DC):
        sl = slice(c * NT, (c + 1) * NT)
        P.do("dve", lambda v, sl=sl: v.tensor_tensor(xf[:, sl], xo[:, sl], B.aB[:], ALU.mult),
             reads=[(B.xo, c), B.aB], writes=[xf])
        P.do("dve", lambda v, sl=sl: v.tensor_tensor(xf[:, sl], xf[:, sl], B.bB[:], ALU.add),
             reads=[xf, B.bB], writes=[xf])
    emit_dma_tile(P, "pool_dma", xf, zT, t, write=True)
    P.do("pe", lambda t_: t_.matmul(B.pb_bc[:, 0:nt], C.onesr[:], r2, start=True, stop=True),
         reads=[row, C.onesr], writes=[B.pb_bc])
    P.do("act", lambda s: s.copy(B.aB[:], B.pb_bc[:, 0:nt]), reads=[B.pb_bc], writes=[B.aB])
    for c in range(DC):
        sl = slice(c * NT, (c + 1) * NT)
        P.do("dve", lambda v, sl=sl: v.tensor_tensor(B.xn[:, sl], B.xn[:, sl], B.aB[:],
                                                     ALU.mult),
             reads=[(B.xn, c), B.aB], writes=[(B.xn, c)])
        P.do("act", lambda s, c=c, sl=sl: s.activation(B.xn[:, sl], B.xn[:, sl],
             AF.Identity, scale=nw[:, c:c + 1]),
             reads=[(B.xn, c), nw], writes=[(B.xn, c)])
    P.do("sp_dma", lambda e, t=t: e.dma_start(
        out=znT.rearrange("(c p) n -> p c n", p=128)[:, :, t * NT:(t + 1) * NT],
        in_=B.xn[:].rearrange("p (c n) -> p c n", n=NT)),
        reads=[(B.xn, c) for c in range(DC)], writes=[(znT.tensor, t)])


def emit_phase_attn(P, C, B, x_in, x_out, WqT, WkT, WvT, WoT, tiles_per_seq=4, seqs=None,
                    zprep=None):
    rope = B.rope
    P.do("dve", lambda v: v.memset(
        B.Vaug[:].rearrange("p (g x) -> p g x", x=16 * 65)[:, :, 64:16 * 65:65], 1.0),
        writes=[B.Vaug])
    if seqs is None:
        seqs = range(NSEQ)
    for s in seqs:
        load_wT(P, B.Wa, WkT)
        load_wT(P, B.Wb, WvT)
        for i in range(tiles_per_seq):
            t = s * tiles_per_seq + i
            xf = B.xf[i % 2]
            emit_dma_tile(P, "sp_dma", xf, x_in, t)
            emit_ln_tile(P, C, B, xf)
            for co in range(DC):
                pb = B.pb_q[:, (co % 4) * 512:(co % 4) * 512 + 512]
                qk = (B.pb_q, co % 4)
                P.do("pe", emit_linear(
                    P, lambda pb=pb: pb,
                    lambda k, co=co: B.Wa[:, k * 1024 + co * 128:k * 1024 + co * 128 + 128],
                    lambda k: B.xn[:, k * NT:(k + 1) * NT], DC, None),
                    reads=[(B.xn, c2) for c2 in range(DC)] + [B.Wa], writes=[qk])
                if rope:
                    P.do("act", lambda sc, co=co, pb=pb, i=i: sc.copy(
                        B.KT[:, co * S + i * NT:co * S + (i + 1) * NT], pb),
                        reads=[qk], writes=[B.KT])
                else:
                    P.do("act", lambda sc, co=co, pb=pb, i=i: sc.activation(
                        B.KT[:, co * S + i * NT:co * S + (i + 1) * NT], pb, AF.Identity,
                        bias=B.bias[:, 1 * DC + co:1 * DC + co + 1]),
                        reads=[qk, B.bias], writes=[B.KT])
            if rope:
                for co in range(DC):
                    emit_rope(P, B, B.KT, co * S + i * NT, i * NT,
                              B.pb_q[:, (co % 4) * 512:(co % 4) * 512 + 512],
                              (B.pb_q, co % 4), par=co)
            for tc in range(4):
                for dvh in range(2):
                    qd = (tc * 2 + dvh) % 4
                    pb = B.pb_q[:, qd * 512:qd * 512 + 512]
                    qk = (B.pb_q, qd)
                    P.do("pe", emit_linear(
                        P, lambda pb=pb: pb,
                        lambda k, tc=tc: B.xn[:, k * NT + tc * 128:k * NT + tc * 128 + 128],
                        lambda k, dvh=dvh: B.Wb[:, k * 1024 + dvh * 512:k * 1024 + (dvh + 1) * 512],
                        DC, None),
                        reads=[(B.xn, c2) for c2 in range(DC)] + [B.Wb], writes=[qk])
                    gtc = i * 4 + tc
                    h0 = dvh * 8
                    dst = B.Vaug[:, gtc * 1040 + h0 * 65:gtc * 1040 + h0 * 65 + 65 * 8]\
                        .rearrange("p (hh x) -> p hh x", x=65)[:, :, 0:64]
                    P.do("act", lambda sc, pb=pb, dst=dst: sc.copy(
                        dst, pb.rearrange("p (hh x) -> p hh x", x=64)),
                        reads=[qk], writes=[B.Vaug])
        load_wT(P, B.Wa, WqT)
        load_wT(P, B.Wb, WoT)
        for i in range(tiles_per_seq):
            t = s * tiles_per_seq + i
            xf = B.xf[i % 2]
            emit_dma_tile(P, "sp_dma", xf, x_in, t)
            emit_ln_tile(P, C, B, xf)
            if rope:
                emit_proj(P, B, B.Wa, B.xn, B.QT)
                for co in range(DC):
                    emit_rope(P, B, B.QT, co * NT, i * NT,
                              B.pb_q[:, (co % 4) * 512:(co % 4) * 512 + 512],
                              (B.pb_q, co % 4), par=co)
            else:
                emit_proj(P, B, B.Wa, B.xn, B.QT, bias=True, bias_idx=0)
            kc = 4 * i + 4

            def normalize_head(h):
                hrow = (h % 2) * 64
                P.do("act", lambda s_, h=h: s_.activation(
                    B.rBs[0:1, :], B.pb_c[h % 2][64:65, :], AF.Ln),
                    reads=[B.pb_c[h % 2]], writes=[B.rBs])
                P.do("act", lambda s_: s_.activation(
                    B.rrow[:], B.rBs[0:1, :], AF.Exp, scale=-1.0),
                    reads=[B.rBs], writes=[B.rrow])
                P.do("pe", lambda t_, h=h: t_.matmul(B.pb_bc[0:64, :], C.onesr[0:1, 0:64],
                                                     B.rrow[:], start=True, stop=True),
                     reads=[B.rrow, C.onesr], writes=[B.pb_bc])
                P.do("act", lambda sc: sc.copy(B.rBs[:], B.pb_bc[0:64, :]),
                     reads=[B.pb_bc], writes=[B.rBs])
                P.do("dve", lambda v, h=h, hrow=hrow: v.tensor_tensor(
                    B.ctxT[hrow:hrow + 64, (h // 2) * NT:(h // 2 + 1) * NT],
                    B.pb_c[h % 2][0:64, :], B.rBs[:], ALU.mult),
                    reads=[B.pb_c[h % 2], B.rBs], writes=[(B.ctxT, h)])
                if not rope:
                    P.do("act", lambda sc, h=h, hrow=hrow: sc.activation(
                        B.ctxT[hrow:hrow + 64, (h // 2) * NT:(h // 2 + 1) * NT],
                        B.ctxT[hrow:hrow + 64, (h // 2) * NT:(h // 2 + 1) * NT],
                        AF.Identity,
                        bias=B.bias[hrow:hrow + 64, 2 * DC + h // 2:2 * DC + h // 2 + 1]),
                        reads=[(B.ctxT, h), B.bias], writes=[(B.ctxT, h)])

            ng2 = kc // 2
            for hp in range(H // 2):
                h0, h1 = 2 * hp, 2 * hp + 1

                def sg(h, w, g):
                    hrow = (h % 2) * 64
                    def f(t_):
                        last = None
                        for jj in range(2):
                            j = 2 * g + jj
                            last = t_.matmul(
                                B.pb_q[:, (2 * w + jj) * 512:(2 * w + jj) * 512 + 512],
                                B.KT[hrow:hrow + 64,
                                     (h // 2) * S + j * 128:(h // 2) * S + j * 128 + 128],
                                B.QT[hrow:hrow + 64, (h // 2) * NT:(h // 2 + 1) * NT],
                                start=True, stop=True)
                        return last
                    return f

                def eg(w, g):
                    et = B.ET[w][:, (g % 2) * 1024:(g % 2) * 1024 + 1024]
                    P.do("act", lambda sc, et=et, w=w: sc.activation(
                        et, B.pb_q[:, 2 * w * 512:2 * w * 512 + 1024],
                        AF.Exp, bias=0.0, scale=0.125),
                        reads=[(B.pb_q, 2 * w), (B.pb_q, 2 * w + 1)],
                        writes=[(B.ET[w], g % 2)])
                    j0 = 2 * g
                    if j0 >= 4 * i:
                        m = j0 - 4 * i
                        P.do("dve", lambda v, et=et, m=m: v.tensor_tensor(
                            et, et, B.masks[:, m * NT:(m + 2) * NT], ALU.mult),
                            reads=[(B.ET[w], g % 2), B.masks], writes=[(B.ET[w], g % 2)])

                def cg(h, w, g):
                    def f(t_):
                        last = None
                        for jj in range(2):
                            j = 2 * g + jj
                            last = t_.matmul(
                                B.pb_c[w][0:65, :],
                                B.Vaug[:, j * 1040 + h * 65:j * 1040 + (h + 1) * 65],
                                B.ET[w][:, ((g % 2) * 2 + jj) * 512:((g % 2) * 2 + jj) * 512 + 512],
                                start=(j == 0), stop=(j == kc - 1))
                        return last
                    return f
                P.do("pe", sg(h0, 0, 0), reads=[B.KT, B.QT],
                     writes=[(B.pb_q, 0), (B.pb_q, 1)])
                P.do("pe", sg(h1, 1, 0), reads=[B.KT, B.QT],
                     writes=[(B.pb_q, 2), (B.pb_q, 3)])
                if hp > 0:
                    normalize_head(2 * hp - 2)
                    normalize_head(2 * hp - 1)
                eg(0, 0)
                eg(1, 0)
                for g in range(1, ng2):
                    P.do("pe", sg(h0, 0, g), reads=[B.KT, B.QT],
                         writes=[(B.pb_q, 0), (B.pb_q, 1)])
                    P.do("pe", sg(h1, 1, g), reads=[B.KT, B.QT],
                         writes=[(B.pb_q, 2), (B.pb_q, 3)])
                    eg(0, g)
                    eg(1, g)
                    P.do("pe", cg(h0, 0, g - 1), reads=[B.Vaug, (B.ET[0], (g - 1) % 2)],
                         writes=[B.pb_c[0]])
                    P.do("pe", cg(h1, 1, g - 1), reads=[B.Vaug, (B.ET[1], (g - 1) % 2)],
                         writes=[B.pb_c[1]])
                P.do("pe", cg(h0, 0, ng2 - 1), reads=[B.Vaug, (B.ET[0], (ng2 - 1) % 2)],
                     writes=[B.pb_c[0]])
                P.do("pe", cg(h1, 1, ng2 - 1), reads=[B.Vaug, (B.ET[1], (ng2 - 1) % 2)],
                     writes=[B.pb_c[1]])
            normalize_head(H - 2)
            normalize_head(H - 1)
            for co in range(DC):
                pb = B.pb_q[:, (co % 4) * 512:(co % 4) * 512 + 512]
                qk = (B.pb_q, co % 4)
                P.do("pe", emit_linear(
                    P, lambda pb=pb: pb,
                    lambda k, co=co: B.Wb[:, k * 1024 + co * 128:k * 1024 + co * 128 + 128],
                    lambda k: B.ctxT[:, k * NT:(k + 1) * NT], DC, None),
                    reads=[(B.ctxT, hh) for hh in range(H)] + [B.Wb], writes=[qk])
                bias_s = 0.0 if rope else B.bias[:, 3 * DC + co:3 * DC + co + 1]
                brd = [] if rope else [B.bias]
                P.do("dve", lambda v, co=co, pb=pb, xf=xf, bias_s=bias_s:
                     v.scalar_tensor_tensor(
                         B.xo[:, co * NT:(co + 1) * NT], pb, bias_s,
                         xf[:, co * NT:(co + 1) * NT], ALU.add, ALU.add),
                     reads=[qk, xf] + brd, writes=[(B.xo, co)])
            emit_dma_tile(P, "sp_dma", B.xo, x_out, t, write=True, subkeys=range(8))
            if zprep is not None:
                emit_attn_zprep(P, C, B, t, i, zprep[0], zprep[1], zprep[2])


class L1FFNBufs:
    def __init__(self, P):
        self.wmacT = sb(P, "e_wmacT", [128, DC * 4], BF16)
        self.wmicT = sb(P, "e_wmicT", [128, DC * 8], BF16)
        self.nw = sb(P, "e_nw", [128, DC], F32)
        self.w1sb = sb(P, "e_w1", [128, 8 * 2048], BF16)
        self.w2sb = sb(P, "e_w2", [128, 16 * 1024], BF16)
        self.z = [sb(P, f"e_z{i}", [128, DC * NT], F32) for i in range(2)]
        self.zb = [sb(P, f"e_zb{i}", [128, DC * NT], BF16) for i in range(2)]
        self.zn = [sb(P, f"e_zn{i}", [128, DC * NT], BF16) for i in range(2)]
        self.y = [sb(P, f"e_y{i}", [128, DC * NT], BF16) for i in range(2)]
        self.hs = sb(P, "e_hs", [128, 16 * NT], BF16)
        self.xr = sb(P, "e_xr", [128, DC * NT], F32)
        self.aB = sb(P, "e_aB", [128, NT], BF16)
        self.bB2 = sb(P, "e_bB2", [128, NT], BF16)
        self.sm = sb(P, "e_sm", [128, 16], F32)
        self.mi = sb(P, "e_mi", [128, 32], F32)
        self.r1 = sb(P, "e_r1", [128, 16], F32)
        self.r2 = sb(P, "e_r2", [128, 16], F32)
        self.r3 = sb(P, "e_r3", [128, 16], F32)
        self.r4 = sb(P, "e_r4", [128, 16], F32)
        self.rs = sb(P, "e_rs", [128, 4 * 4], F32)
        self.rows = sb(P, "e_rows", [1, 6 * NT], F32)
        self.cw = sb(P, "e_cw", [128, 32], F32)
        self.cwT = [sb(P, f"e_cwT{i}", [8, NT], BF16) for i in range(2)]
        self.cwB = sb(P, "e_cwB", [128, NT], BF16)
        self.pb_h = [ps(P, f"e_pbh{i}") for i in range(2)]
        self.pb_y = [ps(P, f"e_pby{i}") for i in range(2)]
        self.pb_r = ps(P, "e_pbr")
        self.pb_bc = ps(P, "e_pbbc")
        self.pb_bc2 = ps(P, "e_pbbc2")
        self.pb_t = ps(P, "e_pbt")


def emit_stage_bf16(P, buf, src_ap, dst_ap, key):
    load_wT(P, buf, src_ap)
    w = dst_ap.shape[-1]
    P.do("sp_dma", lambda e: e.dma_start(out=dst_ap, in_=buf[:, 0:w]),
         reads=[buf], writes=[key])


def emit_weight_convert_l1(P, buf1, buf2, w1T_hbm, w2T_hbm, w1bf, w2bf):
    for e in range(8):
        b = buf1 if e % 2 == 0 else buf2
        load_wT(P, b, w1T_hbm[e])
        P.do("sp_dma", lambda en, e=e, b=b: en.dma_start(out=w1bf[e], in_=b[:]),
             reads=[b], writes=[(w1bf.tensor, e)])
        load_wT(P, b, w2T_hbm[e])
        P.do("act_dma", lambda en, e=e, b=b: en.dma_start(out=w2bf[e], in_=b[:]),
             reads=[b], writes=[(w2bf.tensor, e)])


def emit_routing_l1(P, C, E, cwT):
    v = "dve"
    sm, r1, r2, r3, r4, rs, cw = E.sm, E.r1, E.r2, E.r3, E.r4, E.rs, E.cw

    def v4(buf):
        return buf[:].rearrange("p (tc e) -> p tc e", e=4)

    def v8(buf):
        return buf[:].rearrange("p (tc e) -> p tc e", e=8)

    def rsv(i):
        return rs[:, i * 4:(i + 1) * 4]

    def rsb(i, n=4):
        return rs[:, i * 4:(i + 1) * 4].unsqueeze(2).broadcast_to([128, 4, n])

    mac = E.pb_r[:, 0:16].rearrange("p (tc e) -> p tc e", e=4)
    iota4b = C.iota8[:, 0:4].unsqueeze(1).broadcast_to([128, 4, 4])
    P.do(v, lambda e: e.tensor_reduce(rsv(0), mac, axis=AX.X, op=ALU.max),
         reads=[E.pb_r], writes=[rs])
    P.do(v, lambda e: e.tensor_tensor(v4(sm), mac, rsb(0), ALU.subtract),
         reads=[E.pb_r, rs], writes=[sm])
    P.do("act", lambda s: s.activation(sm[:], sm[:], AF.Exp), reads=[sm], writes=[sm])
    P.do(v, lambda e: e.tensor_reduce(rsv(0), v4(sm), axis=AX.X, op=ALU.add),
         reads=[sm], writes=[rs])
    P.do(v, lambda e: e.reciprocal(rsv(1), rsv(0)), reads=[rs], writes=[rs])
    P.do(v, lambda e: e.tensor_tensor(v4(sm), v4(sm), rsb(1), ALU.mult),
         reads=[sm, rs], writes=[sm])
    P.do(v, lambda e: e.tensor_reduce(rsv(0), v4(sm), axis=AX.X, op=ALU.max),
         reads=[sm], writes=[rs])
    P.do(v, lambda e: e.tensor_tensor(v4(r1), v4(sm), rsb(0), ALU.is_ge),
         reads=[sm, rs], writes=[r1])
    P.do(v, lambda e: e.scalar_tensor_tensor(v4(r1), v4(r1), -999.0, iota4b,
                                             ALU.mult, ALU.add),
         reads=[r1, C.iota8], writes=[r1])
    P.do(v, lambda e: e.tensor_reduce(rsv(1), v4(r1), axis=AX.X, op=ALU.min),
         reads=[r1], writes=[rs])
    P.do(v, lambda e: e.tensor_scalar_add(rsv(1), rsv(1), 999.0), reads=[rs], writes=[rs])
    P.do(v, lambda e: e.tensor_tensor(v4(r1), iota4b, rsb(1), ALU.is_equal),
         reads=[C.iota8, rs], writes=[r1])
    P.do(v, lambda e: e.scalar_tensor_tensor(v4(r2), v4(r1), -1e30, v4(sm),
                                             ALU.mult, ALU.add),
         reads=[r1, sm], writes=[r2])
    P.do(v, lambda e: e.tensor_reduce(rsv(2), v4(r2), axis=AX.X, op=ALU.max),
         reads=[r2], writes=[rs])
    P.do(v, lambda e: e.tensor_tensor(v4(r2), v4(r2), rsb(2), ALU.is_ge),
         reads=[r2, rs], writes=[r2])
    P.do(v, lambda e: e.scalar_tensor_tensor(v4(r2), v4(r2), -999.0, iota4b,
                                             ALU.mult, ALU.add),
         reads=[r2, C.iota8], writes=[r2])
    P.do(v, lambda e: e.tensor_reduce(rsv(3), v4(r2), axis=AX.X, op=ALU.min),
         reads=[r2], writes=[rs])
    P.do(v, lambda e: e.tensor_scalar_add(rsv(3), rsv(3), 999.0), reads=[rs], writes=[rs])
    P.do(v, lambda e: e.tensor_tensor(v4(r2), iota4b, rsb(3), ALU.is_equal),
         reads=[C.iota8, rs], writes=[r2])
    P.do(v, lambda e: e.tensor_tensor(rsv(3), rsv(0), rsv(2), ALU.add),
         reads=[rs], writes=[rs])
    P.do(v, lambda e: e.tensor_scalar_add(rsv(3), rsv(3), 1e-9), reads=[rs], writes=[rs])
    P.do(v, lambda e: e.reciprocal(rsv(3), rsv(3)), reads=[rs], writes=[rs])
    P.do(v, lambda e: e.tensor_tensor(rsv(0), rsv(0), rsv(3), ALU.mult),
         reads=[rs], writes=[rs])
    P.do(v, lambda e: e.tensor_tensor(rsv(2), rsv(2), rsv(3), ALU.mult),
         reads=[rs], writes=[rs])
    P.do(v, lambda e: e.tensor_tensor(v4(r1), v4(r1), rsb(0), ALU.mult),
         reads=[r1, rs], writes=[r1])
    P.do(v, lambda e: e.tensor_tensor(v4(r2), v4(r2), rsb(2), ALU.mult),
         reads=[r2, rs], writes=[r2])
    P.do(v, lambda e: e.tensor_tensor(r1[:], r1[:], r2[:], ALU.add),
         reads=[r1, r2], writes=[r1])
    P.do("act", lambda s: s.copy(E.mi[:], E.pb_r[:, 16:48]), reads=[E.pb_r], writes=[E.mi])
    mic = E.mi[:].rearrange("p (tc e) -> p tc e", e=8)
    P.do(v, lambda e: e.tensor_tensor(v4(r2), mic[:, :, 0:8:2], mic[:, :, 1:8:2],
                                      ALU.subtract),
         reads=[E.mi], writes=[r2])
    P.do("act", lambda s: s.activation(r2[:], r2[:], AF.Sigmoid), reads=[r2], writes=[r2])
    P.do(v, lambda e: e.tensor_scalar(r3[:], r2[:], -1.0, 1.0, ALU.mult, ALU.add),
         reads=[r2], writes=[r3])
    P.do(v, lambda e: e.tensor_tensor(r4[:], r2[:], r3[:], ALU.max),
         reads=[r2, r3], writes=[r4])
    P.do(v, lambda e: e.tensor_tensor(r3[:], r2[:], r3[:], ALU.is_ge),
         reads=[r2, r3], writes=[r3])
    P.do(v, lambda e: e.tensor_scalar_add(r2[:], r4[:], 1e-9), reads=[r4], writes=[r2])
    P.do(v, lambda e: e.reciprocal(r2[:], r2[:]), reads=[r2], writes=[r2])
    P.do(v, lambda e: e.tensor_tensor(r4[:], r4[:], r2[:], ALU.mult),
         reads=[r4, r2], writes=[r4])
    P.do(v, lambda e: e.tensor_tensor(r4[:], r1[:], r4[:], ALU.mult),
         reads=[r1, r4], writes=[r4])
    P.do(v, lambda e: e.tensor_scalar(r2[:], r3[:], -1.0, 1.0, ALU.mult, ALU.add),
         reads=[r3], writes=[r2])
    P.do(v, lambda e: e.tensor_tensor(v8(cw)[:, :, 0::2], v4(r4), v4(r3), ALU.mult),
         reads=[r4, r3], writes=[cw])
    P.do(v, lambda e: e.tensor_tensor(v8(cw)[:, :, 1::2], v4(r4), v4(r2), ALU.mult),
         reads=[r4, r2], writes=[cw])
    for tc in range(4):
        P.do("pe", lambda t, tc=tc: t.transpose(E.pb_t[0:8, 0:128], cw[:, tc * 8:tc * 8 + 8],
                                                C.ident[:]),
             reads=[cw, C.ident], writes=[E.pb_t])
        P.do(v, lambda e, tc=tc: e.tensor_copy(cwT[0:8, tc * 128:(tc + 1) * 128],
                                               E.pb_t[0:8, 0:128]),
             reads=[E.pb_t], writes=[cwT])


def emit_phase_l1ffn_loop(P, C, E, loop, x_in, zT, znT, x_out, w1bf, w2bf,
                          tiles=range(NTILES)):
    tiles = list(tiles)

    def preamble_dma(t, s):
        emit_dma_tile(P, "pool_dma", E.z[s], zT, t)
        P.do("sp_dma", lambda e, t=t, s=s: e.dma_start(
            out=E.zn[s][:].rearrange("p (c n) -> p c n", n=NT),
            in_=znT.rearrange("(c p) n -> p c n", p=128)[:, :, t * NT:(t + 1) * NT]),
            reads=[(znT.tensor, t)], writes=[(E.zn[s], c) for c in range(DC)])

    def preamble_route(t, s):
        zn, cwT = E.zn[s], E.cwT[s]
        for tc in range(4):
            P.do("pe", emit_linear(
                P, lambda tc=tc: E.pb_r[:, tc * 4:(tc + 1) * 4],
                lambda k, tc=tc: zn[:, k * NT + tc * 128:k * NT + tc * 128 + 128],
                lambda k: E.wmacT[:, k * 4:(k + 1) * 4], DC, None),
                reads=[(zn, c2) for c2 in range(DC)] + [E.wmacT], writes=[E.pb_r])
            P.do("pe", emit_linear(
                P, lambda tc=tc: E.pb_r[:, 16 + tc * 8:16 + (tc + 1) * 8],
                lambda k, tc=tc: zn[:, k * NT + tc * 128:k * NT + tc * 128 + 128],
                lambda k: E.wmicT[:, k * 8:(k + 1) * 8], DC, None),
                reads=[(zn, c2) for c2 in range(DC)] + [E.wmicT], writes=[E.pb_r])
        emit_routing_l1(P, C, E, cwT)

    def experts_pair(pair):
        w1 = E.w1sb
        for e in range(8):
            for wh in range(2):
                P.do("sp_dma", lambda en, e=e, wh=wh: en.dma_start(
                    out=w1[:, wh * 8192:(wh + 1) * 8192],
                    in_=w1bf[e][:, wh * 8192:(wh + 1) * 8192]),
                    reads=[(w1bf.tensor, e)], writes=[(w1, wh)])
            P.do("act_dma", lambda en, e=e: en.dma_start(out=E.w2sb[:], in_=w2bf[e]),
                 reads=[(w2bf.tensor, e)], writes=[E.w2sb])
            for pi, (t, s) in enumerate(pair):
                zn, cwT, y = E.zn[s], E.cwT[s], E.y[s]
                first = (e == 0)

                def bcast(e=e, cwT=cwT, pi=pi):
                    bc = E.pb_bc if (e + pi) % 2 == 0 else E.pb_bc2
                    P.do("pe", lambda t_: t_.matmul(
                        bc[:], C.sel8[:, e * 128:(e + 1) * 128],
                        cwT[:], start=True, stop=True),
                        reads=[cwT, C.sel8], writes=[bc])
                    P.do("act", lambda s_: s_.copy(E.cwB[:], bc[:]),
                         reads=[bc], writes=[E.cwB])

                def scale(fc):
                    P.do("dve", lambda v, fc=fc: v.tensor_tensor(
                        E.hs[:, fc * NT:(fc + 1) * NT], E.hs[:, fc * NT:(fc + 1) * NT],
                        E.cwB[:], ALU.mult),
                        reads=[(E.hs, fc), E.cwB], writes=[(E.hs, fc)])
                if not first:
                    bcast()
                for fc in range(16):
                    pb = E.pb_h[fc % 2]

                    def hmm(k0, k1, fc=fc, pb=pb, zn=zn):
                        def f(t_):
                            last = None
                            for k in range(k0, k1):
                                last = t_.matmul(
                                    pb[:], w1[:, k * 2048 + fc * 128:k * 2048 + fc * 128 + 128],
                                    zn[:, k * NT:(k + 1) * NT],
                                    start=(k == 0), stop=(k == DC - 1))
                            return last
                        return f
                    P.do("pe", hmm(0, 4),
                         reads=[(zn, c2) for c2 in range(4)] + [(w1, 0)], writes=[pb])
                    P.do("pe", hmm(4, 8),
                         reads=[(zn, c2) for c2 in range(4, 8)] + [(w1, 1)], writes=[pb])
                    if first and pi == 0 and fc == 0:
                        for t2, s2 in pair:
                            preamble_route(t2, s2)
                    if first and fc == 8:
                        bcast()
                    P.do("act", lambda s_, fc=fc, pb=pb: s_.activation(
                        E.hs[:, fc * NT:(fc + 1) * NT], pb[:], AF.Silu),
                        reads=[pb], writes=[(E.hs, fc)])
                    if not first or fc >= 8:
                        scale(fc)
                        if first and fc >= 8:
                            scale(fc - 8)
                if first:
                    pass
                for co in range(8):
                    pb = E.pb_y[co % 2]

                    def ymm(k0, k1, co=co, pb=pb):
                        def f(t_):
                            last = None
                            for k in range(k0, k1):
                                last = t_.matmul(
                                    pb[:],
                                    E.w2sb[:, k * 1024 + co * 128:k * 1024 + co * 128 + 128],
                                    E.hs[:, k * NT:(k + 1) * NT],
                                    start=(k == 0), stop=(k == 15))
                            return last
                        return f
                    P.do("pe", ymm(0, 14),
                         reads=[(E.hs, k) for k in range(14)] + [E.w2sb], writes=[pb])
                    P.do("pe", ymm(14, 16),
                         reads=[(E.hs, 14), (E.hs, 15), E.w2sb], writes=[pb])
                    if e == 0:
                        P.do("act", lambda s_, co=co, pb=pb, y=y: s_.copy(
                            y[:, co * NT:(co + 1) * NT], pb[:]),
                            reads=[pb], writes=[(y, co)])
                    else:
                        P.do("dve", lambda v, co=co, pb=pb, y=y: v.tensor_tensor(
                            y[:, co * NT:(co + 1) * NT], y[:, co * NT:(co + 1) * NT],
                            pb[:], ALU.add),
                            reads=[(y, co), pb], writes=[(y, co)])

    def finals(t, s):
        z = E.z[s]
        y = E.y[s]
        if loop == 0:
            zn = E.zn[s]
            zbk = y
            nt = NT
            rw = E.rows
            v2 = rw[0:1, (3 * s + 1) * nt:(3 * s + 1) * nt + nt]
            inv = rw[0:1, (3 * s + 2) * nt:(3 * s + 2) * nt + nt]
            P.do("dve", lambda v, y=y: v.tensor_tensor(z[:], z[:], y[:], ALU.add),
                 reads=[z] + [(z, c) for c in range(DC)] + [(y, co) for co in range(8)],
                 writes=[z] + [(z, c) for c in range(DC)])
            emit_dma_tile(P, "sp_dma", z, zT, t, write=True)
            P.do("dve", lambda v, zbk=zbk: v.tensor_tensor(zbk[:], z[:], z[:], ALU.mult),
                 reads=[z], writes=[(zbk, co) for co in range(8)])
            P.do("pe", emit_linear(P, lambda: E.pb_r[64:65, 0:nt], lambda k: C.onesc[:],
                                   lambda k, zbk=zbk: zbk[:, k * nt:(k + 1) * nt], DC, None),
                 reads=[(zbk, co) for co in range(8)] + [C.onesc], writes=[E.pb_r])
            P.do("dve", lambda v: v.tensor_scalar(v2, E.pb_r[64:65, 0:nt],
                                                  1.0 / D, 1e-6, ALU.mult, ALU.add),
                 reads=[E.pb_r], writes=[rw])
            P.do("act", lambda s_: s_.activation(inv, v2, AF.Ln), reads=[rw], writes=[rw])
            P.do("act", lambda s_: s_.activation(inv, inv, AF.Exp, scale=-0.5),
                 reads=[rw], writes=[rw])
            P.do("pe", lambda t_: t_.matmul(E.pb_bc[:, 0:nt], C.onesr[:], inv,
                                            start=True, stop=True),
                 reads=[rw, C.onesr], writes=[E.pb_bc])
            P.do("act", lambda s_: s_.copy(E.aB[:], E.pb_bc[:, 0:nt]),
                 reads=[E.pb_bc], writes=[E.aB])
            for c in range(DC):
                sl = slice(c * NT, (c + 1) * NT)
                P.do("dve", lambda v, sl=sl, zn=zn: v.tensor_tensor(
                    zn[:, sl], z[:, sl], E.aB[:], ALU.mult),
                    reads=[z, E.aB], writes=[(zn, c)])
                P.do("act", lambda s_, c=c, sl=sl, zn=zn: s_.activation(
                    zn[:, sl], zn[:, sl], AF.Identity, scale=E.nw[:, c:c + 1]),
                    reads=[(zn, c), E.nw], writes=[(zn, c)])
            P.do("sp_dma", lambda e, t=t, zn=zn: e.dma_start(
                out=znT.rearrange("(c p) n -> p c n", p=128)[:, :, t * NT:(t + 1) * NT],
                in_=zn[:].rearrange("p (c n) -> p c n", n=NT)),
                reads=[(zn, c) for c in range(DC)], writes=[(znT.tensor, t)])
        else:
            emit_dma_tile(P, "pool_dma", E.xr, x_in, t)
            P.do("dve", lambda v, y=y: v.tensor_tensor(z[:], z[:], y[:], ALU.add),
                 reads=[z] + [(z, c) for c in range(DC)] + [(y, co) for co in range(8)],
                 writes=[z] + [(z, c) for c in range(DC)])
            P.do("dve", lambda v: v.tensor_tensor(z[:], z[:], E.xr[:], ALU.add),
                 reads=[z, E.xr], writes=[z] + [(z, c) for c in range(DC)])
            emit_dma_tile(P, "sp_dma", z, x_out, t, write=True)

    assert len(tiles) % 2 == 0
    for i in range(0, len(tiles), 2):
        pair = [(tiles[i], 0), (tiles[i + 1], 1)]
        for t, s in pair:
            preamble_dma(t, s)
        experts_pair(pair)
        for t, s in pair:
            finals(t, s)

FUSED = True


def host_prep(inputs):
    g = {}
    ac = np.ascontiguousarray
    g["WqT0"], g["WkT0"], g["WvT0"], g["WoT0"] = (
        inputs["L0_attn_Wq"], inputs["L0_attn_Wk"],
        inputs["L0_attn_Wv"], inputs["L0_attn_Wo"])
    ab = np.zeros((128, 32), np.float32)
    for i, n in enumerate(("bq", "bk", "bv", "bo")):
        ab[:, i * 8:(i + 1) * 8] = inputs[f"L0_attn_{n}"].reshape(8, 128).T
    g["abias"] = ab
    g["gateT"] = ac(inputs["L0_ffn_gate"].T)
    g["corr4"] = np.tile(inputs["L0_ffn_corr_bias"], (128, 4)).astype(np.float32)
    g["iota32"] = np.tile(np.arange(8, dtype=np.float32), (128, 4))
    g["fc1T"] = ac(inputs["L0_ffn_fc1_w"].T)
    g["fc2T"] = ac(inputs["L0_ffn_fc2_w"].T)
    g["w1Tl0"] = ac(inputs["L0_ffn_w1"].transpose(0, 2, 1))
    g["w2Tl0"] = ac(inputs["L0_ffn_w2"].transpose(0, 2, 1))
    g["upT"] = ac(inputs["L0_ffn_up_w"].T)
    g["downT"] = ac(inputs["L0_ffn_down_w"].T)
    g["WqT1"] = ac(inputs["L1_attn_Wq"].T)
    g["WkT1"] = ac(inputs["L1_attn_Wk"].T)
    g["WvT1"] = ac(inputs["L1_attn_Wv"].T)
    g["WoT1"] = ac(inputs["L1_attn_Wo"].T)
    g["wmacT"] = ac(inputs["L1_ffn_w_macro"].T)
    g["wmicT"] = ac(inputs["L1_ffn_w_micro"].reshape(8, 1024).T)
    g["nw"] = ac(inputs["L1_ffn_norm_w"].reshape(8, 128).T)
    g["w1Tl1"] = ac(inputs["L1_ffn_w1"].transpose(0, 2, 1))
    g["w2Tl1"] = ac(inputs["L1_ffn_w2"].transpose(0, 2, 1))
    return g


_W_SPECS = {
    "WqT0": [1024, 1024], "WkT0": [1024, 1024], "WvT0": [1024, 1024],
    "WoT0": [1024, 1024], "abias": [128, 32],
    "gateT": [1024, 8], "corr4": [128, 32], "iota32": [128, 32],
    "fc1T": [1024, 512], "fc2T": [512, 1024],
    "w1Tl0": [8, 512, 512], "w2Tl0": [8, 512, 512],
    "upT": [1024, 4096], "downT": [4096, 1024],
    "WqT1": [1024, 1024], "WkT1": [1024, 1024], "WvT1": [1024, 1024],
    "WoT1": [1024, 1024],
    "wmacT": [1024, 4], "wmicT": [1024, 8], "nw": [128, 8],
    "w1Tl1": [8, 1024, 2048], "w2Tl1": [8, 2048, 1024],
}


class _Scoped:

    def __init__(self, P):
        self.P = P

    def __enter__(self):
        import contextlib
        self.outer = self.P.es
        self.inner = contextlib.ExitStack()
        self.inner.__enter__()
        self.P.es = self.inner
        return self

    def __exit__(self, *a):
        self.P.es = self.outer
        return self.inner.__exit__(*a)


def _decl_inputs(nc, names):
    hb = {}
    for n in names:
        hb[n] = nc.dram_tensor(n, _W_SPECS[n], F32, kind="ExternalInput")
    return hb


def _emit_attn_consts(P, nc, B, hb, rope):
    h_masks = nc.inline_tensor(attn_masks_np(), name=_uname("h_masks"))
    P.do("sp_dma", lambda e: e.dma_start(out=B.masks[:], in_=h_masks.ap()),
         writes=[B.masks])
    if rope:
        cosn, sinn, permn = rope_tables_np()
        hc = nc.inline_tensor(cosn, name="h_cos")
        hs = nc.inline_tensor(sinn, name="h_sin")
        hp = nc.inline_tensor(permn, name="h_perm")
        P.do("sp_dma", lambda e: e.dma_start(out=B.cosT[:], in_=hc.ap()), writes=[B.cosT])
        P.do("sp_dma", lambda e: e.dma_start(out=B.sinA[:], in_=hs.ap()), writes=[B.sinA])
        P.do("sp_dma", lambda e: e.dma_start(out=B.perm[:], in_=hp.ap()), writes=[B.perm])
    else:
        P.do("sp_dma", lambda e: e.dma_start(out=B.bias[:], in_=hb["abias"].ap()),
             writes=[B.bias])


def build_full(n_launch_phases=None):
    import contextlib
    progs = []

    def new_nc():
        return bass.Bass("TRN2", target_bir_lowering=False, debug=False)

    if FUSED:
        nc = new_nc()
        hb = _decl_inputs(nc, list(_W_SPECS))
        xin = nc.dram_tensor("xT", [1024, NTOK], F32, kind="ExternalInput")
        yout = nc.dram_tensor("yT", [1024, NTOK], F32, kind="ExternalOutput")
        xr = [nc.dram_tensor(f"xr{i}", [1024, NTOK], F32, kind="Internal")
              for i in range(5)]
        xnsp = nc.dram_tensor("xnsp", [1024, NTOK], BF16, kind="Internal")
        zT = nc.dram_tensor("zT", [1024, NTOK], F32, kind="Internal")
        znT = nc.dram_tensor("znT", [1024, NTOK], BF16, kind="Internal")
        w1bf = nc.dram_tensor("w1bf", [8, 128, 16384], BF16, kind="Internal")
        w2bf = nc.dram_tensor("w2bf", [8, 128, 16384], BF16, kind="Internal")
        upbf = nc.dram_tensor("upbf", [2, 128, 16384], BF16, kind="Internal")
        dnbf = nc.dram_tensor("dnbf", [2, 128, 16384], BF16, kind="Internal")
        w4bf = nc.dram_tensor("w4bf", [4, 128, 8192], BF16, kind="Internal")
        es = contextlib.ExitStack()
        with es:
            P = Prog(nc, es)
            C = Consts(P)
            with _Scoped(P):
                B = AttnBufs(P, False)
                _emit_attn_consts(P, nc, B, hb, False)
                emit_phase_attn(P, C, B, xin.ap(), xr[0].ap(), hb["WqT0"].ap(),
                                hb["WkT0"].ap(), hb["WvT0"].ap(), hb["WoT0"].ap())
                P.barrier()
            P.recycle_dma_sems()
            with _Scoped(P):
                B = L0FFNBufs(P)
                cb1 = sb(P, "cvt1", [128, 16384], BF16)
                cb2 = cb1
                load_l0ffn_weights(P, B, {"gateT": hb["gateT"], "fc1T": hb["fc1T"],
                                          "w1T": hb["w1Tl0"], "w2T": hb["w2Tl0"],
                                          "fc2T": hb["fc2T"], "corr4": hb["corr4"],
                                          "iota32": hb["iota32"]})
                emit_weight_convert_l1(P, cb1, cb2, hb["w1Tl1"].ap(), hb["w2Tl1"].ap(),
                                       w1bf.ap(), w2bf.ap())
                for half in range(2):
                    emit_stage_bf16(P, cb1, hb["upT"].ap()[:, half * 2048:(half + 1) * 2048],
                                    upbf.ap()[half], (upbf, half))
                    emit_stage_bf16(P, cb1, hb["downT"].ap()[half * 2048:(half + 1) * 2048, :],
                                    dnbf.ap()[half], (dnbf, half))
                for wi, wn in enumerate(("WqT1", "WkT1", "WvT1", "WoT1")):
                    emit_stage_bf16(P, cb1, hb[wn].ap(),
                                    w4bf.ap()[wi], (w4bf, wi))
                emit_phase_l0ffn_routed(P, C, B, xr[0].ap(), xr[1].ap(), xnsp.ap())
                P.barrier()
            P.recycle_dma_sems()
            for half in (0, 1):
                with _Scoped(P):
                    B = SharedBufs(P)
                    load_wT(P, B.upT, upbf.ap()[half])
                    load_wT(P, B.downT, dnbf.ap()[half])
                    emit_phase_l0ffn_shared(P, C, B, xr[1 + half].ap(),
                                            xr[2 + half].ap(), xnsp.ap(), half)
                    P.barrier()
                P.recycle_dma_sems()
            with _Scoped(P):
                B = AttnBufs(P, True)
                _emit_attn_consts(P, nc, B, hb, True)
                nw_d = sb(P, "d_nw", [128, DC], F32)
                P.do("sp_dma", lambda e: e.dma_start(out=nw_d[:], in_=hb["nw"].ap()),
                     writes=[nw_d])
                emit_phase_attn(P, C, B, xr[3].ap(), xr[4].ap(), w4bf.ap()[0],
                                w4bf.ap()[1], w4bf.ap()[2], w4bf.ap()[3],
                                zprep=(zT.ap(), znT.ap(), nw_d))
                P.barrier()
            P.recycle_dma_sems()
            with _Scoped(P):
                E = L1FFNBufs(P)
                load_wT(P, E.wmacT, hb["wmacT"].ap())
                load_wT(P, E.wmicT, hb["wmicT"].ap())
                P.do("sp_dma", lambda e: e.dma_start(out=E.nw[:], in_=hb["nw"].ap()),
                     writes=[E.nw])
                emit_phase_l1ffn_loop(P, C, E, 0, xr[4].ap(), zT.ap(), znT.ap(), None,
                                      w1bf.ap(), w2bf.ap())
                emit_phase_l1ffn_loop(P, C, E, 1, xr[4].ap(), zT.ap(), znT.ap(), yout.ap(),
                                      w1bf.ap(), w2bf.ap())
            P.finish()
        ins = list(_W_SPECS) + ["xT"]
        progs.append((nc, ins, ["yT"]))
        return progs

    def mk_io(nc, xname="xT", xdt=F32):
        xin = nc.dram_tensor(xname, [1024, NTOK], xdt, kind="ExternalInput")
        xo = nc.dram_tensor("xo", [1024, NTOK], F32, kind="ExternalOutput")
        return xin, xo

    nc = new_nc()
    hb = _decl_inputs(nc, ["WqT0", "WkT0", "WvT0", "WoT0", "abias"])
    xin, xo = mk_io(nc)
    es = contextlib.ExitStack()
    with es:
        P = Prog(nc, es)
        C = Consts(P)
        B = AttnBufs(P, False)
        _emit_attn_consts(P, nc, B, hb, False)
        emit_phase_attn(P, C, B, xin.ap(), xo.ap(), hb["WqT0"].ap(), hb["WkT0"].ap(),
                        hb["WvT0"].ap(), hb["WoT0"].ap())
        P.finish()
    progs.append((nc, ["WqT0", "WkT0", "WvT0", "WoT0", "abias", "xT"], ["xo"]))

    nc = new_nc()
    hb = _decl_inputs(nc, ["gateT", "fc1T", "w1Tl0", "w2Tl0", "fc2T", "corr4", "iota32"])
    xin, xo = mk_io(nc)
    xnsp = nc.dram_tensor("xnsp", [1024, NTOK], BF16, kind="ExternalOutput")
    es = contextlib.ExitStack()
    with es:
        P = Prog(nc, es)
        C = Consts(P)
        B = L0FFNBufs(P)
        load_l0ffn_weights(P, B, {"gateT": hb["gateT"], "fc1T": hb["fc1T"],
                                  "w1T": hb["w1Tl0"], "w2T": hb["w2Tl0"],
                                  "fc2T": hb["fc2T"], "corr4": hb["corr4"],
                                  "iota32": hb["iota32"]})
        emit_phase_l0ffn_routed(P, C, B, xin.ap(), xo.ap(), xnsp.ap())
        P.finish()
    progs.append((nc, ["gateT", "fc1T", "w1Tl0", "w2Tl0", "fc2T", "corr4", "iota32",
                       "xT"], ["xo", "xnsp"]))

    for half in (0, 1):
        nc = new_nc()
        hb = _decl_inputs(nc, ["upT", "downT"])
        xin, xo = mk_io(nc)
        xnsp = nc.dram_tensor("xnsp", [1024, NTOK], BF16, kind="ExternalInput")
        es = contextlib.ExitStack()
        with es:
            P = Prog(nc, es)
            C = Consts(P)
            B = SharedBufs(P)
            load_shared_weights(P, B, hb, half)
            emit_phase_l0ffn_shared(P, C, B, xin.ap(), xo.ap(), xnsp.ap(), half)
            P.finish()
        progs.append((nc, ["upT", "downT", "xT", "xnsp"], ["xo"]))

    nc = new_nc()
    hb = _decl_inputs(nc, ["WqT1", "WkT1", "WvT1", "WoT1"])
    xin, xo = mk_io(nc)
    es = contextlib.ExitStack()
    with es:
        P = Prog(nc, es)
        C = Consts(P)
        B = AttnBufs(P, True)
        _emit_attn_consts(P, nc, B, hb, True)
        emit_phase_attn(P, C, B, xin.ap(), xo.ap(), hb["WqT1"].ap(), hb["WkT1"].ap(),
                        hb["WvT1"].ap(), hb["WoT1"].ap())
        P.finish()
    progs.append((nc, ["WqT1", "WkT1", "WvT1", "WoT1", "xT"], ["xo"]))

    nc = new_nc()
    hb = _decl_inputs(nc, ["wmacT", "wmicT", "nw", "w1Tl1", "w2Tl1"])
    xin, xo = mk_io(nc)
    zT = nc.dram_tensor("zT", [1024, NTOK], F32, kind="Internal")
    w1bf = nc.dram_tensor("w1bf", [8, 128, 16384], BF16, kind="Internal")
    w2bf = nc.dram_tensor("w2bf", [8, 128, 16384], BF16, kind="Internal")
    es = contextlib.ExitStack()
    with es:
        P = Prog(nc, es)
        C = Consts(P)
        E = L1FFNBufs(P)
        load_wT(P, E.wmacT, hb["wmacT"].ap())
        load_wT(P, E.wmicT, hb["wmicT"].ap())
        P.do("sp_dma", lambda e: e.dma_start(out=E.nw[:], in_=hb["nw"].ap()),
             writes=[E.nw])
        emit_weight_convert_l1(P, E.w1sb, E.w2sb, hb["w1Tl1"].ap(), hb["w2Tl1"].ap(),
                               w1bf.ap(), w2bf.ap())
        znT2 = nc.dram_tensor("znT", [1024, NTOK], BF16, kind="Internal")
        emit_phase_l1ffn_loop(P, C, E, 0, xin.ap(), zT.ap(), znT2.ap(), None,
                              w1bf.ap(), w2bf.ap())
        emit_phase_l1ffn_loop(P, C, E, 1, xin.ap(), zT.ap(), znT2.ap(), xo.ap(),
                              w1bf.ap(), w2bf.ap())
        P.finish()
    progs.append((nc, ["wmacT", "wmicT", "nw", "w1Tl1", "w2Tl1", "xT"], ["xo"]))
    return progs


_CACHED = {"progs": None, "fused": None}


def kernel(**inputs):
    g = host_prep(inputs)
    x = inputs["L0_attn_x"]
    xT = [np.ascontiguousarray(x[4 * c:4 * c + 4].transpose(2, 0, 1).reshape(1024, NTOK))
          for c in range(8)]
    if _CACHED["progs"] is None or _CACHED["fused"] != FUSED:
        _CACHED["progs"] = build_full()
        _CACHED["fused"] = FUSED
    progs = _CACHED["progs"]
    state = {c: {"xT": xT[c]} for c in range(8)}
    for nc, ins, outs in progs:
        in_maps = []
        for c in range(8):
            m = {}
            for n in ins:
                if n in g:
                    m[n] = g[n]
                else:
                    m[n] = state[c][n]
            in_maps.append(m)
        res = run_bass_kernel_spmd(nc, in_maps, core_ids=list(range(8)))
        for c in range(8):
            r = res.results[c]
            if "yT" in r:
                state[c]["xT"] = r["yT"]
            elif "xo" in r:
                state[c]["xT"] = r["xo"]
            if "xnsp" in r:
                state[c]["xnsp"] = r["xnsp"]
    out = np.empty((32, 2048, 1024), dtype=np.float32)
    for c in range(8):
        out[4 * c:4 * c + 4] = np.asarray(state[c]["xT"], dtype=np.float32).reshape(
            1024, 4, 2048).transpose(1, 2, 0)
    return out


# version 29
# speedup vs baseline: 1.1267x; 1.0203x over previous
import numpy as np
import ml_dtypes
import concourse.bass as bass
import concourse.mybir as mybir
from concourse.bass_utils import run_bass_kernel_spmd

F32 = mybir.dt.float32
BF16 = mybir.dt.bfloat16
AF = mybir.ActivationFunctionType
ALU = mybir.AluOpType
AX = mybir.AxisListType
BF = ml_dtypes.bfloat16

D = 1024
S = 2048
NSEQ = 4
NTOK = NSEQ * S
NT = 512
NTILES = NTOK // NT
DC = D // 128
H = 16
HD = 64


class Prog:

    def __init__(self, nc, es, n_dma_sems=48):
        self.nc = nc
        self.es = es
        self.q_eng = {
            "pe": nc.tensor, "act": nc.scalar, "dve": nc.vector,
            "pool": nc.gpsimd, "sp": nc.sync,
            "sp_dma": nc.sync, "act_dma": nc.scalar, "pool_dma": nc.gpsimd,
        }
        self.qsem = {q: es.enter_context(nc.semaphore(f"q_{q}"))
                     for q in ("pe", "act", "dve", "pool", "sp")}
        self.cnt = {q: 0 for q in self.qsem}
        self.dma_pool = [es.enter_context(nc.semaphore(f"dsem{i}"))
                         for i in range(n_dma_sems)]
        self.dma_free = list(self.dma_pool)
        self.semval = {id(s): 0 for s in self.dma_pool}
        self.buf_sem = {}
        self.state = {}
        self.waited = {}

    def _key(self, buf):
        if isinstance(buf, tuple):
            return (id(buf[0]), buf[1])
        return id(buf)

    def _tid(self, buf):
        return id(buf[0]) if isinstance(buf, tuple) else id(buf)

    def _bufstate(self, buf):
        k = self._key(buf)
        if k not in self.state:
            self.state[k] = {"w": None, "r": {}}
        return self.state[k]

    def _dma_sem(self, buf, q):
        t = (self._tid(buf), "sw" if q == "pool_dma" else "hw")
        if t not in self.buf_sem:
            if not self.dma_free:
                raise RuntimeError("out of DMA semaphores; call recycle_dma_sems()")
            self.buf_sem[t] = self.dma_free.pop()
        return self.buf_sem[t]

    def _wait(self, q, sem, val, dep_q):
        if val <= 0:
            return
        if dep_q is None:
            val = max(val, self.semval[id(sem)])
        if self.waited.get((q, id(sem)), 0) >= val:
            return
        self.q_eng[q].wait_ge(sem, val)
        self.waited[(q, id(sem))] = val

    SAME_Q_SKIP = {"pe", "act"}

    def _dep_wait(self, q, dep):
        sem, val, dep_q = dep
        if dep_q == q and q in self.SAME_Q_SKIP:
            return
        self._wait(q, sem, val, dep_q if not (dep_q and dep_q.endswith("_dma")) else None)

    def do(self, q, fn, reads=(), writes=()):
        for b in reads:
            st = self._bufstate(b)
            if st["w"] is not None:
                self._dep_wait(q, st["w"])
        for b in writes:
            st = self._bufstate(b)
            if st["w"] is not None:
                self._dep_wait(q, st["w"])
            for dep in st["r"].values():
                self._dep_wait(q, dep)
        inst = fn(self.q_eng[q])
        if q.endswith("_dma"):
            sem = self._dma_sem(writes[0], q)
            self.semval[id(sem)] += 16
            val = self.semval[id(sem)]
            rec = (sem, val, q)
        else:
            self.cnt[q] += 1
            val = self.cnt[q]
            sem = self.qsem[q]
            rec = (sem, val, q)
        inst.then_inc(sem, 16 if q.endswith("_dma") else 1)
        for b in reads:
            st = self._bufstate(b)
            st["r"][id(sem)] = rec
        for b in writes:
            st = self._bufstate(b)
            st["w"] = rec
            st["r"] = {}
        return inst

    def barrier(self):
        for q in self.qsem:
            for q2, c in self.cnt.items():
                if q2 != q:
                    self._wait(q, self.qsem[q2], c, q2)
            for s in self.dma_pool:
                self._wait(q, s, self.semval[id(s)], None)

    def recycle_dma_sems(self):
        self.buf_sem = {}
        self.dma_free = list(self.dma_pool)

    def finish(self, q="pe"):
        for q2, c in self.cnt.items():
            if q2 != q:
                self._wait(q, self.qsem[q2], c, q2)
        for s in self.dma_pool:
            self._wait(q, s, self.semval[id(s)], None)


_NAME_SEQ = [0]


def _uname(name):
    _NAME_SEQ[0] += 1
    return f"{name}_{_NAME_SEQ[0]}"


def sb(P, name, shape, dtype):
    return P.es.enter_context(P.nc.sbuf_tensor(_uname(name), shape, dtype))


def ps(P, name, shape=(128, 512), dtype=F32):
    return P.es.enter_context(P.nc.psum_tensor(_uname(name), list(shape), dtype))


def emit_linear(P, out_psum_fn, lhsT_fn, rhs_fn, n_k, reads, tag=""):
    def fn(t):
        last = None
        for k in range(n_k):
            last = t.matmul(out_psum_fn(), lhsT_fn(k), rhs_fn(k),
                            start=(k == 0), stop=(k == n_k - 1))
        return last
    return fn


class Consts:

    def __init__(self, P):
        nc = P.nc
        self.onesc = sb(P, "c_onesc", [128, 1], BF16)
        self.onesr = sb(P, "c_onesr", [1, 128], F32)
        self.iota8 = sb(P, "c_iota8", [128, 8], F32)
        self.ident = sb(P, "c_ident", [128, 128], F32)
        h_onesc = nc.inline_tensor(np.ones((128, 1), dtype=BF), name=_uname("h_onesc"))
        h_onesr = nc.inline_tensor(np.ones((1, 128), dtype=np.float32), name=_uname("h_onesr"))
        self.onesrb = sb(P, "c_onesrb", [1, 128], BF16)
        h_onesrb = nc.inline_tensor(np.ones((1, 128), dtype=BF), name=_uname("h_onesrb"))
        h_iota8 = nc.inline_tensor(
            np.tile(np.arange(8, dtype=np.float32), (128, 1)).copy(), name=_uname("h_iota8"))
        h_ident = nc.inline_tensor(np.eye(128, dtype=np.float32), name=_uname("h_ident"))
        self.sel8 = sb(P, "c_sel8", [8, 8 * 128], BF16)
        sel_np = np.zeros((8, 8 * 128), dtype=BF)
        for e in range(8):
            sel_np[e, e * 128:(e + 1) * 128] = 1.0
        h_sel8 = nc.inline_tensor(sel_np, name=_uname("h_sel8"))
        for dst, src in ((self.onesc, h_onesc), (self.onesr, h_onesr),
                         (self.onesrb, h_onesrb),
                         (self.iota8, h_iota8), (self.ident, h_ident),
                         (self.sel8, h_sel8)):
            P.do("sp_dma", lambda e, d=dst, s=src: e.dma_start(out=d[:], in_=s.ap()),
                 writes=[d for d in (dst,)])
        P._ln_rows = sb(P, "ln_rows", [1, 3 * NT], F32)


def emit_stats_ln(P, C, xb, sq, stat_ps, brd_ps, aB, bB, eps=1e-5, n_chunks=DC,
                  rms_only=False, denom=None):
    nt = NT
    dn = denom if denom is not None else n_chunks * 128
    P.do("dve", lambda v: v.tensor_tensor(sq[:], xb[:], xb[:], ALU.mult),
         reads=[xb], writes=[sq])
    if not rms_only:
        P.do("pe", emit_linear(P, lambda: stat_ps[0:1, 0:nt],
                               lambda k: C.onesc[:],
                               lambda k: xb[:, k * nt:(k + 1) * nt], n_chunks, None),
             reads=[xb, C.onesc], writes=[stat_ps])
    P.do("pe", emit_linear(P, lambda: stat_ps[64:65, 0:nt],
                           lambda k: C.onesc[:],
                           lambda k: sq[:, k * nt:(k + 1) * nt], n_chunks, None),
         reads=[sq, C.onesc], writes=[stat_ps])
    row = getattr(P, "_ln_rows", None)
    if row is None:
        row = sb(P, "ln_rows", [1, 3 * NT], F32)
        P._ln_rows = row
    m = row[0:1, 0:nt]
    v2 = row[0:1, nt:2 * nt]
    inv = row[0:1, 2 * nt:3 * nt]
    if rms_only:
        P.do("dve", lambda v: v.tensor_scalar(v2, stat_ps[64:65, 0:nt],
                                              1.0 / dn, eps, ALU.mult, ALU.add),
             reads=[stat_ps], writes=[row])
        P.do("act", lambda s: s.activation(inv, v2, AF.Ln), reads=[row], writes=[row])
        P.do("act", lambda s: s.activation(inv, inv, AF.Exp, scale=-0.5),
             reads=[row], writes=[row])
    else:
        P.do("dve", lambda v: v.tensor_scalar_mul(m, stat_ps[0:1, 0:nt], 1.0 / dn),
             reads=[stat_ps], writes=[row])
        P.do("dve", lambda v: v.tensor_scalar_mul(v2, stat_ps[64:65, 0:nt], 1.0 / dn),
             reads=[stat_ps], writes=[row])
        P.do("dve", lambda v: v.tensor_mul(inv, m, m), reads=[row], writes=[row])
        P.do("dve", lambda v: v.tensor_sub(v2, v2, inv), reads=[row], writes=[row])
        P.do("dve", lambda v: v.tensor_scalar_add(v2, v2, eps), reads=[row], writes=[row])
        P.do("act", lambda s: s.activation(inv, v2, AF.Ln), reads=[row], writes=[row])
        P.do("act", lambda s: s.activation(inv, inv, AF.Exp, scale=-0.5),
             reads=[row], writes=[row])
        P.do("dve", lambda v: v.tensor_mul(m, m, inv), reads=[row], writes=[row])
        P.do("dve", lambda v: v.tensor_scalar_mul(m, m, -1.0), reads=[row], writes=[row])
    P.do("pe", lambda t: t.matmul(brd_ps[:, 0:nt], C.onesr[:], inv,
                                  start=True, stop=True),
         reads=[row, C.onesr], writes=[brd_ps])
    P.do("act", lambda s: s.copy(aB[:], brd_ps[:, 0:nt]), reads=[brd_ps], writes=[aB])
    if not rms_only:
        P.do("pe", lambda t: t.matmul(brd_ps[:, 0:nt], C.onesr[:], m,
                                      start=True, stop=True),
             reads=[row, C.onesr], writes=[brd_ps])
        P.do("act", lambda s: s.copy(bB[:], brd_ps[:, 0:nt]), reads=[brd_ps], writes=[bB])


def emit_ln_apply(P, xb, aB, bB, xn, n_chunks=DC):
    for c in range(n_chunks):
        sl = slice(c * NT, (c + 1) * NT)
        P.do("dve", lambda v, sl=sl: v.tensor_tensor(xn[:, sl], xb[:, sl], aB[:], ALU.mult),
             reads=[xb, aB], writes=[xn])
        if bB is not None:
            P.do("dve", lambda v, sl=sl: v.tensor_tensor(xn[:, sl], xn[:, sl], bB[:], ALU.add),
                 reads=[xn, bB], writes=[xn])


def load_wT(P, dst_sb, hbm_ap, q="pool_dma", col0=0):
    r, c = hbm_ap.shape
    if r == 128:
        P.do("sp_dma", lambda e: e.dma_start(out=dst_sb[:, col0:col0 + c], in_=hbm_ap),
             writes=[dst_sb])
        return
    n = r // 128
    src = hbm_ap.rearrange("(n p) c -> p n c", p=128)
    dst = dst_sb[:, col0:col0 + n * c].rearrange("p (n c) -> p n c", c=c)
    P.do(q, lambda e: e.dma_start(out=dst, in_=src), writes=[dst_sb])


def emit_dma_tile(P, q, sbuf, hbm, t, write=False, subkeys=None):
    src = hbm.rearrange("(c p) n -> p c n", p=128)[:, :, t * NT:(t + 1) * NT]
    keys = [sbuf] if subkeys is None else [(sbuf, k) for k in subkeys]
    if write:
        P.do(q, lambda e: e.dma_start(out=src, in_=sbuf[:].rearrange("p (c n) -> p c n", n=NT)),
             reads=keys, writes=[(hbm.tensor, t)])
    else:
        P.do(q, lambda e: e.dma_start(out=sbuf[:].rearrange("p (c n) -> p c n", n=NT), in_=src),
             reads=[(hbm.tensor, t)], writes=keys)


class L0FFNBufs:
    def __init__(self, P):
        self.gateT = sb(P, "b_gateT", [128, DC * 8], BF16)
        self.corr = sb(P, "b_corr", [128, 32], F32)
        self.iota32 = sb(P, "b_iota32", [128, 32], F32)
        self.fc1T = sb(P, "b_fc1T", [128, DC * 512], BF16)
        self.w1T = sb(P, "b_w1T", [128, 8 * 2048], BF16)
        self.w2T = sb(P, "b_w2T", [128, 8 * 2048], BF16)
        self.fc2T = sb(P, "b_fc2T", [128, 4 * 1024], BF16)
        self.xf = [sb(P, f"b_xf{i}", [128, DC * NT], F32) for i in range(2)]
        self.xb = sb(P, "b_xb", [128, DC * NT], BF16)
        self.aB = sb(P, "b_aB", [128, NT], BF16)
        self.bB = sb(P, "b_bB", [128, NT], BF16)
        self.xn = sb(P, "b_xn", [128, DC * NT], BF16)
        self.s_sc = sb(P, "b_ssc", [128, 32], F32)
        self.r1 = sb(P, "b_r1", [128, 32], F32)
        self.r2 = sb(P, "b_r2", [128, 32], F32)
        self.r3 = sb(P, "b_r3", [128, 32], F32)
        self.rs = sb(P, "b_rs", [128, 6 * 4], F32)
        self.cw = sb(P, "b_cw", [128, 32], F32)
        self.cwT = sb(P, "b_cwT", [8, NT], BF16)
        self.cwB = sb(P, "b_cwB", [128, NT], BF16)
        self.latT = sb(P, "b_latT", [128, 4 * NT], BF16)
        self.hr2 = sb(P, "b_hr2", [128, 2 * NT], BF16)
        self.hs = sb(P, "b_hs", [128, 4 * NT], BF16)
        self.combT = sb(P, "b_combT", [128, 4 * NT], BF16)
        self.xo = sb(P, "b_xo", [128, DC * NT], F32)
        self.pb_comb = ps(P, "pb_comb", (128, 4 * 512))
        self.pb_h = ps(P, "pb_h01", (128, 2 * 512))
        self.pb_bc = ps(P, "pb_bc")
        self.pb_s = ps(P, "pb_s")


def load_l0ffn_weights(P, B, hb):
    load_wT(P, B.gateT, hb["gateT"].ap())
    load_wT(P, B.fc1T, hb["fc1T"].ap())
    for e in range(8):
        load_wT(P, B.w1T, hb["w1T"].ap()[e], col0=e * 2048)
        load_wT(P, B.w2T, hb["w2T"].ap()[e], col0=e * 2048)
    load_wT(P, B.fc2T, hb["fc2T"].ap())
    P.do("sp_dma", lambda e: e.dma_start(out=B.corr[:], in_=hb["corr4"].ap()),
         writes=[B.corr])
    P.do("sp_dma", lambda e: e.dma_start(out=B.iota32[:], in_=hb["iota32"].ap()),
         writes=[B.iota32])


def emit_routing_l0(P, C, B):
    v = "dve"
    sc, r1, r2, r3, cw = B.s_sc, B.r1, B.r2, B.r3, B.cw
    rs = B.rs
    def grp(buf, g):
        return buf[:].rearrange("p (tc e) -> p tc e", e=8)[:, :, g * 4:(g + 1) * 4]
    def full(buf):
        return buf[:].rearrange("p (tc e) -> p tc e", e=8)
    def col(buf, j):
        return buf[:, j:32:8]
    def rsv(i):
        return rs[:, i * 4:(i + 1) * 4]
    def rsb(i, n=8):
        return rs[:, i * 4:(i + 1) * 4].unsqueeze(2).broadcast_to([128, 4, n])

    P.do("act", lambda s: s.activation(sc[:], B.pb_s[:, 0:32], AF.Sigmoid),
         reads=[B.pb_s], writes=[sc])
    P.do(v, lambda e: e.tensor_tensor(r1[:], sc[:], B.corr[:], ALU.add),
         reads=[sc, B.corr], writes=[r1])
    for g in range(2):
        first = True
        for i in range(4):
            for j in range(i + 1, 4):
                pair = rsv(5)
                P.do(v, lambda e, i=i, j=j, g=g, pair=pair: e.tensor_tensor(
                    pair, col(r1, g * 4 + i), col(r1, g * 4 + j), ALU.add),
                    reads=[r1], writes=[rs])
                if first:
                    P.do(v, lambda e, g=g, pair=pair: e.tensor_copy(rsv(g), pair),
                         reads=[rs], writes=[rs])
                    first = False
                else:
                    P.do(v, lambda e, g=g, pair=pair: e.tensor_tensor(
                        rsv(g), rsv(g), pair, ALU.max), reads=[rs], writes=[rs])
    P.do(v, lambda e: e.tensor_tensor(rsv(2), rsv(0), rsv(1), ALU.is_ge),
         reads=[rs], writes=[rs])
    P.do(v, lambda e: e.tensor_tensor(grp(r2, 0), grp(r1, 0), rsb(2, 4), ALU.mult),
         reads=[r1, rs], writes=[r2])
    P.do(v, lambda e: e.tensor_scalar(rsv(3), rsv(2), -1.0, 1.0, ALU.mult, ALU.add),
         reads=[rs], writes=[rs])
    P.do(v, lambda e: e.tensor_tensor(grp(r2, 1), grp(r1, 1), rsb(3, 4), ALU.mult),
         reads=[r1, rs], writes=[r2])
    P.do(v, lambda e: e.tensor_reduce(rsv(0), full(r2), axis=AX.X, op=ALU.max),
         reads=[r2], writes=[rs])
    P.do(v, lambda e: e.tensor_tensor(full(r3), full(r2), rsb(0), ALU.is_ge),
         reads=[r2, rs], writes=[r3])
    P.do(v, lambda e: e.scalar_tensor_tensor(r3[:], r3[:], -999.0, B.iota32[:],
                                             ALU.mult, ALU.add),
         reads=[r3, B.iota32], writes=[r3])
    P.do(v, lambda e: e.tensor_reduce(rsv(1), full(r3), axis=AX.X, op=ALU.min),
         reads=[r3], writes=[rs])
    P.do(v, lambda e: e.tensor_scalar_add(rsv(1), rsv(1), 999.0), reads=[rs], writes=[rs])
    P.do(v, lambda e: e.tensor_tensor(full(r3), full(B.iota32), rsb(1), ALU.is_equal),
         reads=[B.iota32, rs], writes=[r3])
    P.do(v, lambda e: e.scalar_tensor_tensor(cw[:], r3[:], -1e30, r2[:], ALU.mult, ALU.add),
         reads=[r3, r2], writes=[cw])
    P.do(v, lambda e: e.tensor_reduce(rsv(2), full(cw), axis=AX.X, op=ALU.max),
         reads=[cw], writes=[rs])
    P.do(v, lambda e: e.tensor_tensor(full(r2), full(cw), rsb(2), ALU.is_ge),
         reads=[cw, rs], writes=[r2])
    P.do(v, lambda e: e.scalar_tensor_tensor(r2[:], r2[:], -999.0, B.iota32[:],
                                             ALU.mult, ALU.add),
         reads=[r2, B.iota32], writes=[r2])
    P.do(v, lambda e: e.tensor_reduce(rsv(3), full(r2), axis=AX.X, op=ALU.min),
         reads=[r2], writes=[rs])
    P.do(v, lambda e: e.tensor_scalar_add(rsv(3), rsv(3), 999.0), reads=[rs], writes=[rs])
    P.do(v, lambda e: e.tensor_tensor(full(r2), full(B.iota32), rsb(3), ALU.is_equal),
         reads=[B.iota32, rs], writes=[r2])
    P.do(v, lambda e: e.tensor_tensor(cw[:], r3[:], sc[:], ALU.mult),
         reads=[r3, sc], writes=[cw])
    P.do(v, lambda e: e.tensor_reduce(rsv(0), full(cw), axis=AX.X, op=ALU.add),
         reads=[cw], writes=[rs])
    P.do(v, lambda e: e.tensor_tensor(cw[:], r2[:], sc[:], ALU.mult),
         reads=[r2, sc], writes=[cw])
    P.do(v, lambda e: e.tensor_reduce(rsv(2), full(cw), axis=AX.X, op=ALU.add),
         reads=[cw], writes=[rs])
    P.do(v, lambda e: e.tensor_tensor(rsv(4), rsv(0), rsv(2), ALU.add),
         reads=[rs], writes=[rs])
    P.do(v, lambda e: e.tensor_scalar_add(rsv(4), rsv(4), 1e-20), reads=[rs], writes=[rs])
    P.do(v, lambda e: e.reciprocal(rsv(5), rsv(4)), reads=[rs], writes=[rs])
    P.do(v, lambda e: e.tensor_tensor(rsv(4), rsv(0), rsv(5), ALU.mult),
         reads=[rs], writes=[rs])
    P.do(v, lambda e: e.tensor_tensor(rsv(5), rsv(2), rsv(5), ALU.mult),
         reads=[rs], writes=[rs])
    P.do(v, lambda e: e.tensor_tensor(full(r3), full(r3), rsb(4), ALU.mult),
         reads=[r3, rs], writes=[r3])
    P.do(v, lambda e: e.tensor_tensor(full(r2), full(r2), rsb(5), ALU.mult),
         reads=[r2, rs], writes=[r2])
    P.do(v, lambda e: e.tensor_tensor(cw[:], r3[:], r2[:], ALU.add),
         reads=[r3, r2], writes=[cw])
    for tc in range(4):
        P.do("pe", lambda t, tc=tc: t.transpose(B.pb_s[0:8, 128:256], cw[:, tc * 8:tc * 8 + 8],
                                                C.ident[:]),
             reads=[cw, C.ident], writes=[B.pb_s])
        P.do(v, lambda e, tc=tc: e.tensor_copy(B.cwT[0:8, tc * 128:(tc + 1) * 128],
                                               B.pb_s[0:8, 128:256]),
             reads=[B.pb_s], writes=[B.cwT])


def emit_phase_l0ffn_routed(P, C, B, x_in, x_out, xn_spill, tiles=range(NTILES)):
    for t in tiles:
        xf = B.xf[t % 2]
        emit_dma_tile(P, "sp_dma", xf, x_in, t)
        P.do("dve", lambda v: v.tensor_copy(B.xb[:], xf[:]), reads=[xf], writes=[B.xb])
        emit_stats_ln(P, C, B.xb, B.xn, B.pb_s, B.pb_bc, B.aB, B.bB)
        emit_ln_apply(P, B.xb, B.aB, B.bB, B.xn)
        emit_dma_tile(P, "pool_dma", B.xn, xn_spill, t, write=True)

        def routing():
            for tc in range(4):
                P.do("pe", emit_linear(
                    P, lambda tc=tc: B.pb_s[:, tc * 8:(tc + 1) * 8],
                    lambda k, tc=tc: B.xn[:, k * NT + tc * 128:k * NT + tc * 128 + 128],
                    lambda k: B.gateT[:, k * 8:(k + 1) * 8], DC, None),
                    reads=[B.xn, B.gateT], writes=[B.pb_s])
            emit_routing_l0(P, C, B)
        for lc in range(4):
            pb = B.pb_h[:, (lc % 2) * 512:(lc % 2) * 512 + 512]
            hk = (B.pb_h, lc % 2)
            P.do("pe", emit_linear(
                P, lambda pb=pb: pb,
                lambda k, lc=lc: B.fc1T[:, k * 512 + lc * 128:k * 512 + lc * 128 + 128],
                lambda k: B.xn[:, k * NT:(k + 1) * NT], DC, None),
                reads=[B.xn, B.fc1T], writes=[hk])
            P.do("act", lambda s, lc=lc, pb=pb: s.copy(B.latT[:, lc * NT:(lc + 1) * NT], pb),
                 reads=[hk], writes=[B.latT])
            if lc == 0:
                routing()
        for e in range(8):
            P.do("pe", lambda t_, e=e: t_.matmul(B.pb_bc[:], C.sel8[:, e * 128:(e + 1) * 128],
                                                 B.cwT[:], start=True, stop=True),
                 reads=[B.cwT, C.sel8], writes=[B.pb_bc])
            P.do("act", lambda s: s.copy(B.cwB[:], B.pb_bc[:]),
                 reads=[B.pb_bc], writes=[B.cwB])
            def he1pair(p, e=e):
                def f(t_):
                    last = None
                    for icx in range(2):
                        ic = 2 * p + icx
                        for k in range(4):
                            last = t_.matmul(
                                B.pb_h[:, icx * 512:(icx + 1) * 512],
                                B.w1T[:, e * 2048 + k * 512 + ic * 128:
                                      e * 2048 + k * 512 + ic * 128 + 128],
                                B.latT[:, k * NT:(k + 1) * NT],
                                start=(k == 0), stop=(k == 3))
                    return last
                P.do("pe", f, reads=[B.latT, B.w1T],
                     writes=[(B.pb_h, 0), (B.pb_h, 1)])
                P.do("act", lambda s: s.activation(B.hr2[:], B.pb_h[:], AF.Relu),
                     reads=[(B.pb_h, 0), (B.pb_h, 1)], writes=[B.hr2])
                sl = slice(2 * p * NT, (2 * p + 2) * NT)
                P.do("dve", lambda v, sl=sl: v.tensor_tensor(
                    B.hs[:, sl], B.hr2[:], B.hr2[:], ALU.mult),
                    reads=[B.hr2], writes=[(B.hs, p)])
                P.do("dve", lambda v, sl=sl: v.tensor_tensor(
                    B.hs[:, sl].rearrange("p (a n) -> p a n", n=NT),
                    B.hs[:, sl].rearrange("p (a n) -> p a n", n=NT),
                    B.cwB[:].unsqueeze(1).broadcast_to([128, 2, NT]), ALU.mult),
                    reads=[(B.hs, p), B.cwB], writes=[(B.hs, p)])

            def he2pair(p, e=e):
                def f(t_):
                    last = None
                    for icx in range(2):
                        ic = 2 * p + icx
                        for lc in range(4):
                            last = t_.matmul(
                                B.pb_comb[:, lc * 512:(lc + 1) * 512],
                                B.w2T[:, e * 2048 + ic * 512 + lc * 128:
                                      e * 2048 + ic * 512 + lc * 128 + 128],
                                B.hs[:, ic * NT:(ic + 1) * NT],
                                start=(e == 0 and ic == 0), stop=(e == 7 and ic == 3))
                    return last
                P.do("pe", f, reads=[(B.hs, p), B.w2T], writes=[B.pb_comb])

            he1pair(0)
            he1pair(1)
            he2pair(0)
            he2pair(1)
        for lc in range(4):
            P.do("act", lambda s, lc=lc: s.copy(B.combT[:, lc * NT:(lc + 1) * NT],
                                                B.pb_comb[:, lc * 512:(lc + 1) * 512]),
                 reads=[B.pb_comb], writes=[B.combT])
        for co in range(8):
            pb = B.pb_h[:, (co % 2) * 512:(co % 2) * 512 + 512]
            hk = (B.pb_h, co % 2)
            P.do("pe", emit_linear(
                P, lambda pb=pb: pb,
                lambda k, co=co: B.fc2T[:, k * 1024 + co * 128:k * 1024 + co * 128 + 128],
                lambda k: B.combT[:, k * NT:(k + 1) * NT], 4, None),
                reads=[B.combT, B.fc2T], writes=[hk])
            P.do("dve", lambda v, co=co, pb=pb, xf=xf: v.tensor_tensor(
                B.xo[:, co * NT:(co + 1) * NT], pb, xf[:, co * NT:(co + 1) * NT], ALU.add),
                reads=[hk, xf], writes=[B.xo])
        emit_dma_tile(P, "sp_dma", B.xo, x_out, t, write=True)


class SharedBufs:
    def __init__(self, P):
        self.upT = sb(P, "s_upT", [128, DC * 2048], BF16)
        self.downT = sb(P, "s_downT", [128, 16 * 1024], BF16)
        self.xn = [sb(P, f"s_xn{i}", [128, DC * NT], BF16) for i in range(2)]
        self.xf = [sb(P, f"s_xf{i}", [128, DC * NT], F32) for i in range(2)]
        self.ur = sb(P, "s_ur", [128, NT], BF16)
        self.us = sb(P, "s_us", [128, 16 * NT], BF16)
        self.xo = sb(P, "s_xo", [128, DC * NT], F32)
        self.pb = [ps(P, f"s_pb{i}") for i in range(4)]


def load_shared_weights(P, B, hb, half):
    load_wT(P, B.upT, hb["upT"].ap()[:, half * 2048:(half + 1) * 2048])
    load_wT(P, B.downT, hb["downT"].ap()[half * 2048:(half + 1) * 2048, :])


def emit_phase_l0ffn_shared(P, C, B, x_in, x_out, xn_spill, half, tiles=range(NTILES)):
    for t in tiles:
        xn = B.xn[t % 2]
        xf = B.xf[t % 2]
        emit_dma_tile(P, "pool_dma", xn, xn_spill, t)
        emit_dma_tile(P, "sp_dma", xf, x_in, t)
        for fc in range(16):
            pb = B.pb[fc % 2]
            P.do("pe", emit_linear(
                P, lambda pb=pb: pb[:],
                lambda k, fc=fc: B.upT[:, k * 2048 + fc * 128:k * 2048 + fc * 128 + 128],
                lambda k, xn=xn: xn[:, k * NT:(k + 1) * NT], DC, None),
                reads=[xn, B.upT], writes=[pb])
            P.do("act", lambda s, pb=pb: s.activation(B.ur[:], pb[:], AF.Relu),
                 reads=[pb], writes=[B.ur])
            P.do("dve", lambda v, fc=fc: v.tensor_tensor(
                B.us[:, fc * NT:(fc + 1) * NT], B.ur[:], B.ur[:], ALU.mult),
                reads=[B.ur], writes=[B.us])
        for co in range(8):
            pb = B.pb[2 + co % 2]
            P.do("pe", emit_linear(
                P, lambda pb=pb: pb[:],
                lambda k, co=co: B.downT[:, k * 1024 + co * 128:k * 1024 + co * 128 + 128],
                lambda k: B.us[:, k * NT:(k + 1) * NT], 16, None),
                reads=[B.us, B.downT], writes=[pb])
            P.do("dve", lambda v, co=co, pb=pb, xf=xf: v.tensor_tensor(
                B.xo[:, co * NT:(co + 1) * NT], pb[:], xf[:, co * NT:(co + 1) * NT], ALU.add),
                reads=[pb, xf], writes=[B.xo])
        emit_dma_tile(P, "sp_dma", B.xo, x_out, t, write=True)


class AttnBufs:
    def __init__(self, P, rope: bool):
        self.Wa = sb(P, "a_Wa", [128, DC * 1024], BF16)
        self.Wb = sb(P, "a_Wb", [128, DC * 1024], BF16)
        self.xf = [sb(P, f"a_xf{i}", [128, DC * NT], F32) for i in range(2)]
        self.aB = sb(P, "a_aB", [128, NT], BF16)
        self.bB = sb(P, "a_bB", [128, NT], BF16)
        self.xn = sb(P, "a_xn", [128, DC * NT], BF16)
        self.KT = sb(P, "a_KT", [128, DC * S], BF16)
        self.Vaug = sb(P, "a_Vaug", [128, 16 * (16 * 65)], BF16)
        self.QT = sb(P, "a_QT", [128, DC * NT], BF16)
        self.ET = [sb(P, f"a_ET{i}", [128, 4 * NT], BF16) for i in range(2)]
        self.ctxT = sb(P, "a_ctxT", [128, DC * NT], BF16)
        self.rrow = sb(P, "a_rrow", [1, NT], F32)
        self.rBs = sb(P, "a_rBs", [64, NT], F32)
        self.masks = sb(P, "a_masks", [128, 4 * NT], BF16)
        self.xo = sb(P, "a_xo", [128, DC * NT], F32)
        self.rope = rope
        if rope:
            self.cosT = sb(P, "a_cosT", [128, S], BF16)
            self.sinA = sb(P, "a_sinA", [128, S], BF16)
            self.perm = sb(P, "a_perm", [128, 128], BF16)
            self.rtmp = [sb(P, f"a_rtmp{i}", [128, NT], BF16) for i in range(2)]
        else:
            self.bias = sb(P, "a_bias", [128, 4 * DC], F32)
        self.pb_q = ps(P, "a_pbq", (128, 4 * 512))
        self.pb_c = [ps(P, f"a_pbc{i}") for i in range(2)]
        self.pb_st = ps(P, "a_pbst")
        self.pb_bc = ps(P, "a_pbbc")


def attn_masks_np():
    m = np.zeros((128, 4 * NT), dtype=BF)
    for j in range(4):
        kk = np.arange(128)[:, None]
        qq = np.arange(NT)[None, :]
        m[:, j * NT:(j + 1) * NT] = (kk <= qq - 128 * j).astype(np.float32)
    return m


def rope_tables_np():
    inv = 1.0 / (10000.0 ** (np.arange(0, HD, 2, dtype=np.float64) / HD))
    t = np.arange(S, dtype=np.float64)
    fr = np.outer(inv, t)
    cos = np.cos(fr)
    sin = np.sin(fr)
    cosT = np.concatenate([cos, cos], 0)
    sinA = np.concatenate([-sin, sin], 0)
    cos128 = np.concatenate([cosT, cosT], 0).astype(np.float32)
    sin128 = np.concatenate([sinA, sinA], 0).astype(np.float32)
    perm = np.zeros((128, 128), dtype=np.float32)
    for b in range(2):
        for j in range(32):
            perm[b * 64 + j + 32, b * 64 + j] = 1.0
            perm[b * 64 + j, b * 64 + j + 32] = 1.0
    return cos128.astype(BF), sin128.astype(BF), perm.astype(BF)


def emit_rope(P, B, dst, dst_cols, pos0, pb, pbkey=None, par=0):
    if pbkey is None:
        pbkey = pb
    rt = B.rtmp[par % 2]
    sl = slice(dst_cols, dst_cols + NT)
    tsl = slice(pos0, pos0 + NT)
    P.do("pe", lambda t: t.matmul(pb[:], B.perm[:], dst[:, sl], start=True, stop=True),
         reads=[dst, B.perm], writes=[pbkey])
    P.do("dve", lambda v: v.tensor_tensor(rt[:], pb[:], B.sinA[:, tsl], ALU.mult),
         reads=[pbkey, B.sinA], writes=[rt])
    P.do("dve", lambda v: v.tensor_tensor(dst[:, sl], dst[:, sl], B.cosT[:, tsl], ALU.mult),
         reads=[dst, B.cosT], writes=[dst])
    P.do("dve", lambda v: v.tensor_tensor(dst[:, sl], dst[:, sl], rt[:], ALU.add),
         reads=[dst, rt], writes=[dst])


def emit_proj(P, B, W, xn, out_sb, co_range=range(DC), bias=None, bias_idx=0,
              evac_dtype_note="bf16"):
    for co in co_range:
        pb = B.pb_q[:, (co % 4) * 512:(co % 4) * 512 + 512]
        qk = (B.pb_q, co % 4)
        P.do("pe", emit_linear(
            P, lambda pb=pb: pb,
            lambda k, co=co, W=W: W[:, k * 1024 + co * 128:k * 1024 + co * 128 + 128],
            lambda k, xn=xn: xn[:, k * NT:(k + 1) * NT], DC, None),
            reads=[(xn, c2) for c2 in range(DC)] + [W], writes=[qk])
        if bias is not None:
            P.do("act", lambda s, co=co, pb=pb: s.activation(
                out_sb[:, co * NT:(co + 1) * NT], pb, AF.Identity,
                bias=B.bias[:, bias_idx * DC + co:bias_idx * DC + co + 1]),
                reads=[qk, B.bias], writes=[out_sb])
        else:
            P.do("act", lambda s, co=co, pb=pb: s.copy(
                out_sb[:, co * NT:(co + 1) * NT], pb),
                reads=[qk], writes=[out_sb])


def emit_ln_tile(P, C, B, xf, src_keys=None):
    nt = NT
    if src_keys is None:
        src_keys = [xf]
    ALLXN = [(B.xn, c) for c in range(DC)]
    P.do("dve", lambda v: v.tensor_copy(B.xn[:], xf[:]), reads=src_keys, writes=ALLXN)
    P.do("pe", emit_linear(P, lambda: B.pb_st[0:1, 0:nt], lambda k: C.onesc[:],
                           lambda k: B.xn[:, k * nt:(k + 1) * nt], DC, None),
         reads=ALLXN + [C.onesc], writes=[B.pb_st])
    P.do("dve", lambda v: v.tensor_tensor(B.xn[:], B.xn[:], B.xn[:], ALU.mult),
         reads=ALLXN, writes=ALLXN)
    P.do("pe", emit_linear(P, lambda: B.pb_st[64:65, 0:nt], lambda k: C.onesc[:],
                           lambda k: B.xn[:, k * nt:(k + 1) * nt], DC, None),
         reads=ALLXN + [C.onesc], writes=[B.pb_st])
    row = P._ln_rows
    m = row[0:1, 0:nt]
    v2 = row[0:1, nt:2 * nt]
    inv = row[0:1, 2 * nt:3 * nt]
    P.do("dve", lambda v: v.tensor_scalar_mul(m, B.pb_st[0:1, 0:nt], 1.0 / D),
         reads=[B.pb_st], writes=[row])
    P.do("dve", lambda v: v.tensor_scalar_mul(v2, B.pb_st[64:65, 0:nt], 1.0 / D),
         reads=[B.pb_st], writes=[row])
    P.do("dve", lambda v: v.tensor_mul(inv, m, m), reads=[row], writes=[row])
    P.do("dve", lambda v: v.tensor_sub(v2, v2, inv), reads=[row], writes=[row])
    P.do("dve", lambda v: v.tensor_scalar_add(v2, v2, 1e-5), reads=[row], writes=[row])
    P.do("act", lambda s: s.activation(inv, v2, AF.Ln), reads=[row], writes=[row])
    P.do("act", lambda s: s.activation(inv, inv, AF.Exp, scale=-0.5),
         reads=[row], writes=[row])
    P.do("dve", lambda v: v.tensor_mul(m, m, inv), reads=[row], writes=[row])
    P.do("dve", lambda v: v.tensor_scalar_mul(m, m, -1.0), reads=[row], writes=[row])
    P.do("pe", lambda t: t.matmul(B.pb_bc[:, 0:nt], C.onesr[:], inv, start=True, stop=True),
         reads=[row, C.onesr], writes=[B.pb_bc])
    P.do("act", lambda s: s.copy(B.aB[:], B.pb_bc[:, 0:nt]), reads=[B.pb_bc], writes=[B.aB])
    P.do("pe", lambda t: t.matmul(B.pb_bc[:, 0:nt], C.onesr[:], m, start=True, stop=True),
         reads=[row, C.onesr], writes=[B.pb_bc])
    P.do("act", lambda s: s.copy(B.bB[:], B.pb_bc[:, 0:nt]), reads=[B.pb_bc], writes=[B.bB])
    for c in range(DC):
        sl = slice(c * NT, (c + 1) * NT)
        P.do("dve", lambda v, sl=sl: v.tensor_tensor(B.xn[:, sl], xf[:, sl], B.aB[:],
                                                     ALU.mult),
             reads=src_keys + [B.aB], writes=[(B.xn, c)])
        P.do("dve", lambda v, sl=sl: v.tensor_tensor(B.xn[:, sl], B.xn[:, sl], B.bB[:],
                                                     ALU.add),
             reads=[(B.xn, c), B.bB], writes=[(B.xn, c)])


def emit_attn_zprep(P, C, B, t, i, zT, znT, nw):
    nt = NT
    xo = B.xo
    xokeys = [(B.xo, c) for c in range(DC)]
    xf = B.xf[i % 2]
    emit_ln_tile(P, C, B, xo, src_keys=xokeys)
    row = P._ln_rows
    inv = row[0:1, 2 * nt:3 * nt]
    r2 = row[0:1, nt:2 * nt]
    P.do("dve", lambda v: v.tensor_mul(r2, inv, inv), reads=[row], writes=[row])
    P.do("dve", lambda v: v.tensor_scalar(r2, r2, -1e-5, 1.0 + 1e-6, ALU.mult, ALU.add),
         reads=[row], writes=[row])
    P.do("act", lambda s: s.activation(r2, r2, AF.Ln), reads=[row], writes=[row])
    P.do("act", lambda s: s.activation(r2, r2, AF.Exp, scale=-0.5), reads=[row], writes=[row])
    for c in range(DC):
        sl = slice(c * NT, (c + 1) * NT)
        P.do("dve", lambda v, sl=sl: v.tensor_tensor(xf[:, sl], xo[:, sl], B.aB[:], ALU.mult),
             reads=[(B.xo, c), B.aB], writes=[xf])
        P.do("dve", lambda v, sl=sl: v.tensor_tensor(xf[:, sl], xf[:, sl], B.bB[:], ALU.add),
             reads=[xf, B.bB], writes=[xf])
    emit_dma_tile(P, "pool_dma", xf, zT, t, write=True)
    P.do("pe", lambda t_: t_.matmul(B.pb_bc[:, 0:nt], C.onesr[:], r2, start=True, stop=True),
         reads=[row, C.onesr], writes=[B.pb_bc])
    P.do("act", lambda s: s.copy(B.aB[:], B.pb_bc[:, 0:nt]), reads=[B.pb_bc], writes=[B.aB])
    for c in range(DC):
        sl = slice(c * NT, (c + 1) * NT)
        P.do("dve", lambda v, sl=sl: v.tensor_tensor(B.xn[:, sl], B.xn[:, sl], B.aB[:],
                                                     ALU.mult),
             reads=[(B.xn, c), B.aB], writes=[(B.xn, c)])
        P.do("act", lambda s, c=c, sl=sl: s.activation(B.xn[:, sl], B.xn[:, sl],
             AF.Identity, scale=nw[:, c:c + 1]),
             reads=[(B.xn, c), nw], writes=[(B.xn, c)])
    P.do("sp_dma", lambda e, t=t: e.dma_start(
        out=znT.rearrange("(c p) n -> p c n", p=128)[:, :, t * NT:(t + 1) * NT],
        in_=B.xn[:].rearrange("p (c n) -> p c n", n=NT)),
        reads=[(B.xn, c) for c in range(DC)], writes=[(znT.tensor, t)])


def emit_phase_attn(P, C, B, x_in, x_out, WqT, WkT, WvT, WoT, tiles_per_seq=4, seqs=None,
                    zprep=None):
    rope = B.rope
    P.do("dve", lambda v: v.memset(
        B.Vaug[:].rearrange("p (g x) -> p g x", x=16 * 65)[:, :, 64:16 * 65:65], 1.0),
        writes=[B.Vaug])
    if seqs is None:
        seqs = range(NSEQ)
    for s in seqs:
        load_wT(P, B.Wa, WkT)
        load_wT(P, B.Wb, WvT)
        for i in range(tiles_per_seq):
            t = s * tiles_per_seq + i
            xf = B.xf[i % 2]
            emit_dma_tile(P, "sp_dma", xf, x_in, t)
            emit_ln_tile(P, C, B, xf)
            for co in range(DC):
                pb = B.pb_q[:, (co % 4) * 512:(co % 4) * 512 + 512]
                qk = (B.pb_q, co % 4)
                P.do("pe", emit_linear(
                    P, lambda pb=pb: pb,
                    lambda k, co=co: B.Wa[:, k * 1024 + co * 128:k * 1024 + co * 128 + 128],
                    lambda k: B.xn[:, k * NT:(k + 1) * NT], DC, None),
                    reads=[(B.xn, c2) for c2 in range(DC)] + [B.Wa], writes=[qk])
                if rope:
                    P.do("act", lambda sc, co=co, pb=pb, i=i: sc.copy(
                        B.KT[:, co * S + i * NT:co * S + (i + 1) * NT], pb),
                        reads=[qk], writes=[B.KT])
                else:
                    P.do("act", lambda sc, co=co, pb=pb, i=i: sc.activation(
                        B.KT[:, co * S + i * NT:co * S + (i + 1) * NT], pb, AF.Identity,
                        bias=B.bias[:, 1 * DC + co:1 * DC + co + 1]),
                        reads=[qk, B.bias], writes=[B.KT])
            if rope:
                for co in range(DC):
                    emit_rope(P, B, B.KT, co * S + i * NT, i * NT,
                              B.pb_q[:, (co % 4) * 512:(co % 4) * 512 + 512],
                              (B.pb_q, co % 4), par=co)
            for tc in range(4):
                for dvh in range(2):
                    qd = (tc * 2 + dvh) % 4
                    pb = B.pb_q[:, qd * 512:qd * 512 + 512]
                    qk = (B.pb_q, qd)
                    P.do("pe", emit_linear(
                        P, lambda pb=pb: pb,
                        lambda k, tc=tc: B.xn[:, k * NT + tc * 128:k * NT + tc * 128 + 128],
                        lambda k, dvh=dvh: B.Wb[:, k * 1024 + dvh * 512:k * 1024 + (dvh + 1) * 512],
                        DC, None),
                        reads=[(B.xn, c2) for c2 in range(DC)] + [B.Wb], writes=[qk])
                    gtc = i * 4 + tc
                    h0 = dvh * 8
                    dst = B.Vaug[:, gtc * 1040 + h0 * 65:gtc * 1040 + h0 * 65 + 65 * 8]\
                        .rearrange("p (hh x) -> p hh x", x=65)[:, :, 0:64]
                    P.do("act", lambda sc, pb=pb, dst=dst: sc.copy(
                        dst, pb.rearrange("p (hh x) -> p hh x", x=64)),
                        reads=[qk], writes=[B.Vaug])
        load_wT(P, B.Wa, WqT)
        load_wT(P, B.Wb, WoT)
        for i in range(tiles_per_seq):
            t = s * tiles_per_seq + i
            xf = B.xf[i % 2]
            emit_dma_tile(P, "sp_dma", xf, x_in, t)
            emit_ln_tile(P, C, B, xf)
            if rope:
                emit_proj(P, B, B.Wa, B.xn, B.QT)
                for co in range(DC):
                    emit_rope(P, B, B.QT, co * NT, i * NT,
                              B.pb_q[:, (co % 4) * 512:(co % 4) * 512 + 512],
                              (B.pb_q, co % 4), par=co)
            else:
                emit_proj(P, B, B.Wa, B.xn, B.QT, bias=True, bias_idx=0)
            kc = 4 * i + 4

            def normalize_head(h):
                hrow = (h % 2) * 64
                P.do("act", lambda s_, h=h: s_.activation(
                    B.rBs[0:1, :], B.pb_c[h % 2][64:65, :], AF.Ln),
                    reads=[B.pb_c[h % 2]], writes=[B.rBs])
                P.do("act", lambda s_: s_.activation(
                    B.rrow[:], B.rBs[0:1, :], AF.Exp, scale=-1.0),
                    reads=[B.rBs], writes=[B.rrow])
                P.do("pe", lambda t_, h=h: t_.matmul(B.pb_bc[0:64, :], C.onesr[0:1, 0:64],
                                                     B.rrow[:], start=True, stop=True),
                     reads=[B.rrow, C.onesr], writes=[B.pb_bc])
                P.do("act", lambda sc: sc.copy(B.rBs[:], B.pb_bc[0:64, :]),
                     reads=[B.pb_bc], writes=[B.rBs])
                P.do("dve", lambda v, h=h, hrow=hrow: v.tensor_tensor(
                    B.ctxT[hrow:hrow + 64, (h // 2) * NT:(h // 2 + 1) * NT],
                    B.pb_c[h % 2][0:64, :], B.rBs[:], ALU.mult),
                    reads=[B.pb_c[h % 2], B.rBs], writes=[(B.ctxT, h)])
                if not rope:
                    P.do("act", lambda sc, h=h, hrow=hrow: sc.activation(
                        B.ctxT[hrow:hrow + 64, (h // 2) * NT:(h // 2 + 1) * NT],
                        B.ctxT[hrow:hrow + 64, (h // 2) * NT:(h // 2 + 1) * NT],
                        AF.Identity,
                        bias=B.bias[hrow:hrow + 64, 2 * DC + h // 2:2 * DC + h // 2 + 1]),
                        reads=[(B.ctxT, h), B.bias], writes=[(B.ctxT, h)])

            ng2 = kc // 2
            for hp in range(H // 2):
                h0, h1 = 2 * hp, 2 * hp + 1

                def sg(h, w, g):
                    hrow = (h % 2) * 64
                    def f(t_):
                        last = None
                        for jj in range(2):
                            j = 2 * g + jj
                            last = t_.matmul(
                                B.pb_q[:, (2 * w + jj) * 512:(2 * w + jj) * 512 + 512],
                                B.KT[hrow:hrow + 64,
                                     (h // 2) * S + j * 128:(h // 2) * S + j * 128 + 128],
                                B.QT[hrow:hrow + 64, (h // 2) * NT:(h // 2 + 1) * NT],
                                start=True, stop=True)
                        return last
                    return f

                def eg(w, g):
                    et = B.ET[w][:, (g % 2) * 1024:(g % 2) * 1024 + 1024]
                    P.do("act", lambda sc, et=et, w=w: sc.activation(
                        et, B.pb_q[:, 2 * w * 512:2 * w * 512 + 1024],
                        AF.Exp, bias=0.0, scale=0.125),
                        reads=[(B.pb_q, 2 * w), (B.pb_q, 2 * w + 1)],
                        writes=[(B.ET[w], g % 2)])
                    j0 = 2 * g
                    if j0 >= 4 * i:
                        m = j0 - 4 * i
                        P.do("dve", lambda v, et=et, m=m: v.tensor_tensor(
                            et, et, B.masks[:, m * NT:(m + 2) * NT], ALU.mult),
                            reads=[(B.ET[w], g % 2), B.masks], writes=[(B.ET[w], g % 2)])

                def cg(h, w, g):
                    def f(t_):
                        last = None
                        for jj in range(2):
                            j = 2 * g + jj
                            last = t_.matmul(
                                B.pb_c[w][0:65, :],
                                B.Vaug[:, j * 1040 + h * 65:j * 1040 + (h + 1) * 65],
                                B.ET[w][:, ((g % 2) * 2 + jj) * 512:((g % 2) * 2 + jj) * 512 + 512],
                                start=(j == 0), stop=(j == kc - 1))
                        return last
                    return f
                P.do("pe", sg(h0, 0, 0), reads=[B.KT, B.QT],
                     writes=[(B.pb_q, 0), (B.pb_q, 1)])
                P.do("pe", sg(h1, 1, 0), reads=[B.KT, B.QT],
                     writes=[(B.pb_q, 2), (B.pb_q, 3)])
                if hp > 0:
                    normalize_head(2 * hp - 2)
                    normalize_head(2 * hp - 1)
                eg(0, 0)
                eg(1, 0)
                for g in range(1, ng2):
                    P.do("pe", sg(h0, 0, g), reads=[B.KT, B.QT],
                         writes=[(B.pb_q, 0), (B.pb_q, 1)])
                    P.do("pe", sg(h1, 1, g), reads=[B.KT, B.QT],
                         writes=[(B.pb_q, 2), (B.pb_q, 3)])
                    eg(0, g)
                    eg(1, g)
                    P.do("pe", cg(h0, 0, g - 1), reads=[B.Vaug, (B.ET[0], (g - 1) % 2)],
                         writes=[B.pb_c[0]])
                    P.do("pe", cg(h1, 1, g - 1), reads=[B.Vaug, (B.ET[1], (g - 1) % 2)],
                         writes=[B.pb_c[1]])
                P.do("pe", cg(h0, 0, ng2 - 1), reads=[B.Vaug, (B.ET[0], (ng2 - 1) % 2)],
                     writes=[B.pb_c[0]])
                P.do("pe", cg(h1, 1, ng2 - 1), reads=[B.Vaug, (B.ET[1], (ng2 - 1) % 2)],
                     writes=[B.pb_c[1]])
            normalize_head(H - 2)
            normalize_head(H - 1)
            for co in range(DC):
                pb = B.pb_q[:, (co % 4) * 512:(co % 4) * 512 + 512]
                qk = (B.pb_q, co % 4)
                P.do("pe", emit_linear(
                    P, lambda pb=pb: pb,
                    lambda k, co=co: B.Wb[:, k * 1024 + co * 128:k * 1024 + co * 128 + 128],
                    lambda k: B.ctxT[:, k * NT:(k + 1) * NT], DC, None),
                    reads=[(B.ctxT, hh) for hh in range(H)] + [B.Wb], writes=[qk])
                bias_s = 0.0 if rope else B.bias[:, 3 * DC + co:3 * DC + co + 1]
                brd = [] if rope else [B.bias]
                P.do("dve", lambda v, co=co, pb=pb, xf=xf, bias_s=bias_s:
                     v.scalar_tensor_tensor(
                         B.xo[:, co * NT:(co + 1) * NT], pb, bias_s,
                         xf[:, co * NT:(co + 1) * NT], ALU.add, ALU.add),
                     reads=[qk, xf] + brd, writes=[(B.xo, co)])
            emit_dma_tile(P, "sp_dma", B.xo, x_out, t, write=True, subkeys=range(8))
            if zprep is not None:
                emit_attn_zprep(P, C, B, t, i, zprep[0], zprep[1], zprep[2])


class L1FFNBufs:
    def __init__(self, P):
        self.wmacT = sb(P, "e_wmacT", [128, DC * 4], BF16)
        self.wmicT = sb(P, "e_wmicT", [128, DC * 8], BF16)
        self.nw = sb(P, "e_nw", [128, DC], F32)
        self.w1sb = sb(P, "e_w1", [128, 8 * 2048], BF16)
        self.w2sb = sb(P, "e_w2", [128, 16 * 1024], BF16)
        self.z = [sb(P, f"e_z{i}", [128, DC * NT], F32) for i in range(2)]
        self.zb = [sb(P, f"e_zb{i}", [128, DC * NT], BF16) for i in range(2)]
        self.zn = [sb(P, f"e_zn{i}", [128, DC * NT], BF16) for i in range(2)]
        self.y = [sb(P, f"e_y{i}", [128, DC * NT], BF16) for i in range(2)]
        self.hs = sb(P, "e_hs", [128, 16 * NT], BF16)
        self.xr = sb(P, "e_xr", [128, DC * NT], F32)
        self.aB = sb(P, "e_aB", [128, NT], BF16)
        self.bB2 = sb(P, "e_bB2", [128, NT], BF16)
        self.sm = sb(P, "e_sm", [128, 16], F32)
        self.mi = sb(P, "e_mi", [128, 32], F32)
        self.r1 = sb(P, "e_r1", [128, 16], F32)
        self.r2 = sb(P, "e_r2", [128, 16], F32)
        self.r3 = sb(P, "e_r3", [128, 16], F32)
        self.r4 = sb(P, "e_r4", [128, 16], F32)
        self.rs = sb(P, "e_rs", [128, 4 * 4], F32)
        self.rows = sb(P, "e_rows", [1, 6 * NT], F32)
        self.cw = sb(P, "e_cw", [128, 32], F32)
        self.cwT = [sb(P, f"e_cwT{i}", [8, NT], BF16) for i in range(2)]
        self.cwB = [sb(P, f"e_cwB{i}", [128, NT], BF16) for i in range(2)]
        self.pb_h = [ps(P, f"e_pbh{i}") for i in range(2)]
        self.pb_y = [ps(P, f"e_pby{i}") for i in range(2)]
        self.pb_r = ps(P, "e_pbr")
        self.pb_bc = ps(P, "e_pbbc")
        self.pb_bc2 = ps(P, "e_pbbc2")
        self.pb_t = ps(P, "e_pbt")


def emit_stage_bf16(P, buf, src_ap, dst_ap, key):
    load_wT(P, buf, src_ap)
    w = dst_ap.shape[-1]
    P.do("sp_dma", lambda e: e.dma_start(out=dst_ap, in_=buf[:, 0:w]),
         reads=[buf], writes=[key])


def emit_weight_convert_l1(P, buf1, buf2, w1T_hbm, w2T_hbm, w1bf, w2bf):
    for e in range(8):
        b = buf1 if e % 2 == 0 else buf2
        load_wT(P, b, w1T_hbm[e])
        P.do("sp_dma", lambda en, e=e, b=b: en.dma_start(out=w1bf[e], in_=b[:]),
             reads=[b], writes=[(w1bf.tensor, e)])
        load_wT(P, b, w2T_hbm[e])
        P.do("act_dma", lambda en, e=e, b=b: en.dma_start(out=w2bf[e], in_=b[:]),
             reads=[b], writes=[(w2bf.tensor, e)])


def emit_routing_l1(P, C, E, cwT):
    v = "dve"
    sm, r1, r2, r3, r4, rs, cw = E.sm, E.r1, E.r2, E.r3, E.r4, E.rs, E.cw

    def v4(buf):
        return buf[:].rearrange("p (tc e) -> p tc e", e=4)

    def v8(buf):
        return buf[:].rearrange("p (tc e) -> p tc e", e=8)

    def rsv(i):
        return rs[:, i * 4:(i + 1) * 4]

    def rsb(i, n=4):
        return rs[:, i * 4:(i + 1) * 4].unsqueeze(2).broadcast_to([128, 4, n])

    mac = E.pb_r[:, 0:16].rearrange("p (tc e) -> p tc e", e=4)
    iota4b = C.iota8[:, 0:4].unsqueeze(1).broadcast_to([128, 4, 4])
    P.do(v, lambda e: e.tensor_reduce(rsv(0), mac, axis=AX.X, op=ALU.max),
         reads=[E.pb_r], writes=[rs])
    P.do(v, lambda e: e.tensor_tensor(v4(sm), mac, rsb(0), ALU.subtract),
         reads=[E.pb_r, rs], writes=[sm])
    P.do("act", lambda s: s.activation(sm[:], sm[:], AF.Exp), reads=[sm], writes=[sm])
    P.do(v, lambda e: e.tensor_reduce(rsv(0), v4(sm), axis=AX.X, op=ALU.add),
         reads=[sm], writes=[rs])
    P.do(v, lambda e: e.reciprocal(rsv(1), rsv(0)), reads=[rs], writes=[rs])
    P.do(v, lambda e: e.tensor_tensor(v4(sm), v4(sm), rsb(1), ALU.mult),
         reads=[sm, rs], writes=[sm])
    P.do(v, lambda e: e.tensor_reduce(rsv(0), v4(sm), axis=AX.X, op=ALU.max),
         reads=[sm], writes=[rs])
    P.do(v, lambda e: e.tensor_tensor(v4(r1), v4(sm), rsb(0), ALU.is_ge),
         reads=[sm, rs], writes=[r1])
    P.do(v, lambda e: e.scalar_tensor_tensor(v4(r1), v4(r1), -999.0, iota4b,
                                             ALU.mult, ALU.add),
         reads=[r1, C.iota8], writes=[r1])
    P.do(v, lambda e: e.tensor_reduce(rsv(1), v4(r1), axis=AX.X, op=ALU.min),
         reads=[r1], writes=[rs])
    P.do(v, lambda e: e.tensor_scalar_add(rsv(1), rsv(1), 999.0), reads=[rs], writes=[rs])
    P.do(v, lambda e: e.tensor_tensor(v4(r1), iota4b, rsb(1), ALU.is_equal),
         reads=[C.iota8, rs], writes=[r1])
    P.do(v, lambda e: e.scalar_tensor_tensor(v4(r2), v4(r1), -1e30, v4(sm),
                                             ALU.mult, ALU.add),
         reads=[r1, sm], writes=[r2])
    P.do(v, lambda e: e.tensor_reduce(rsv(2), v4(r2), axis=AX.X, op=ALU.max),
         reads=[r2], writes=[rs])
    P.do(v, lambda e: e.tensor_tensor(v4(r2), v4(r2), rsb(2), ALU.is_ge),
         reads=[r2, rs], writes=[r2])
    P.do(v, lambda e: e.scalar_tensor_tensor(v4(r2), v4(r2), -999.0, iota4b,
                                             ALU.mult, ALU.add),
         reads=[r2, C.iota8], writes=[r2])
    P.do(v, lambda e: e.tensor_reduce(rsv(3), v4(r2), axis=AX.X, op=ALU.min),
         reads=[r2], writes=[rs])
    P.do(v, lambda e: e.tensor_scalar_add(rsv(3), rsv(3), 999.0), reads=[rs], writes=[rs])
    P.do(v, lambda e: e.tensor_tensor(v4(r2), iota4b, rsb(3), ALU.is_equal),
         reads=[C.iota8, rs], writes=[r2])
    P.do(v, lambda e: e.tensor_tensor(rsv(3), rsv(0), rsv(2), ALU.add),
         reads=[rs], writes=[rs])
    P.do(v, lambda e: e.tensor_scalar_add(rsv(3), rsv(3), 1e-9), reads=[rs], writes=[rs])
    P.do(v, lambda e: e.reciprocal(rsv(3), rsv(3)), reads=[rs], writes=[rs])
    P.do(v, lambda e: e.tensor_tensor(rsv(0), rsv(0), rsv(3), ALU.mult),
         reads=[rs], writes=[rs])
    P.do(v, lambda e: e.tensor_tensor(rsv(2), rsv(2), rsv(3), ALU.mult),
         reads=[rs], writes=[rs])
    P.do(v, lambda e: e.tensor_tensor(v4(r1), v4(r1), rsb(0), ALU.mult),
         reads=[r1, rs], writes=[r1])
    P.do(v, lambda e: e.tensor_tensor(v4(r2), v4(r2), rsb(2), ALU.mult),
         reads=[r2, rs], writes=[r2])
    P.do(v, lambda e: e.tensor_tensor(r1[:], r1[:], r2[:], ALU.add),
         reads=[r1, r2], writes=[r1])
    P.do("act", lambda s: s.copy(E.mi[:], E.pb_r[:, 16:48]), reads=[E.pb_r], writes=[E.mi])
    mic = E.mi[:].rearrange("p (tc e) -> p tc e", e=8)
    P.do(v, lambda e: e.tensor_tensor(v4(r2), mic[:, :, 0:8:2], mic[:, :, 1:8:2],
                                      ALU.subtract),
         reads=[E.mi], writes=[r2])
    P.do("act", lambda s: s.activation(r2[:], r2[:], AF.Sigmoid), reads=[r2], writes=[r2])
    P.do(v, lambda e: e.tensor_scalar(r3[:], r2[:], -1.0, 1.0, ALU.mult, ALU.add),
         reads=[r2], writes=[r3])
    P.do(v, lambda e: e.tensor_tensor(r4[:], r2[:], r3[:], ALU.max),
         reads=[r2, r3], writes=[r4])
    P.do(v, lambda e: e.tensor_tensor(r3[:], r2[:], r3[:], ALU.is_ge),
         reads=[r2, r3], writes=[r3])
    P.do(v, lambda e: e.tensor_scalar_add(r2[:], r4[:], 1e-9), reads=[r4], writes=[r2])
    P.do(v, lambda e: e.reciprocal(r2[:], r2[:]), reads=[r2], writes=[r2])
    P.do(v, lambda e: e.tensor_tensor(r4[:], r4[:], r2[:], ALU.mult),
         reads=[r4, r2], writes=[r4])
    P.do(v, lambda e: e.tensor_tensor(r4[:], r1[:], r4[:], ALU.mult),
         reads=[r1, r4], writes=[r4])
    P.do(v, lambda e: e.tensor_scalar(r2[:], r3[:], -1.0, 1.0, ALU.mult, ALU.add),
         reads=[r3], writes=[r2])
    P.do(v, lambda e: e.tensor_tensor(v8(cw)[:, :, 0::2], v4(r4), v4(r3), ALU.mult),
         reads=[r4, r3], writes=[cw])
    P.do(v, lambda e: e.tensor_tensor(v8(cw)[:, :, 1::2], v4(r4), v4(r2), ALU.mult),
         reads=[r4, r2], writes=[cw])
    for tc in range(4):
        P.do("pe", lambda t, tc=tc: t.transpose(E.pb_t[0:8, 0:128], cw[:, tc * 8:tc * 8 + 8],
                                                C.ident[:]),
             reads=[cw, C.ident], writes=[E.pb_t])
        P.do(v, lambda e, tc=tc: e.tensor_copy(cwT[0:8, tc * 128:(tc + 1) * 128],
                                               E.pb_t[0:8, 0:128]),
             reads=[E.pb_t], writes=[cwT])


def emit_phase_l1ffn_loop(P, C, E, loop, x_in, zT, znT, x_out, w1bf, w2bf,
                          tiles=range(NTILES)):
    tiles = list(tiles)

    def preamble_dma(t, s):
        emit_dma_tile(P, "pool_dma", E.z[s], zT, t)
        P.do("sp_dma", lambda e, t=t, s=s: e.dma_start(
            out=E.zn[s][:].rearrange("p (c n) -> p c n", n=NT),
            in_=znT.rearrange("(c p) n -> p c n", p=128)[:, :, t * NT:(t + 1) * NT]),
            reads=[(znT.tensor, t)], writes=[(E.zn[s], c) for c in range(DC)])

    def preamble_route(t, s):
        zn, cwT = E.zn[s], E.cwT[s]
        for tc in range(4):
            P.do("pe", emit_linear(
                P, lambda tc=tc: E.pb_r[:, tc * 4:(tc + 1) * 4],
                lambda k, tc=tc: zn[:, k * NT + tc * 128:k * NT + tc * 128 + 128],
                lambda k: E.wmacT[:, k * 4:(k + 1) * 4], DC, None),
                reads=[(zn, c2) for c2 in range(DC)] + [E.wmacT], writes=[E.pb_r])
            P.do("pe", emit_linear(
                P, lambda tc=tc: E.pb_r[:, 16 + tc * 8:16 + (tc + 1) * 8],
                lambda k, tc=tc: zn[:, k * NT + tc * 128:k * NT + tc * 128 + 128],
                lambda k: E.wmicT[:, k * 8:(k + 1) * 8], DC, None),
                reads=[(zn, c2) for c2 in range(DC)] + [E.wmicT], writes=[E.pb_r])
        emit_routing_l1(P, C, E, cwT)

    def experts_pair(pair):
        w1 = E.w1sb
        for e in range(8):
            for wh in range(2):
                P.do("sp_dma", lambda en, e=e, wh=wh: en.dma_start(
                    out=w1[:, wh * 8192:(wh + 1) * 8192],
                    in_=w1bf[e][:, wh * 8192:(wh + 1) * 8192]),
                    reads=[(w1bf.tensor, e)], writes=[(w1, wh)])
            P.do("act_dma", lambda en, e=e: en.dma_start(out=E.w2sb[:], in_=w2bf[e]),
                 reads=[(w2bf.tensor, e)], writes=[E.w2sb])
            for pi, (t, s) in enumerate(pair):
                zn, cwT, y = E.zn[s], E.cwT[s], E.y[s]
                first = (e == 0)

                par = (e + pi) % 2
                cwb = E.cwB[par]

                def bcast(e=e, cwT=cwT, par=par, cwb=cwb):
                    bc = E.pb_bc if par == 0 else E.pb_bc2
                    P.do("pe", lambda t_: t_.matmul(
                        bc[:], C.sel8[:, e * 128:(e + 1) * 128],
                        cwT[:], start=True, stop=True),
                        reads=[cwT, C.sel8], writes=[bc])
                    P.do("act", lambda s_: s_.copy(cwb[:], bc[:]),
                         reads=[bc], writes=[cwb])

                def scale(fc, cwb=cwb):
                    P.do("dve", lambda v, fc=fc: v.tensor_tensor(
                        E.hs[:, fc * NT:(fc + 1) * NT], E.hs[:, fc * NT:(fc + 1) * NT],
                        cwb[:], ALU.mult),
                        reads=[(E.hs, fc), cwb], writes=[(E.hs, fc)])
                if not first:
                    bcast()
                for fc in range(16):
                    pb = E.pb_h[fc % 2]

                    def hmm(k0, k1, fc=fc, pb=pb, zn=zn):
                        def f(t_):
                            last = None
                            for k in range(k0, k1):
                                last = t_.matmul(
                                    pb[:], w1[:, k * 2048 + fc * 128:k * 2048 + fc * 128 + 128],
                                    zn[:, k * NT:(k + 1) * NT],
                                    start=(k == 0), stop=(k == DC - 1))
                            return last
                        return f
                    P.do("pe", hmm(0, 4),
                         reads=[(zn, c2) for c2 in range(4)] + [(w1, 0)], writes=[pb])
                    P.do("pe", hmm(4, 8),
                         reads=[(zn, c2) for c2 in range(4, 8)] + [(w1, 1)], writes=[pb])
                    if first and pi == 0 and fc == 0:
                        for t2, s2 in pair:
                            preamble_route(t2, s2)
                    if first and fc == 10:
                        bcast()
                    P.do("act", lambda s_, fc=fc, pb=pb: s_.activation(
                        E.hs[:, fc * NT:(fc + 1) * NT], pb[:], AF.Silu),
                        reads=[pb], writes=[(E.hs, fc)])
                    if not first or fc >= 10:
                        scale(fc)
                        if first and fc >= 10:
                            scale(fc - 10)
                if first:
                    for fcx in range(6, 10):
                        scale(fcx)
                if first:
                    pass
                for co in range(8):
                    pb = E.pb_y[co % 2]

                    def ymm(k0, k1, co=co, pb=pb):
                        def f(t_):
                            last = None
                            for k in range(k0, k1):
                                last = t_.matmul(
                                    pb[:],
                                    E.w2sb[:, k * 1024 + co * 128:k * 1024 + co * 128 + 128],
                                    E.hs[:, k * NT:(k + 1) * NT],
                                    start=(k == 0), stop=(k == 15))
                            return last
                        return f
                    P.do("pe", ymm(0, 14),
                         reads=[(E.hs, k) for k in range(14)] + [E.w2sb], writes=[pb])
                    P.do("pe", ymm(14, 16),
                         reads=[(E.hs, 14), (E.hs, 15), E.w2sb], writes=[pb])
                    if e == 0:
                        P.do("act", lambda s_, co=co, pb=pb, y=y: s_.copy(
                            y[:, co * NT:(co + 1) * NT], pb[:]),
                            reads=[pb], writes=[(y, co)])
                    else:
                        P.do("dve", lambda v, co=co, pb=pb, y=y: v.tensor_tensor(
                            y[:, co * NT:(co + 1) * NT], y[:, co * NT:(co + 1) * NT],
                            pb[:], ALU.add),
                            reads=[(y, co), pb], writes=[(y, co)])

    def finals(t, s):
        z = E.z[s]
        y = E.y[s]
        if loop == 0:
            zn = E.zn[s]
            zbk = y
            nt = NT
            rw = E.rows
            v2 = rw[0:1, (3 * s + 1) * nt:(3 * s + 1) * nt + nt]
            inv = rw[0:1, (3 * s + 2) * nt:(3 * s + 2) * nt + nt]
            P.do("dve", lambda v, y=y: v.tensor_tensor(z[:], z[:], y[:], ALU.add),
                 reads=[z] + [(z, c) for c in range(DC)] + [(y, co) for co in range(8)],
                 writes=[z] + [(z, c) for c in range(DC)])
            emit_dma_tile(P, "sp_dma", z, zT, t, write=True)
            P.do("dve", lambda v, zbk=zbk: v.tensor_tensor(zbk[:], z[:], z[:], ALU.mult),
                 reads=[z], writes=[(zbk, co) for co in range(8)])
            P.do("pe", emit_linear(P, lambda: E.pb_r[64:65, 0:nt], lambda k: C.onesc[:],
                                   lambda k, zbk=zbk: zbk[:, k * nt:(k + 1) * nt], DC, None),
                 reads=[(zbk, co) for co in range(8)] + [C.onesc], writes=[E.pb_r])
            P.do("dve", lambda v: v.tensor_scalar(v2, E.pb_r[64:65, 0:nt],
                                                  1.0 / D, 1e-6, ALU.mult, ALU.add),
                 reads=[E.pb_r], writes=[rw])
            P.do("act", lambda s_: s_.activation(inv, v2, AF.Ln), reads=[rw], writes=[rw])
            P.do("act", lambda s_: s_.activation(inv, inv, AF.Exp, scale=-0.5),
                 reads=[rw], writes=[rw])
            P.do("pe", lambda t_: t_.matmul(E.pb_bc[:, 0:nt], C.onesr[:], inv,
                                            start=True, stop=True),
                 reads=[rw, C.onesr], writes=[E.pb_bc])
            P.do("act", lambda s_: s_.copy(E.aB[:], E.pb_bc[:, 0:nt]),
                 reads=[E.pb_bc], writes=[E.aB])
            for c in range(DC):
                sl = slice(c * NT, (c + 1) * NT)
                P.do("dve", lambda v, sl=sl, zn=zn: v.tensor_tensor(
                    zn[:, sl], z[:, sl], E.aB[:], ALU.mult),
                    reads=[z, E.aB], writes=[(zn, c)])
                P.do("act", lambda s_, c=c, sl=sl, zn=zn: s_.activation(
                    zn[:, sl], zn[:, sl], AF.Identity, scale=E.nw[:, c:c + 1]),
                    reads=[(zn, c), E.nw], writes=[(zn, c)])
            P.do("sp_dma", lambda e, t=t, zn=zn: e.dma_start(
                out=znT.rearrange("(c p) n -> p c n", p=128)[:, :, t * NT:(t + 1) * NT],
                in_=zn[:].rearrange("p (c n) -> p c n", n=NT)),
                reads=[(zn, c) for c in range(DC)], writes=[(znT.tensor, t)])
        else:
            emit_dma_tile(P, "pool_dma", E.xr, x_in, t)
            P.do("dve", lambda v, y=y: v.tensor_tensor(z[:], z[:], y[:], ALU.add),
                 reads=[z] + [(z, c) for c in range(DC)] + [(y, co) for co in range(8)],
                 writes=[z] + [(z, c) for c in range(DC)])
            P.do("dve", lambda v: v.tensor_tensor(z[:], z[:], E.xr[:], ALU.add),
                 reads=[z, E.xr], writes=[z] + [(z, c) for c in range(DC)])
            emit_dma_tile(P, "sp_dma", z, x_out, t, write=True)

    assert len(tiles) % 2 == 0
    for i in range(0, len(tiles), 2):
        pair = [(tiles[i], 0), (tiles[i + 1], 1)]
        for t, s in pair:
            preamble_dma(t, s)
        experts_pair(pair)
        for t, s in pair:
            finals(t, s)

FUSED = True


def host_prep(inputs):
    g = {}
    ac = np.ascontiguousarray
    g["WqT0"], g["WkT0"], g["WvT0"], g["WoT0"] = (
        inputs["L0_attn_Wq"], inputs["L0_attn_Wk"],
        inputs["L0_attn_Wv"], inputs["L0_attn_Wo"])
    ab = np.zeros((128, 32), np.float32)
    for i, n in enumerate(("bq", "bk", "bv", "bo")):
        ab[:, i * 8:(i + 1) * 8] = inputs[f"L0_attn_{n}"].reshape(8, 128).T
    g["abias"] = ab
    g["gateT"] = ac(inputs["L0_ffn_gate"].T)
    g["corr4"] = np.tile(inputs["L0_ffn_corr_bias"], (128, 4)).astype(np.float32)
    g["iota32"] = np.tile(np.arange(8, dtype=np.float32), (128, 4))
    g["fc1T"] = ac(inputs["L0_ffn_fc1_w"].T)
    g["fc2T"] = ac(inputs["L0_ffn_fc2_w"].T)
    g["w1Tl0"] = ac(inputs["L0_ffn_w1"].transpose(0, 2, 1))
    g["w2Tl0"] = ac(inputs["L0_ffn_w2"].transpose(0, 2, 1))
    g["upT"] = ac(inputs["L0_ffn_up_w"].T)
    g["downT"] = ac(inputs["L0_ffn_down_w"].T)
    g["WqT1"] = ac(inputs["L1_attn_Wq"].T)
    g["WkT1"] = ac(inputs["L1_attn_Wk"].T)
    g["WvT1"] = ac(inputs["L1_attn_Wv"].T)
    g["WoT1"] = ac(inputs["L1_attn_Wo"].T)
    g["wmacT"] = ac(inputs["L1_ffn_w_macro"].T)
    g["wmicT"] = ac(inputs["L1_ffn_w_micro"].reshape(8, 1024).T)
    g["nw"] = ac(inputs["L1_ffn_norm_w"].reshape(8, 128).T)
    g["w1Tl1"] = ac(inputs["L1_ffn_w1"].transpose(0, 2, 1))
    g["w2Tl1"] = ac(inputs["L1_ffn_w2"].transpose(0, 2, 1))
    return g


_W_SPECS = {
    "WqT0": [1024, 1024], "WkT0": [1024, 1024], "WvT0": [1024, 1024],
    "WoT0": [1024, 1024], "abias": [128, 32],
    "gateT": [1024, 8], "corr4": [128, 32], "iota32": [128, 32],
    "fc1T": [1024, 512], "fc2T": [512, 1024],
    "w1Tl0": [8, 512, 512], "w2Tl0": [8, 512, 512],
    "upT": [1024, 4096], "downT": [4096, 1024],
    "WqT1": [1024, 1024], "WkT1": [1024, 1024], "WvT1": [1024, 1024],
    "WoT1": [1024, 1024],
    "wmacT": [1024, 4], "wmicT": [1024, 8], "nw": [128, 8],
    "w1Tl1": [8, 1024, 2048], "w2Tl1": [8, 2048, 1024],
}


class _Scoped:

    def __init__(self, P):
        self.P = P

    def __enter__(self):
        import contextlib
        self.outer = self.P.es
        self.inner = contextlib.ExitStack()
        self.inner.__enter__()
        self.P.es = self.inner
        return self

    def __exit__(self, *a):
        self.P.es = self.outer
        return self.inner.__exit__(*a)


def _decl_inputs(nc, names):
    hb = {}
    for n in names:
        hb[n] = nc.dram_tensor(n, _W_SPECS[n], F32, kind="ExternalInput")
    return hb


def _emit_attn_consts(P, nc, B, hb, rope):
    h_masks = nc.inline_tensor(attn_masks_np(), name=_uname("h_masks"))
    P.do("sp_dma", lambda e: e.dma_start(out=B.masks[:], in_=h_masks.ap()),
         writes=[B.masks])
    if rope:
        cosn, sinn, permn = rope_tables_np()
        hc = nc.inline_tensor(cosn, name="h_cos")
        hs = nc.inline_tensor(sinn, name="h_sin")
        hp = nc.inline_tensor(permn, name="h_perm")
        P.do("sp_dma", lambda e: e.dma_start(out=B.cosT[:], in_=hc.ap()), writes=[B.cosT])
        P.do("sp_dma", lambda e: e.dma_start(out=B.sinA[:], in_=hs.ap()), writes=[B.sinA])
        P.do("sp_dma", lambda e: e.dma_start(out=B.perm[:], in_=hp.ap()), writes=[B.perm])
    else:
        P.do("sp_dma", lambda e: e.dma_start(out=B.bias[:], in_=hb["abias"].ap()),
             writes=[B.bias])


def build_full(n_launch_phases=None):
    import contextlib
    progs = []

    def new_nc():
        return bass.Bass("TRN2", target_bir_lowering=False, debug=False)

    if FUSED:
        nc = new_nc()
        hb = _decl_inputs(nc, list(_W_SPECS))
        xin = nc.dram_tensor("xT", [1024, NTOK], F32, kind="ExternalInput")
        yout = nc.dram_tensor("yT", [1024, NTOK], F32, kind="ExternalOutput")
        xr = [nc.dram_tensor(f"xr{i}", [1024, NTOK], F32, kind="Internal")
              for i in range(5)]
        xnsp = nc.dram_tensor("xnsp", [1024, NTOK], BF16, kind="Internal")
        zT = nc.dram_tensor("zT", [1024, NTOK], F32, kind="Internal")
        znT = nc.dram_tensor("znT", [1024, NTOK], BF16, kind="Internal")
        w1bf = nc.dram_tensor("w1bf", [8, 128, 16384], BF16, kind="Internal")
        w2bf = nc.dram_tensor("w2bf", [8, 128, 16384], BF16, kind="Internal")
        upbf = nc.dram_tensor("upbf", [2, 128, 16384], BF16, kind="Internal")
        dnbf = nc.dram_tensor("dnbf", [2, 128, 16384], BF16, kind="Internal")
        w4bf = nc.dram_tensor("w4bf", [4, 128, 8192], BF16, kind="Internal")
        es = contextlib.ExitStack()
        with es:
            P = Prog(nc, es)
            C = Consts(P)
            with _Scoped(P):
                B = AttnBufs(P, False)
                _emit_attn_consts(P, nc, B, hb, False)
                emit_phase_attn(P, C, B, xin.ap(), xr[0].ap(), hb["WqT0"].ap(),
                                hb["WkT0"].ap(), hb["WvT0"].ap(), hb["WoT0"].ap())
                P.barrier()
            P.recycle_dma_sems()
            with _Scoped(P):
                B = L0FFNBufs(P)
                cb1 = sb(P, "cvt1", [128, 16384], BF16)
                cb2 = cb1
                load_l0ffn_weights(P, B, {"gateT": hb["gateT"], "fc1T": hb["fc1T"],
                                          "w1T": hb["w1Tl0"], "w2T": hb["w2Tl0"],
                                          "fc2T": hb["fc2T"], "corr4": hb["corr4"],
                                          "iota32": hb["iota32"]})
                emit_weight_convert_l1(P, cb1, cb2, hb["w1Tl1"].ap(), hb["w2Tl1"].ap(),
                                       w1bf.ap(), w2bf.ap())
                for half in range(2):
                    emit_stage_bf16(P, cb1, hb["upT"].ap()[:, half * 2048:(half + 1) * 2048],
                                    upbf.ap()[half], (upbf, half))
                    emit_stage_bf16(P, cb1, hb["downT"].ap()[half * 2048:(half + 1) * 2048, :],
                                    dnbf.ap()[half], (dnbf, half))
                for wi, wn in enumerate(("WqT1", "WkT1", "WvT1", "WoT1")):
                    emit_stage_bf16(P, cb1, hb[wn].ap(),
                                    w4bf.ap()[wi], (w4bf, wi))
                emit_phase_l0ffn_routed(P, C, B, xr[0].ap(), xr[1].ap(), xnsp.ap())
                P.barrier()
            P.recycle_dma_sems()
            for half in (0, 1):
                with _Scoped(P):
                    B = SharedBufs(P)
                    load_wT(P, B.upT, upbf.ap()[half])
                    load_wT(P, B.downT, dnbf.ap()[half])
                    emit_phase_l0ffn_shared(P, C, B, xr[1 + half].ap(),
                                            xr[2 + half].ap(), xnsp.ap(), half)
                    P.barrier()
                P.recycle_dma_sems()
            with _Scoped(P):
                B = AttnBufs(P, True)
                _emit_attn_consts(P, nc, B, hb, True)
                nw_d = sb(P, "d_nw", [128, DC], F32)
                P.do("sp_dma", lambda e: e.dma_start(out=nw_d[:], in_=hb["nw"].ap()),
                     writes=[nw_d])
                emit_phase_attn(P, C, B, xr[3].ap(), xr[4].ap(), w4bf.ap()[0],
                                w4bf.ap()[1], w4bf.ap()[2], w4bf.ap()[3],
                                zprep=(zT.ap(), znT.ap(), nw_d))
                P.barrier()
            P.recycle_dma_sems()
            with _Scoped(P):
                E = L1FFNBufs(P)
                load_wT(P, E.wmacT, hb["wmacT"].ap())
                load_wT(P, E.wmicT, hb["wmicT"].ap())
                P.do("sp_dma", lambda e: e.dma_start(out=E.nw[:], in_=hb["nw"].ap()),
                     writes=[E.nw])
                emit_phase_l1ffn_loop(P, C, E, 0, xr[4].ap(), zT.ap(), znT.ap(), None,
                                      w1bf.ap(), w2bf.ap())
                emit_phase_l1ffn_loop(P, C, E, 1, xr[4].ap(), zT.ap(), znT.ap(), yout.ap(),
                                      w1bf.ap(), w2bf.ap())
            P.finish()
        ins = list(_W_SPECS) + ["xT"]
        progs.append((nc, ins, ["yT"]))
        return progs

    def mk_io(nc, xname="xT", xdt=F32):
        xin = nc.dram_tensor(xname, [1024, NTOK], xdt, kind="ExternalInput")
        xo = nc.dram_tensor("xo", [1024, NTOK], F32, kind="ExternalOutput")
        return xin, xo

    nc = new_nc()
    hb = _decl_inputs(nc, ["WqT0", "WkT0", "WvT0", "WoT0", "abias"])
    xin, xo = mk_io(nc)
    es = contextlib.ExitStack()
    with es:
        P = Prog(nc, es)
        C = Consts(P)
        B = AttnBufs(P, False)
        _emit_attn_consts(P, nc, B, hb, False)
        emit_phase_attn(P, C, B, xin.ap(), xo.ap(), hb["WqT0"].ap(), hb["WkT0"].ap(),
                        hb["WvT0"].ap(), hb["WoT0"].ap())
        P.finish()
    progs.append((nc, ["WqT0", "WkT0", "WvT0", "WoT0", "abias", "xT"], ["xo"]))

    nc = new_nc()
    hb = _decl_inputs(nc, ["gateT", "fc1T", "w1Tl0", "w2Tl0", "fc2T", "corr4", "iota32"])
    xin, xo = mk_io(nc)
    xnsp = nc.dram_tensor("xnsp", [1024, NTOK], BF16, kind="ExternalOutput")
    es = contextlib.ExitStack()
    with es:
        P = Prog(nc, es)
        C = Consts(P)
        B = L0FFNBufs(P)
        load_l0ffn_weights(P, B, {"gateT": hb["gateT"], "fc1T": hb["fc1T"],
                                  "w1T": hb["w1Tl0"], "w2T": hb["w2Tl0"],
                                  "fc2T": hb["fc2T"], "corr4": hb["corr4"],
                                  "iota32": hb["iota32"]})
        emit_phase_l0ffn_routed(P, C, B, xin.ap(), xo.ap(), xnsp.ap())
        P.finish()
    progs.append((nc, ["gateT", "fc1T", "w1Tl0", "w2Tl0", "fc2T", "corr4", "iota32",
                       "xT"], ["xo", "xnsp"]))

    for half in (0, 1):
        nc = new_nc()
        hb = _decl_inputs(nc, ["upT", "downT"])
        xin, xo = mk_io(nc)
        xnsp = nc.dram_tensor("xnsp", [1024, NTOK], BF16, kind="ExternalInput")
        es = contextlib.ExitStack()
        with es:
            P = Prog(nc, es)
            C = Consts(P)
            B = SharedBufs(P)
            load_shared_weights(P, B, hb, half)
            emit_phase_l0ffn_shared(P, C, B, xin.ap(), xo.ap(), xnsp.ap(), half)
            P.finish()
        progs.append((nc, ["upT", "downT", "xT", "xnsp"], ["xo"]))

    nc = new_nc()
    hb = _decl_inputs(nc, ["WqT1", "WkT1", "WvT1", "WoT1"])
    xin, xo = mk_io(nc)
    es = contextlib.ExitStack()
    with es:
        P = Prog(nc, es)
        C = Consts(P)
        B = AttnBufs(P, True)
        _emit_attn_consts(P, nc, B, hb, True)
        emit_phase_attn(P, C, B, xin.ap(), xo.ap(), hb["WqT1"].ap(), hb["WkT1"].ap(),
                        hb["WvT1"].ap(), hb["WoT1"].ap())
        P.finish()
    progs.append((nc, ["WqT1", "WkT1", "WvT1", "WoT1", "xT"], ["xo"]))

    nc = new_nc()
    hb = _decl_inputs(nc, ["wmacT", "wmicT", "nw", "w1Tl1", "w2Tl1"])
    xin, xo = mk_io(nc)
    zT = nc.dram_tensor("zT", [1024, NTOK], F32, kind="Internal")
    w1bf = nc.dram_tensor("w1bf", [8, 128, 16384], BF16, kind="Internal")
    w2bf = nc.dram_tensor("w2bf", [8, 128, 16384], BF16, kind="Internal")
    es = contextlib.ExitStack()
    with es:
        P = Prog(nc, es)
        C = Consts(P)
        E = L1FFNBufs(P)
        load_wT(P, E.wmacT, hb["wmacT"].ap())
        load_wT(P, E.wmicT, hb["wmicT"].ap())
        P.do("sp_dma", lambda e: e.dma_start(out=E.nw[:], in_=hb["nw"].ap()),
             writes=[E.nw])
        emit_weight_convert_l1(P, E.w1sb, E.w2sb, hb["w1Tl1"].ap(), hb["w2Tl1"].ap(),
                               w1bf.ap(), w2bf.ap())
        znT2 = nc.dram_tensor("znT", [1024, NTOK], BF16, kind="Internal")
        emit_phase_l1ffn_loop(P, C, E, 0, xin.ap(), zT.ap(), znT2.ap(), None,
                              w1bf.ap(), w2bf.ap())
        emit_phase_l1ffn_loop(P, C, E, 1, xin.ap(), zT.ap(), znT2.ap(), xo.ap(),
                              w1bf.ap(), w2bf.ap())
        P.finish()
    progs.append((nc, ["wmacT", "wmicT", "nw", "w1Tl1", "w2Tl1", "xT"], ["xo"]))
    return progs


_CACHED = {"progs": None, "fused": None}


def kernel(**inputs):
    g = host_prep(inputs)
    x = inputs["L0_attn_x"]
    xT = [np.ascontiguousarray(x[4 * c:4 * c + 4].transpose(2, 0, 1).reshape(1024, NTOK))
          for c in range(8)]
    if _CACHED["progs"] is None or _CACHED["fused"] != FUSED:
        _CACHED["progs"] = build_full()
        _CACHED["fused"] = FUSED
    progs = _CACHED["progs"]
    state = {c: {"xT": xT[c]} for c in range(8)}
    for nc, ins, outs in progs:
        in_maps = []
        for c in range(8):
            m = {}
            for n in ins:
                if n in g:
                    m[n] = g[n]
                else:
                    m[n] = state[c][n]
            in_maps.append(m)
        res = run_bass_kernel_spmd(nc, in_maps, core_ids=list(range(8)))
        for c in range(8):
            r = res.results[c]
            if "yT" in r:
                state[c]["xT"] = r["yT"]
            elif "xo" in r:
                state[c]["xT"] = r["xo"]
            if "xnsp" in r:
                state[c]["xnsp"] = r["xnsp"]
    out = np.empty((32, 2048, 1024), dtype=np.float32)
    for c in range(8):
        out[4 * c:4 * c + 4] = np.asarray(state[c]["xT"], dtype=np.float32).reshape(
            1024, 4, 2048).transpose(1, 2, 0)
    return out


# version 34
# speedup vs baseline: 1.1566x; 1.0265x over previous
import numpy as np
import ml_dtypes
import concourse.bass as bass
import concourse.mybir as mybir
from concourse.bass_utils import run_bass_kernel_spmd

F32 = mybir.dt.float32
BF16 = mybir.dt.bfloat16
AF = mybir.ActivationFunctionType
ALU = mybir.AluOpType
AX = mybir.AxisListType
BF = ml_dtypes.bfloat16

D = 1024
S = 2048
NSEQ = 4
NTOK = NSEQ * S
NT = 512
NTILES = NTOK // NT
DC = D // 128
H = 16
HD = 64


class Prog:

    def __init__(self, nc, es, n_dma_sems=48):
        self.nc = nc
        self.es = es
        self.q_eng = {
            "pe": nc.tensor, "act": nc.scalar, "dve": nc.vector,
            "pool": nc.gpsimd, "sp": nc.sync,
            "sp_dma": nc.sync, "act_dma": nc.scalar, "pool_dma": nc.gpsimd,
        }
        self.qsem = {q: es.enter_context(nc.semaphore(f"q_{q}"))
                     for q in ("pe", "act", "dve", "pool", "sp")}
        self.cnt = {q: 0 for q in self.qsem}
        self.dma_pool = [es.enter_context(nc.semaphore(f"dsem{i}"))
                         for i in range(n_dma_sems)]
        self.dma_free = list(self.dma_pool)
        self.semval = {id(s): 0 for s in self.dma_pool}
        self.buf_sem = {}
        self.state = {}
        self.waited = {}

    def _key(self, buf):
        if isinstance(buf, tuple):
            return (id(buf[0]), buf[1])
        return id(buf)

    def _tid(self, buf):
        return id(buf[0]) if isinstance(buf, tuple) else id(buf)

    def _bufstate(self, buf):
        k = self._key(buf)
        if k not in self.state:
            self.state[k] = {"w": None, "r": {}}
        return self.state[k]

    def _dma_sem(self, buf, q):
        t = (self._tid(buf), "sw" if q == "pool_dma" else "hw")
        if t not in self.buf_sem:
            if not self.dma_free:
                raise RuntimeError("out of DMA semaphores; call recycle_dma_sems()")
            self.buf_sem[t] = self.dma_free.pop()
        return self.buf_sem[t]

    def _wait(self, q, sem, val, dep_q):
        if val <= 0:
            return
        if dep_q is None:
            val = max(val, self.semval[id(sem)])
        if self.waited.get((q, id(sem)), 0) >= val:
            return
        self.q_eng[q].wait_ge(sem, val)
        self.waited[(q, id(sem))] = val

    SAME_Q_SKIP = {"pe", "act"}

    def _dep_wait(self, q, dep):
        sem, val, dep_q = dep
        if dep_q == q and q in self.SAME_Q_SKIP:
            return
        self._wait(q, sem, val, dep_q if not (dep_q and dep_q.endswith("_dma")) else None)

    def do(self, q, fn, reads=(), writes=()):
        for b in reads:
            st = self._bufstate(b)
            if st["w"] is not None:
                self._dep_wait(q, st["w"])
        for b in writes:
            st = self._bufstate(b)
            if st["w"] is not None:
                self._dep_wait(q, st["w"])
            for dep in st["r"].values():
                self._dep_wait(q, dep)
        inst = fn(self.q_eng[q])
        if q.endswith("_dma"):
            sem = self._dma_sem(writes[0], q)
            self.semval[id(sem)] += 16
            val = self.semval[id(sem)]
            rec = (sem, val, q)
        else:
            self.cnt[q] += 1
            val = self.cnt[q]
            sem = self.qsem[q]
            rec = (sem, val, q)
        inst.then_inc(sem, 16 if q.endswith("_dma") else 1)
        for b in reads:
            st = self._bufstate(b)
            st["r"][id(sem)] = rec
        for b in writes:
            st = self._bufstate(b)
            st["w"] = rec
            st["r"] = {}
        return inst

    def barrier(self):
        for q in self.qsem:
            for q2, c in self.cnt.items():
                if q2 != q:
                    self._wait(q, self.qsem[q2], c, q2)
            for s in self.dma_pool:
                self._wait(q, s, self.semval[id(s)], None)

    def recycle_dma_sems(self):
        self.buf_sem = {}
        self.dma_free = list(self.dma_pool)

    def finish(self, q="pe"):
        for q2, c in self.cnt.items():
            if q2 != q:
                self._wait(q, self.qsem[q2], c, q2)
        for s in self.dma_pool:
            self._wait(q, s, self.semval[id(s)], None)


_NAME_SEQ = [0]


def _uname(name):
    _NAME_SEQ[0] += 1
    return f"{name}_{_NAME_SEQ[0]}"


def sb(P, name, shape, dtype):
    return P.es.enter_context(P.nc.sbuf_tensor(_uname(name), shape, dtype))


def ps(P, name, shape=(128, 512), dtype=F32):
    return P.es.enter_context(P.nc.psum_tensor(_uname(name), list(shape), dtype))


def emit_linear(P, out_psum_fn, lhsT_fn, rhs_fn, n_k, reads, tag=""):
    def fn(t):
        last = None
        for k in range(n_k):
            last = t.matmul(out_psum_fn(), lhsT_fn(k), rhs_fn(k),
                            start=(k == 0), stop=(k == n_k - 1))
        return last
    return fn


class Consts:

    def __init__(self, P):
        nc = P.nc
        self.onesc = sb(P, "c_onesc", [128, 1], BF16)
        self.onesr = sb(P, "c_onesr", [1, 128], F32)
        self.iota8 = sb(P, "c_iota8", [128, 8], F32)
        self.ident = sb(P, "c_ident", [128, 128], F32)
        h_onesc = nc.inline_tensor(np.ones((128, 1), dtype=BF), name=_uname("h_onesc"))
        h_onesr = nc.inline_tensor(np.ones((1, 128), dtype=np.float32), name=_uname("h_onesr"))
        self.onesrb = sb(P, "c_onesrb", [1, 128], BF16)
        h_onesrb = nc.inline_tensor(np.ones((1, 128), dtype=BF), name=_uname("h_onesrb"))
        h_iota8 = nc.inline_tensor(
            np.tile(np.arange(8, dtype=np.float32), (128, 1)).copy(), name=_uname("h_iota8"))
        h_ident = nc.inline_tensor(np.eye(128, dtype=np.float32), name=_uname("h_ident"))
        self.sel8 = sb(P, "c_sel8", [8, 8 * 128], BF16)
        sel_np = np.zeros((8, 8 * 128), dtype=BF)
        for e in range(8):
            sel_np[e, e * 128:(e + 1) * 128] = 1.0
        h_sel8 = nc.inline_tensor(sel_np, name=_uname("h_sel8"))
        for dst, src in ((self.onesc, h_onesc), (self.onesr, h_onesr),
                         (self.onesrb, h_onesrb),
                         (self.iota8, h_iota8), (self.ident, h_ident),
                         (self.sel8, h_sel8)):
            P.do("sp_dma", lambda e, d=dst, s=src: e.dma_start(out=d[:], in_=s.ap()),
                 writes=[d for d in (dst,)])
        P._ln_rows = sb(P, "ln_rows", [1, 3 * NT], F32)


def emit_stats_ln_ap(P, C, xb_ap, xb_keys, sq, stat_ps, brd_ps, aB, bB, eps=1e-5):

    class _W:
        def __getitem__(self, sl):
            return xb_ap[sl] if not isinstance(sl, tuple) else xb_ap[sl]

    nt = NT
    P.do("dve", lambda v: v.tensor_tensor(sq[:], xb_ap, xb_ap, ALU.mult),
         reads=xb_keys, writes=[sq])
    P.do("pe", emit_linear(P, lambda: stat_ps[0:1, 0:nt], lambda k: C.onesc[:],
                           lambda k: xb_ap[:, k * nt:(k + 1) * nt], DC, None),
         reads=xb_keys + [C.onesc], writes=[stat_ps])
    P.do("pe", emit_linear(P, lambda: stat_ps[64:65, 0:nt], lambda k: C.onesc[:],
                           lambda k: sq[:, k * nt:(k + 1) * nt], DC, None),
         reads=[sq, C.onesc], writes=[stat_ps])
    row = P._ln_rows
    m = row[0:1, 0:nt]
    v2 = row[0:1, nt:2 * nt]
    inv = row[0:1, 2 * nt:3 * nt]
    P.do("dve", lambda v: v.tensor_scalar_mul(m, stat_ps[0:1, 0:nt], 1.0 / D),
         reads=[stat_ps], writes=[row])
    P.do("dve", lambda v: v.tensor_scalar_mul(v2, stat_ps[64:65, 0:nt], 1.0 / D),
         reads=[stat_ps], writes=[row])
    P.do("dve", lambda v: v.tensor_mul(inv, m, m), reads=[row], writes=[row])
    P.do("dve", lambda v: v.tensor_sub(v2, v2, inv), reads=[row], writes=[row])
    P.do("dve", lambda v: v.tensor_scalar_add(v2, v2, eps), reads=[row], writes=[row])
    P.do("act", lambda s: s.activation(inv, v2, AF.Ln), reads=[row], writes=[row])
    P.do("act", lambda s: s.activation(inv, inv, AF.Exp, scale=-0.5),
         reads=[row], writes=[row])
    P.do("dve", lambda v: v.tensor_mul(m, m, inv), reads=[row], writes=[row])
    P.do("dve", lambda v: v.tensor_scalar_mul(m, m, -1.0), reads=[row], writes=[row])
    P.do("pe", lambda t: t.matmul(brd_ps[:, 0:nt], C.onesr[:], inv, start=True, stop=True),
         reads=[row, C.onesr], writes=[brd_ps])
    P.do("act", lambda s: s.copy(aB[:], brd_ps[:, 0:nt]), reads=[brd_ps], writes=[aB])
    P.do("pe", lambda t: t.matmul(brd_ps[:, 0:nt], C.onesr[:], m, start=True, stop=True),
         reads=[row, C.onesr], writes=[brd_ps])
    P.do("act", lambda s: s.copy(bB[:], brd_ps[:, 0:nt]), reads=[brd_ps], writes=[bB])


def emit_stats_ln(P, C, xb, sq, stat_ps, brd_ps, aB, bB, eps=1e-5, n_chunks=DC,
                  rms_only=False, denom=None):
    nt = NT
    dn = denom if denom is not None else n_chunks * 128
    P.do("dve", lambda v: v.tensor_tensor(sq[:], xb[:], xb[:], ALU.mult),
         reads=[xb], writes=[sq])
    if not rms_only:
        P.do("pe", emit_linear(P, lambda: stat_ps[0:1, 0:nt],
                               lambda k: C.onesc[:],
                               lambda k: xb[:, k * nt:(k + 1) * nt], n_chunks, None),
             reads=[xb, C.onesc], writes=[stat_ps])
    P.do("pe", emit_linear(P, lambda: stat_ps[64:65, 0:nt],
                           lambda k: C.onesc[:],
                           lambda k: sq[:, k * nt:(k + 1) * nt], n_chunks, None),
         reads=[sq, C.onesc], writes=[stat_ps])
    row = getattr(P, "_ln_rows", None)
    if row is None:
        row = sb(P, "ln_rows", [1, 3 * NT], F32)
        P._ln_rows = row
    m = row[0:1, 0:nt]
    v2 = row[0:1, nt:2 * nt]
    inv = row[0:1, 2 * nt:3 * nt]
    if rms_only:
        P.do("dve", lambda v: v.tensor_scalar(v2, stat_ps[64:65, 0:nt],
                                              1.0 / dn, eps, ALU.mult, ALU.add),
             reads=[stat_ps], writes=[row])
        P.do("act", lambda s: s.activation(inv, v2, AF.Ln), reads=[row], writes=[row])
        P.do("act", lambda s: s.activation(inv, inv, AF.Exp, scale=-0.5),
             reads=[row], writes=[row])
    else:
        P.do("dve", lambda v: v.tensor_scalar_mul(m, stat_ps[0:1, 0:nt], 1.0 / dn),
             reads=[stat_ps], writes=[row])
        P.do("dve", lambda v: v.tensor_scalar_mul(v2, stat_ps[64:65, 0:nt], 1.0 / dn),
             reads=[stat_ps], writes=[row])
        P.do("dve", lambda v: v.tensor_mul(inv, m, m), reads=[row], writes=[row])
        P.do("dve", lambda v: v.tensor_sub(v2, v2, inv), reads=[row], writes=[row])
        P.do("dve", lambda v: v.tensor_scalar_add(v2, v2, eps), reads=[row], writes=[row])
        P.do("act", lambda s: s.activation(inv, v2, AF.Ln), reads=[row], writes=[row])
        P.do("act", lambda s: s.activation(inv, inv, AF.Exp, scale=-0.5),
             reads=[row], writes=[row])
        P.do("dve", lambda v: v.tensor_mul(m, m, inv), reads=[row], writes=[row])
        P.do("dve", lambda v: v.tensor_scalar_mul(m, m, -1.0), reads=[row], writes=[row])
    P.do("pe", lambda t: t.matmul(brd_ps[:, 0:nt], C.onesr[:], inv,
                                  start=True, stop=True),
         reads=[row, C.onesr], writes=[brd_ps])
    P.do("act", lambda s: s.copy(aB[:], brd_ps[:, 0:nt]), reads=[brd_ps], writes=[aB])
    if not rms_only:
        P.do("pe", lambda t: t.matmul(brd_ps[:, 0:nt], C.onesr[:], m,
                                      start=True, stop=True),
             reads=[row, C.onesr], writes=[brd_ps])
        P.do("act", lambda s: s.copy(bB[:], brd_ps[:, 0:nt]), reads=[brd_ps], writes=[bB])


def emit_ln_apply(P, xb, aB, bB, xn, n_chunks=DC):
    for c in range(n_chunks):
        sl = slice(c * NT, (c + 1) * NT)
        P.do("dve", lambda v, sl=sl: v.tensor_tensor(xn[:, sl], xb[:, sl], aB[:], ALU.mult),
             reads=[xb, aB], writes=[xn])
        if bB is not None:
            P.do("dve", lambda v, sl=sl: v.tensor_tensor(xn[:, sl], xn[:, sl], bB[:], ALU.add),
                 reads=[xn, bB], writes=[xn])


def load_wT(P, dst_sb, hbm_ap, q="pool_dma", col0=0):
    r, c = hbm_ap.shape
    if r == 128:
        P.do("sp_dma", lambda e: e.dma_start(out=dst_sb[:, col0:col0 + c], in_=hbm_ap),
             writes=[dst_sb])
        return
    n = r // 128
    src = hbm_ap.rearrange("(n p) c -> p n c", p=128)
    dst = dst_sb[:, col0:col0 + n * c].rearrange("p (n c) -> p n c", c=c)
    P.do(q, lambda e: e.dma_start(out=dst, in_=src), writes=[dst_sb])


def emit_dma_tile(P, q, sbuf, hbm, t, write=False, subkeys=None):
    src = hbm.rearrange("(c p) n -> p c n", p=128)[:, :, t * NT:(t + 1) * NT]
    keys = [sbuf] if subkeys is None else [(sbuf, k) for k in subkeys]
    if write:
        P.do(q, lambda e: e.dma_start(out=src, in_=sbuf[:].rearrange("p (c n) -> p c n", n=NT)),
             reads=keys, writes=[(hbm.tensor, t)])
    else:
        P.do(q, lambda e: e.dma_start(out=sbuf[:].rearrange("p (c n) -> p c n", n=NT), in_=src),
             reads=[(hbm.tensor, t)], writes=keys)


class L0FFNBufs:
    def __init__(self, P):
        self.gateT = sb(P, "b_gateT", [128, DC * 8], BF16)
        self.corr = sb(P, "b_corr", [128, 32], F32)
        self.iota32 = sb(P, "b_iota32", [128, 32], F32)
        self.fc1T = sb(P, "b_fc1T", [128, DC * 512], BF16)
        self.w1T = sb(P, "b_w1T", [128, 8 * 2048], BF16)
        self.w2T = sb(P, "b_w2T", [128, 8 * 2048], BF16)
        self.fc2T = sb(P, "b_fc2T", [128, 4 * 1024], BF16)
        self.xf = [sb(P, f"b_xf{i}", [128, DC * NT], F32) for i in range(2)]
        self.aB = sb(P, "b_aB", [128, NT], BF16)
        self.bB = sb(P, "b_bB", [128, NT], BF16)
        self.xn = [sb(P, f"b_xn{i}", [128, DC * NT], BF16) for i in range(2)]
        self.s_sc = sb(P, "b_ssc", [128, 32], F32)
        self.r1 = sb(P, "b_r1", [128, 32], F32)
        self.r2 = sb(P, "b_r2", [128, 32], F32)
        self.r3 = sb(P, "b_r3", [128, 32], F32)
        self.rs = sb(P, "b_rs", [128, 6 * 4], F32)
        self.cw = sb(P, "b_cw", [128, 32], F32)
        self.cwT = sb(P, "b_cwT", [8, NT], BF16)
        self.cwB = sb(P, "b_cwB", [128, NT], BF16)
        self.latT = sb(P, "b_latT", [128, 4 * NT], BF16)
        self.hr2 = sb(P, "b_hr2", [128, 2 * NT], BF16)
        self.hs = sb(P, "b_hs", [128, 4 * NT], BF16)
        self.combT = sb(P, "b_combT", [128, 4 * NT], BF16)
        self.xo = sb(P, "b_xo", [128, DC * NT], F32)
        self.pb_comb = ps(P, "pb_comb", (128, 4 * 512))
        self.pb_h = ps(P, "pb_h01", (128, 2 * 512))
        self.pb_bc = ps(P, "pb_bc")
        self.pb_s = ps(P, "pb_s")


def load_l0ffn_weights(P, B, hb):
    load_wT(P, B.gateT, hb["gateT"].ap())
    load_wT(P, B.fc1T, hb["fc1T"].ap())
    for e in range(8):
        load_wT(P, B.w1T, hb["w1T"].ap()[e], col0=e * 2048)
        load_wT(P, B.w2T, hb["w2T"].ap()[e], col0=e * 2048)
    load_wT(P, B.fc2T, hb["fc2T"].ap())
    P.do("sp_dma", lambda e: e.dma_start(out=B.corr[:], in_=hb["corr4"].ap()),
         writes=[B.corr])
    P.do("sp_dma", lambda e: e.dma_start(out=B.iota32[:], in_=hb["iota32"].ap()),
         writes=[B.iota32])


def emit_routing_l0(P, C, B):
    v = "dve"
    sc, r1, r2, r3, cw = B.s_sc, B.r1, B.r2, B.r3, B.cw
    rs = B.rs
    def grp(buf, g):
        return buf[:].rearrange("p (tc e) -> p tc e", e=8)[:, :, g * 4:(g + 1) * 4]
    def full(buf):
        return buf[:].rearrange("p (tc e) -> p tc e", e=8)
    def col(buf, j):
        return buf[:, j:32:8]
    def rsv(i):
        return rs[:, i * 4:(i + 1) * 4]
    def rsb(i, n=8):
        return rs[:, i * 4:(i + 1) * 4].unsqueeze(2).broadcast_to([128, 4, n])

    P.do("act", lambda s: s.activation(sc[:], B.pb_s[:, 0:32], AF.Sigmoid),
         reads=[B.pb_s], writes=[sc])
    P.do(v, lambda e: e.tensor_tensor(r1[:], sc[:], B.corr[:], ALU.add),
         reads=[sc, B.corr], writes=[r1])
    for g in range(2):
        first = True
        for i in range(4):
            for j in range(i + 1, 4):
                pair = rsv(5)
                P.do(v, lambda e, i=i, j=j, g=g, pair=pair: e.tensor_tensor(
                    pair, col(r1, g * 4 + i), col(r1, g * 4 + j), ALU.add),
                    reads=[r1], writes=[rs])
                if first:
                    P.do(v, lambda e, g=g, pair=pair: e.tensor_copy(rsv(g), pair),
                         reads=[rs], writes=[rs])
                    first = False
                else:
                    P.do(v, lambda e, g=g, pair=pair: e.tensor_tensor(
                        rsv(g), rsv(g), pair, ALU.max), reads=[rs], writes=[rs])
    P.do(v, lambda e: e.tensor_tensor(rsv(2), rsv(0), rsv(1), ALU.is_ge),
         reads=[rs], writes=[rs])
    P.do(v, lambda e: e.tensor_tensor(grp(r2, 0), grp(r1, 0), rsb(2, 4), ALU.mult),
         reads=[r1, rs], writes=[r2])
    P.do(v, lambda e: e.tensor_scalar(rsv(3), rsv(2), -1.0, 1.0, ALU.mult, ALU.add),
         reads=[rs], writes=[rs])
    P.do(v, lambda e: e.tensor_tensor(grp(r2, 1), grp(r1, 1), rsb(3, 4), ALU.mult),
         reads=[r1, rs], writes=[r2])
    P.do(v, lambda e: e.tensor_reduce(rsv(0), full(r2), axis=AX.X, op=ALU.max),
         reads=[r2], writes=[rs])
    P.do(v, lambda e: e.tensor_tensor(full(r3), full(r2), rsb(0), ALU.is_ge),
         reads=[r2, rs], writes=[r3])
    P.do(v, lambda e: e.scalar_tensor_tensor(r3[:], r3[:], -999.0, B.iota32[:],
                                             ALU.mult, ALU.add),
         reads=[r3, B.iota32], writes=[r3])
    P.do(v, lambda e: e.tensor_reduce(rsv(1), full(r3), axis=AX.X, op=ALU.min),
         reads=[r3], writes=[rs])
    P.do(v, lambda e: e.tensor_scalar_add(rsv(1), rsv(1), 999.0), reads=[rs], writes=[rs])
    P.do(v, lambda e: e.tensor_tensor(full(r3), full(B.iota32), rsb(1), ALU.is_equal),
         reads=[B.iota32, rs], writes=[r3])
    P.do(v, lambda e: e.scalar_tensor_tensor(cw[:], r3[:], -1e30, r2[:], ALU.mult, ALU.add),
         reads=[r3, r2], writes=[cw])
    P.do(v, lambda e: e.tensor_reduce(rsv(2), full(cw), axis=AX.X, op=ALU.max),
         reads=[cw], writes=[rs])
    P.do(v, lambda e: e.tensor_tensor(full(r2), full(cw), rsb(2), ALU.is_ge),
         reads=[cw, rs], writes=[r2])
    P.do(v, lambda e: e.scalar_tensor_tensor(r2[:], r2[:], -999.0, B.iota32[:],
                                             ALU.mult, ALU.add),
         reads=[r2, B.iota32], writes=[r2])
    P.do(v, lambda e: e.tensor_reduce(rsv(3), full(r2), axis=AX.X, op=ALU.min),
         reads=[r2], writes=[rs])
    P.do(v, lambda e: e.tensor_scalar_add(rsv(3), rsv(3), 999.0), reads=[rs], writes=[rs])
    P.do(v, lambda e: e.tensor_tensor(full(r2), full(B.iota32), rsb(3), ALU.is_equal),
         reads=[B.iota32, rs], writes=[r2])
    P.do(v, lambda e: e.tensor_tensor(cw[:], r3[:], sc[:], ALU.mult),
         reads=[r3, sc], writes=[cw])
    P.do(v, lambda e: e.tensor_reduce(rsv(0), full(cw), axis=AX.X, op=ALU.add),
         reads=[cw], writes=[rs])
    P.do(v, lambda e: e.tensor_tensor(cw[:], r2[:], sc[:], ALU.mult),
         reads=[r2, sc], writes=[cw])
    P.do(v, lambda e: e.tensor_reduce(rsv(2), full(cw), axis=AX.X, op=ALU.add),
         reads=[cw], writes=[rs])
    P.do(v, lambda e: e.tensor_tensor(rsv(4), rsv(0), rsv(2), ALU.add),
         reads=[rs], writes=[rs])
    P.do(v, lambda e: e.tensor_scalar_add(rsv(4), rsv(4), 1e-20), reads=[rs], writes=[rs])
    P.do(v, lambda e: e.reciprocal(rsv(5), rsv(4)), reads=[rs], writes=[rs])
    P.do(v, lambda e: e.tensor_tensor(rsv(4), rsv(0), rsv(5), ALU.mult),
         reads=[rs], writes=[rs])
    P.do(v, lambda e: e.tensor_tensor(rsv(5), rsv(2), rsv(5), ALU.mult),
         reads=[rs], writes=[rs])
    P.do(v, lambda e: e.tensor_tensor(full(r3), full(r3), rsb(4), ALU.mult),
         reads=[r3, rs], writes=[r3])
    P.do(v, lambda e: e.tensor_tensor(full(r2), full(r2), rsb(5), ALU.mult),
         reads=[r2, rs], writes=[r2])
    P.do(v, lambda e: e.tensor_tensor(cw[:], r3[:], r2[:], ALU.add),
         reads=[r3, r2], writes=[cw])
    for tc in range(4):
        P.do("pe", lambda t, tc=tc: t.transpose(B.pb_s[0:8, 128:256], cw[:, tc * 8:tc * 8 + 8],
                                                C.ident[:]),
             reads=[cw, C.ident], writes=[B.pb_s])
        P.do(v, lambda e, tc=tc: e.tensor_copy(B.cwT[0:8, tc * 128:(tc + 1) * 128],
                                               B.pb_s[0:8, 128:256]),
             reads=[B.pb_s], writes=[B.cwT])


def emit_phase_l0ffn_routed(P, C, B, x_in, x_out, xn_spill, tiles=range(NTILES)):
    tiles = list(tiles)

    XOK = [B.xo] + [(B.xo, co) for co in range(8)]

    def preamble(t, s):
        xf = B.xf[t % 2]
        xn = B.xn[s]
        xb = B.xo[:].bitcast(BF16)[:, 0:DC * NT]
        emit_dma_tile(P, "sp_dma", xf, x_in, t)
        P.do("dve", lambda v, xf=xf, xb=xb: v.tensor_copy(xb, xf[:]),
             reads=[xf] + XOK, writes=XOK)
        emit_stats_ln_ap(P, C, xb, XOK, xn, B.pb_s, B.pb_bc, B.aB, B.bB)
        for c in range(DC):
            sl = slice(c * NT, (c + 1) * NT)
            P.do("dve", lambda v, sl=sl, xb=xb, xn=xn: v.tensor_tensor(
                xn[:, sl], xb[:, sl], B.aB[:], ALU.mult),
                reads=XOK + [B.aB], writes=[xn])
            P.do("dve", lambda v, sl=sl, xn=xn: v.tensor_tensor(
                xn[:, sl], xn[:, sl], B.bB[:], ALU.add),
                reads=[xn, B.bB], writes=[xn])
        emit_dma_tile(P, "pool_dma", xn, xn_spill, t, write=True)

    preamble(tiles[0], 0)
    for ti, t in enumerate(tiles):
        s = ti % 2
        xn = B.xn[s]
        xf = B.xf[t % 2]
        nxt = (tiles[ti + 1], 1 - s) if ti + 1 < len(tiles) else None

        def routing(xn=xn):
            for tc in range(4):
                P.do("pe", emit_linear(
                    P, lambda tc=tc: B.pb_s[:, tc * 8:(tc + 1) * 8],
                    lambda k, tc=tc, xn=xn: xn[:, k * NT + tc * 128:k * NT + tc * 128 + 128],
                    lambda k: B.gateT[:, k * 8:(k + 1) * 8], DC, None),
                    reads=[xn, B.gateT], writes=[B.pb_s])
            emit_routing_l0(P, C, B)
        for lc in range(4):
            pb = B.pb_h[:, (lc % 2) * 512:(lc % 2) * 512 + 512]
            hk = (B.pb_h, lc % 2)
            P.do("pe", emit_linear(
                P, lambda pb=pb: pb,
                lambda k, lc=lc: B.fc1T[:, k * 512 + lc * 128:k * 512 + lc * 128 + 128],
                lambda k, xn=xn: xn[:, k * NT:(k + 1) * NT], DC, None),
                reads=[xn, B.fc1T], writes=[hk])
            P.do("act", lambda s_, lc=lc, pb=pb: s_.copy(B.latT[:, lc * NT:(lc + 1) * NT], pb),
                 reads=[hk], writes=[B.latT])
            if lc == 0:
                routing()
        for e in range(8):
            if e == 2 and nxt is not None:
                preamble(*nxt)
            P.do("pe", lambda t_, e=e: t_.matmul(B.pb_bc[:], C.sel8[:, e * 128:(e + 1) * 128],
                                                 B.cwT[:], start=True, stop=True),
                 reads=[B.cwT, C.sel8], writes=[B.pb_bc])
            P.do("act", lambda s: s.copy(B.cwB[:], B.pb_bc[:]),
                 reads=[B.pb_bc], writes=[B.cwB])
            def he1pair(p, e=e):
                def f(t_):
                    last = None
                    for icx in range(2):
                        ic = 2 * p + icx
                        for k in range(4):
                            last = t_.matmul(
                                B.pb_h[:, icx * 512:(icx + 1) * 512],
                                B.w1T[:, e * 2048 + k * 512 + ic * 128:
                                      e * 2048 + k * 512 + ic * 128 + 128],
                                B.latT[:, k * NT:(k + 1) * NT],
                                start=(k == 0), stop=(k == 3))
                    return last
                P.do("pe", f, reads=[B.latT, B.w1T],
                     writes=[(B.pb_h, 0), (B.pb_h, 1)])
                P.do("act", lambda s: s.activation(B.hr2[:], B.pb_h[:], AF.Relu),
                     reads=[(B.pb_h, 0), (B.pb_h, 1)], writes=[B.hr2])
                sl = slice(2 * p * NT, (2 * p + 2) * NT)
                P.do("dve", lambda v, sl=sl: v.tensor_tensor(
                    B.hs[:, sl], B.hr2[:], B.hr2[:], ALU.mult),
                    reads=[B.hr2], writes=[(B.hs, p)])
                P.do("dve", lambda v, sl=sl: v.tensor_tensor(
                    B.hs[:, sl].rearrange("p (a n) -> p a n", n=NT),
                    B.hs[:, sl].rearrange("p (a n) -> p a n", n=NT),
                    B.cwB[:].unsqueeze(1).broadcast_to([128, 2, NT]), ALU.mult),
                    reads=[(B.hs, p), B.cwB], writes=[(B.hs, p)])

            def he2pair(p, e=e):
                def f(t_):
                    last = None
                    for icx in range(2):
                        ic = 2 * p + icx
                        for lc in range(4):
                            last = t_.matmul(
                                B.pb_comb[:, lc * 512:(lc + 1) * 512],
                                B.w2T[:, e * 2048 + ic * 512 + lc * 128:
                                      e * 2048 + ic * 512 + lc * 128 + 128],
                                B.hs[:, ic * NT:(ic + 1) * NT],
                                start=(e == 0 and ic == 0), stop=(e == 7 and ic == 3))
                    return last
                P.do("pe", f, reads=[(B.hs, p), B.w2T], writes=[B.pb_comb])

            he1pair(0)
            he1pair(1)
            he2pair(0)
            he2pair(1)
        for lc in range(4):
            P.do("act", lambda s, lc=lc: s.copy(B.combT[:, lc * NT:(lc + 1) * NT],
                                                B.pb_comb[:, lc * 512:(lc + 1) * 512]),
                 reads=[B.pb_comb], writes=[B.combT])
        for co in range(8):
            pb = B.pb_h[:, (co % 2) * 512:(co % 2) * 512 + 512]
            hk = (B.pb_h, co % 2)
            P.do("pe", emit_linear(
                P, lambda pb=pb: pb,
                lambda k, co=co: B.fc2T[:, k * 1024 + co * 128:k * 1024 + co * 128 + 128],
                lambda k: B.combT[:, k * NT:(k + 1) * NT], 4, None),
                reads=[B.combT, B.fc2T], writes=[hk])
            P.do("dve", lambda v, co=co, pb=pb, xf=xf: v.tensor_tensor(
                B.xo[:, co * NT:(co + 1) * NT], pb, xf[:, co * NT:(co + 1) * NT], ALU.add),
                reads=[hk, xf], writes=[B.xo])
        emit_dma_tile(P, "sp_dma", B.xo, x_out, t, write=True)


class SharedBufs:
    def __init__(self, P):
        self.upT = sb(P, "s_upT", [128, DC * 2048], BF16)
        self.downT = sb(P, "s_downT", [128, 16 * 1024], BF16)
        self.xn = [sb(P, f"s_xn{i}", [128, DC * NT], BF16) for i in range(2)]
        self.xf = [sb(P, f"s_xf{i}", [128, DC * NT], F32) for i in range(2)]
        self.ur = sb(P, "s_ur", [128, NT], BF16)
        self.us = sb(P, "s_us", [128, 16 * NT], BF16)
        self.xo = sb(P, "s_xo", [128, DC * NT], F32)
        self.pb = [ps(P, f"s_pb{i}") for i in range(4)]


def load_shared_weights(P, B, hb, half):
    load_wT(P, B.upT, hb["upT"].ap()[:, half * 2048:(half + 1) * 2048])
    load_wT(P, B.downT, hb["downT"].ap()[half * 2048:(half + 1) * 2048, :])


def emit_phase_l0ffn_shared(P, C, B, x_in, x_out, xn_spill, half, tiles=range(NTILES)):
    for t in tiles:
        xn = B.xn[t % 2]
        xf = B.xf[t % 2]
        emit_dma_tile(P, "pool_dma", xn, xn_spill, t)
        emit_dma_tile(P, "sp_dma", xf, x_in, t)
        for fc in range(16):
            pb = B.pb[fc % 2]
            P.do("pe", emit_linear(
                P, lambda pb=pb: pb[:],
                lambda k, fc=fc: B.upT[:, k * 2048 + fc * 128:k * 2048 + fc * 128 + 128],
                lambda k, xn=xn: xn[:, k * NT:(k + 1) * NT], DC, None),
                reads=[xn, B.upT], writes=[pb])
            P.do("act", lambda s, pb=pb: s.activation(B.ur[:], pb[:], AF.Relu),
                 reads=[pb], writes=[B.ur])
            P.do("dve", lambda v, fc=fc: v.tensor_tensor(
                B.us[:, fc * NT:(fc + 1) * NT], B.ur[:], B.ur[:], ALU.mult),
                reads=[B.ur], writes=[B.us])
        for co in range(8):
            pb = B.pb[2 + co % 2]
            P.do("pe", emit_linear(
                P, lambda pb=pb: pb[:],
                lambda k, co=co: B.downT[:, k * 1024 + co * 128:k * 1024 + co * 128 + 128],
                lambda k: B.us[:, k * NT:(k + 1) * NT], 16, None),
                reads=[B.us, B.downT], writes=[pb])
            P.do("dve", lambda v, co=co, pb=pb, xf=xf: v.tensor_tensor(
                B.xo[:, co * NT:(co + 1) * NT], pb[:], xf[:, co * NT:(co + 1) * NT], ALU.add),
                reads=[pb, xf], writes=[B.xo])
        emit_dma_tile(P, "sp_dma", B.xo, x_out, t, write=True)


class AttnBufs:
    def __init__(self, P, rope: bool):
        self.Wa = sb(P, "a_Wa", [128, DC * 1024], BF16)
        self.Wb = sb(P, "a_Wb", [128, DC * 1024], BF16)
        self.xf = [sb(P, f"a_xf{i}", [128, DC * NT], F32) for i in range(2)]
        self.aB = sb(P, "a_aB", [128, NT], BF16)
        self.bB = sb(P, "a_bB", [128, NT], BF16)
        self.xn = sb(P, "a_xn", [128, DC * NT], BF16)
        self.KT = sb(P, "a_KT", [128, DC * S], BF16)
        self.Vaug = sb(P, "a_Vaug", [128, 16 * (16 * 65)], BF16)
        self.QT = sb(P, "a_QT", [128, DC * NT], BF16)
        self.ET = [sb(P, f"a_ET{i}", [128, 4 * NT], BF16) for i in range(2)]
        self.ctxT = sb(P, "a_ctxT", [128, DC * NT], BF16)
        self.rrow = sb(P, "a_rrow", [1, NT], F32)
        self.rBs = sb(P, "a_rBs", [64, NT], F32)
        self.masks = sb(P, "a_masks", [128, 4 * NT], BF16)
        self.xo = sb(P, "a_xo", [128, DC * NT], F32)
        self.rope = rope
        if rope:
            self.cosT = sb(P, "a_cosT", [128, S], BF16)
            self.sinA = sb(P, "a_sinA", [128, S], BF16)
            self.perm = sb(P, "a_perm", [128, 128], BF16)
            self.rtmp = [sb(P, f"a_rtmp{i}", [128, NT], BF16) for i in range(2)]
        else:
            self.bias = sb(P, "a_bias", [128, 4 * DC], F32)
        self.pb_q = ps(P, "a_pbq", (128, 4 * 512))
        self.pb_c = [ps(P, f"a_pbc{i}") for i in range(2)]
        self.pb_st = ps(P, "a_pbst")
        self.pb_bc = ps(P, "a_pbbc")


def attn_masks_np():
    m = np.zeros((128, 4 * NT), dtype=BF)
    for j in range(4):
        kk = np.arange(128)[:, None]
        qq = np.arange(NT)[None, :]
        m[:, j * NT:(j + 1) * NT] = (kk <= qq - 128 * j).astype(np.float32)
    return m


def rope_tables_np():
    inv = 1.0 / (10000.0 ** (np.arange(0, HD, 2, dtype=np.float64) / HD))
    t = np.arange(S, dtype=np.float64)
    fr = np.outer(inv, t)
    cos = np.cos(fr)
    sin = np.sin(fr)
    cosT = np.concatenate([cos, cos], 0)
    sinA = np.concatenate([-sin, sin], 0)
    cos128 = np.concatenate([cosT, cosT], 0).astype(np.float32)
    sin128 = np.concatenate([sinA, sinA], 0).astype(np.float32)
    perm = np.zeros((128, 128), dtype=np.float32)
    for b in range(2):
        for j in range(32):
            perm[b * 64 + j + 32, b * 64 + j] = 1.0
            perm[b * 64 + j, b * 64 + j + 32] = 1.0
    return cos128.astype(BF), sin128.astype(BF), perm.astype(BF)


def emit_rope(P, B, dst, dst_cols, pos0, pb, pbkey=None, par=0):
    if pbkey is None:
        pbkey = pb
    rt = B.rtmp[par % 2]
    sl = slice(dst_cols, dst_cols + NT)
    tsl = slice(pos0, pos0 + NT)
    P.do("pe", lambda t: t.matmul(pb[:], B.perm[:], dst[:, sl], start=True, stop=True),
         reads=[dst, B.perm], writes=[pbkey])
    P.do("dve", lambda v: v.tensor_tensor(rt[:], pb[:], B.sinA[:, tsl], ALU.mult),
         reads=[pbkey, B.sinA], writes=[rt])
    P.do("dve", lambda v: v.tensor_tensor(dst[:, sl], dst[:, sl], B.cosT[:, tsl], ALU.mult),
         reads=[dst, B.cosT], writes=[dst])
    P.do("dve", lambda v: v.tensor_tensor(dst[:, sl], dst[:, sl], rt[:], ALU.add),
         reads=[dst, rt], writes=[dst])


def emit_proj(P, B, W, xn, out_sb, co_range=range(DC), bias=None, bias_idx=0,
              evac_dtype_note="bf16"):
    for co in co_range:
        pb = B.pb_q[:, (co % 4) * 512:(co % 4) * 512 + 512]
        qk = (B.pb_q, co % 4)
        P.do("pe", emit_linear(
            P, lambda pb=pb: pb,
            lambda k, co=co, W=W: W[:, k * 1024 + co * 128:k * 1024 + co * 128 + 128],
            lambda k, xn=xn: xn[:, k * NT:(k + 1) * NT], DC, None),
            reads=[(xn, c2) for c2 in range(DC)] + [W], writes=[qk])
        if bias is not None:
            P.do("act", lambda s, co=co, pb=pb: s.activation(
                out_sb[:, co * NT:(co + 1) * NT], pb, AF.Identity,
                bias=B.bias[:, bias_idx * DC + co:bias_idx * DC + co + 1]),
                reads=[qk, B.bias], writes=[out_sb])
        else:
            P.do("act", lambda s, co=co, pb=pb: s.copy(
                out_sb[:, co * NT:(co + 1) * NT], pb),
                reads=[qk], writes=[out_sb])


def emit_ln_tile(P, C, B, xf, src_keys=None):
    nt = NT
    if src_keys is None:
        src_keys = [xf]
    ALLXN = [(B.xn, c) for c in range(DC)]
    P.do("dve", lambda v: v.tensor_copy(B.xn[:], xf[:]), reads=src_keys, writes=ALLXN)
    P.do("pe", emit_linear(P, lambda: B.pb_st[0:1, 0:nt], lambda k: C.onesc[:],
                           lambda k: B.xn[:, k * nt:(k + 1) * nt], DC, None),
         reads=ALLXN + [C.onesc], writes=[B.pb_st])
    P.do("dve", lambda v: v.tensor_tensor(B.xn[:], B.xn[:], B.xn[:], ALU.mult),
         reads=ALLXN, writes=ALLXN)
    P.do("pe", emit_linear(P, lambda: B.pb_st[64:65, 0:nt], lambda k: C.onesc[:],
                           lambda k: B.xn[:, k * nt:(k + 1) * nt], DC, None),
         reads=ALLXN + [C.onesc], writes=[B.pb_st])
    row = P._ln_rows
    m = row[0:1, 0:nt]
    v2 = row[0:1, nt:2 * nt]
    inv = row[0:1, 2 * nt:3 * nt]
    P.do("dve", lambda v: v.tensor_scalar_mul(m, B.pb_st[0:1, 0:nt], 1.0 / D),
         reads=[B.pb_st], writes=[row])
    P.do("dve", lambda v: v.tensor_scalar_mul(v2, B.pb_st[64:65, 0:nt], 1.0 / D),
         reads=[B.pb_st], writes=[row])
    P.do("dve", lambda v: v.tensor_mul(inv, m, m), reads=[row], writes=[row])
    P.do("dve", lambda v: v.tensor_sub(v2, v2, inv), reads=[row], writes=[row])
    P.do("dve", lambda v: v.tensor_scalar_add(v2, v2, 1e-5), reads=[row], writes=[row])
    P.do("act", lambda s: s.activation(inv, v2, AF.Ln), reads=[row], writes=[row])
    P.do("act", lambda s: s.activation(inv, inv, AF.Exp, scale=-0.5),
         reads=[row], writes=[row])
    P.do("dve", lambda v: v.tensor_mul(m, m, inv), reads=[row], writes=[row])
    P.do("dve", lambda v: v.tensor_scalar_mul(m, m, -1.0), reads=[row], writes=[row])
    P.do("pe", lambda t: t.matmul(B.pb_bc[:, 0:nt], C.onesr[:], inv, start=True, stop=True),
         reads=[row, C.onesr], writes=[B.pb_bc])
    P.do("act", lambda s: s.copy(B.aB[:], B.pb_bc[:, 0:nt]), reads=[B.pb_bc], writes=[B.aB])
    P.do("pe", lambda t: t.matmul(B.pb_bc[:, 0:nt], C.onesr[:], m, start=True, stop=True),
         reads=[row, C.onesr], writes=[B.pb_bc])
    P.do("act", lambda s: s.copy(B.bB[:], B.pb_bc[:, 0:nt]), reads=[B.pb_bc], writes=[B.bB])
    for c in range(DC):
        sl = slice(c * NT, (c + 1) * NT)
        P.do("dve", lambda v, sl=sl: v.tensor_tensor(B.xn[:, sl], xf[:, sl], B.aB[:],
                                                     ALU.mult),
             reads=src_keys + [B.aB], writes=[(B.xn, c)])
        P.do("dve", lambda v, sl=sl: v.tensor_tensor(B.xn[:, sl], B.xn[:, sl], B.bB[:],
                                                     ALU.add),
             reads=[(B.xn, c), B.bB], writes=[(B.xn, c)])


def emit_attn_zprep(P, C, B, t, i, zT, znT, nw):
    nt = NT
    xo = B.xo
    xokeys = [(B.xo, c) for c in range(DC)]
    xf = B.xf[i % 2]
    emit_ln_tile(P, C, B, xo, src_keys=xokeys)
    row = P._ln_rows
    inv = row[0:1, 2 * nt:3 * nt]
    r2 = row[0:1, nt:2 * nt]
    P.do("dve", lambda v: v.tensor_mul(r2, inv, inv), reads=[row], writes=[row])
    P.do("dve", lambda v: v.tensor_scalar(r2, r2, -1e-5, 1.0 + 1e-6, ALU.mult, ALU.add),
         reads=[row], writes=[row])
    P.do("act", lambda s: s.activation(r2, r2, AF.Ln), reads=[row], writes=[row])
    P.do("act", lambda s: s.activation(r2, r2, AF.Exp, scale=-0.5), reads=[row], writes=[row])
    for c in range(DC):
        sl = slice(c * NT, (c + 1) * NT)
        P.do("dve", lambda v, sl=sl: v.tensor_tensor(xf[:, sl], xo[:, sl], B.aB[:], ALU.mult),
             reads=[(B.xo, c), B.aB], writes=[xf])
        P.do("dve", lambda v, sl=sl: v.tensor_tensor(xf[:, sl], xf[:, sl], B.bB[:], ALU.add),
             reads=[xf, B.bB], writes=[xf])
    emit_dma_tile(P, "pool_dma", xf, zT, t, write=True)
    P.do("pe", lambda t_: t_.matmul(B.pb_bc[:, 0:nt], C.onesr[:], r2, start=True, stop=True),
         reads=[row, C.onesr], writes=[B.pb_bc])
    P.do("act", lambda s: s.copy(B.aB[:], B.pb_bc[:, 0:nt]), reads=[B.pb_bc], writes=[B.aB])
    for c in range(DC):
        sl = slice(c * NT, (c + 1) * NT)
        P.do("dve", lambda v, sl=sl: v.tensor_tensor(B.xn[:, sl], B.xn[:, sl], B.aB[:],
                                                     ALU.mult),
             reads=[(B.xn, c), B.aB], writes=[(B.xn, c)])
        P.do("act", lambda s, c=c, sl=sl: s.activation(B.xn[:, sl], B.xn[:, sl],
             AF.Identity, scale=nw[:, c:c + 1]),
             reads=[(B.xn, c), nw], writes=[(B.xn, c)])
    P.do("sp_dma", lambda e, t=t: e.dma_start(
        out=znT.rearrange("(c p) n -> p c n", p=128)[:, :, t * NT:(t + 1) * NT],
        in_=B.xn[:].rearrange("p (c n) -> p c n", n=NT)),
        reads=[(B.xn, c) for c in range(DC)], writes=[(znT.tensor, t)])


def emit_phase_attn(P, C, B, x_in, x_out, WqT, WkT, WvT, WoT, tiles_per_seq=4, seqs=None,
                    zprep=None):
    rope = B.rope
    P.do("dve", lambda v: v.memset(
        B.Vaug[:].rearrange("p (g x) -> p g x", x=16 * 65)[:, :, 64:16 * 65:65], 1.0),
        writes=[B.Vaug])
    if seqs is None:
        seqs = range(NSEQ)
    for s in seqs:
        load_wT(P, B.Wa, WkT)
        load_wT(P, B.Wb, WvT)
        for i in range(tiles_per_seq):
            t = s * tiles_per_seq + i
            xf = B.xf[i % 2]
            emit_dma_tile(P, "sp_dma", xf, x_in, t)
            emit_ln_tile(P, C, B, xf)
            for co in range(DC):
                pb = B.pb_q[:, (co % 4) * 512:(co % 4) * 512 + 512]
                qk = (B.pb_q, co % 4)
                P.do("pe", emit_linear(
                    P, lambda pb=pb: pb,
                    lambda k, co=co: B.Wa[:, k * 1024 + co * 128:k * 1024 + co * 128 + 128],
                    lambda k: B.xn[:, k * NT:(k + 1) * NT], DC, None),
                    reads=[(B.xn, c2) for c2 in range(DC)] + [B.Wa], writes=[qk])
                if rope:
                    P.do("act", lambda sc, co=co, pb=pb, i=i: sc.copy(
                        B.KT[:, co * S + i * NT:co * S + (i + 1) * NT], pb),
                        reads=[qk], writes=[B.KT])
                else:
                    P.do("act", lambda sc, co=co, pb=pb, i=i: sc.activation(
                        B.KT[:, co * S + i * NT:co * S + (i + 1) * NT], pb, AF.Identity,
                        bias=B.bias[:, 1 * DC + co:1 * DC + co + 1]),
                        reads=[qk, B.bias], writes=[B.KT])
            if rope:
                for co in range(DC):
                    emit_rope(P, B, B.KT, co * S + i * NT, i * NT,
                              B.pb_q[:, (co % 4) * 512:(co % 4) * 512 + 512],
                              (B.pb_q, co % 4), par=co)
            for tc in range(4):
                for dvh in range(2):
                    qd = (tc * 2 + dvh) % 4
                    pb = B.pb_q[:, qd * 512:qd * 512 + 512]
                    qk = (B.pb_q, qd)
                    P.do("pe", emit_linear(
                        P, lambda pb=pb: pb,
                        lambda k, tc=tc: B.xn[:, k * NT + tc * 128:k * NT + tc * 128 + 128],
                        lambda k, dvh=dvh: B.Wb[:, k * 1024 + dvh * 512:k * 1024 + (dvh + 1) * 512],
                        DC, None),
                        reads=[(B.xn, c2) for c2 in range(DC)] + [B.Wb], writes=[qk])
                    gtc = i * 4 + tc
                    h0 = dvh * 8
                    dst = B.Vaug[:, gtc * 1040 + h0 * 65:gtc * 1040 + h0 * 65 + 65 * 8]\
                        .rearrange("p (hh x) -> p hh x", x=65)[:, :, 0:64]
                    P.do("act", lambda sc, pb=pb, dst=dst: sc.copy(
                        dst, pb.rearrange("p (hh x) -> p hh x", x=64)),
                        reads=[qk], writes=[B.Vaug])
        load_wT(P, B.Wa, WqT)
        load_wT(P, B.Wb, WoT)
        for i in range(tiles_per_seq):
            t = s * tiles_per_seq + i
            xf = B.xf[i % 2]
            emit_dma_tile(P, "sp_dma", xf, x_in, t)
            emit_ln_tile(P, C, B, xf)
            if rope:
                emit_proj(P, B, B.Wa, B.xn, B.QT)
                for co in range(DC):
                    emit_rope(P, B, B.QT, co * NT, i * NT,
                              B.pb_q[:, (co % 4) * 512:(co % 4) * 512 + 512],
                              (B.pb_q, co % 4), par=co)
            else:
                emit_proj(P, B, B.Wa, B.xn, B.QT, bias=True, bias_idx=0)
            kc = 4 * i + 4

            def normalize_head(h):
                hrow = (h % 2) * 64
                P.do("act", lambda s_, h=h: s_.activation(
                    B.rBs[0:1, :], B.pb_c[h % 2][64:65, :], AF.Ln),
                    reads=[B.pb_c[h % 2]], writes=[B.rBs])
                P.do("act", lambda s_: s_.activation(
                    B.rrow[:], B.rBs[0:1, :], AF.Exp, scale=-1.0),
                    reads=[B.rBs], writes=[B.rrow])
                P.do("pe", lambda t_, h=h: t_.matmul(B.pb_bc[0:64, :], C.onesr[0:1, 0:64],
                                                     B.rrow[:], start=True, stop=True),
                     reads=[B.rrow, C.onesr], writes=[B.pb_bc])
                P.do("act", lambda sc: sc.copy(B.rBs[:], B.pb_bc[0:64, :]),
                     reads=[B.pb_bc], writes=[B.rBs])
                P.do("dve", lambda v, h=h, hrow=hrow: v.tensor_tensor(
                    B.ctxT[hrow:hrow + 64, (h // 2) * NT:(h // 2 + 1) * NT],
                    B.pb_c[h % 2][0:64, :], B.rBs[:], ALU.mult),
                    reads=[B.pb_c[h % 2], B.rBs], writes=[(B.ctxT, h)])
                if not rope:
                    P.do("act", lambda sc, h=h, hrow=hrow: sc.activation(
                        B.ctxT[hrow:hrow + 64, (h // 2) * NT:(h // 2 + 1) * NT],
                        B.ctxT[hrow:hrow + 64, (h // 2) * NT:(h // 2 + 1) * NT],
                        AF.Identity,
                        bias=B.bias[hrow:hrow + 64, 2 * DC + h // 2:2 * DC + h // 2 + 1]),
                        reads=[(B.ctxT, h), B.bias], writes=[(B.ctxT, h)])

            ng2 = kc // 2
            for hp in range(H // 2):
                h0, h1 = 2 * hp, 2 * hp + 1

                def sg(h, w, g):
                    hrow = (h % 2) * 64
                    def f(t_):
                        last = None
                        for jj in range(2):
                            j = 2 * g + jj
                            last = t_.matmul(
                                B.pb_q[:, (2 * w + jj) * 512:(2 * w + jj) * 512 + 512],
                                B.KT[hrow:hrow + 64,
                                     (h // 2) * S + j * 128:(h // 2) * S + j * 128 + 128],
                                B.QT[hrow:hrow + 64, (h // 2) * NT:(h // 2 + 1) * NT],
                                start=True, stop=True)
                        return last
                    return f

                def eg(w, g):
                    et = B.ET[w][:, (g % 2) * 1024:(g % 2) * 1024 + 1024]
                    P.do("act", lambda sc, et=et, w=w: sc.activation(
                        et, B.pb_q[:, 2 * w * 512:2 * w * 512 + 1024],
                        AF.Exp, bias=0.0, scale=0.125),
                        reads=[(B.pb_q, 2 * w), (B.pb_q, 2 * w + 1)],
                        writes=[(B.ET[w], g % 2)])
                    j0 = 2 * g
                    if j0 >= 4 * i:
                        m = j0 - 4 * i
                        P.do("dve", lambda v, et=et, m=m: v.tensor_tensor(
                            et, et, B.masks[:, m * NT:(m + 2) * NT], ALU.mult),
                            reads=[(B.ET[w], g % 2), B.masks], writes=[(B.ET[w], g % 2)])

                def cg(h, w, g):
                    def f(t_):
                        last = None
                        for jj in range(2):
                            j = 2 * g + jj
                            last = t_.matmul(
                                B.pb_c[w][0:65, :],
                                B.Vaug[:, j * 1040 + h * 65:j * 1040 + (h + 1) * 65],
                                B.ET[w][:, ((g % 2) * 2 + jj) * 512:((g % 2) * 2 + jj) * 512 + 512],
                                start=(j == 0), stop=(j == kc - 1))
                        return last
                    return f
                P.do("pe", sg(h0, 0, 0), reads=[B.KT, B.QT],
                     writes=[(B.pb_q, 0), (B.pb_q, 1)])
                P.do("pe", sg(h1, 1, 0), reads=[B.KT, B.QT],
                     writes=[(B.pb_q, 2), (B.pb_q, 3)])
                if hp > 0:
                    normalize_head(2 * hp - 2)
                    normalize_head(2 * hp - 1)
                eg(0, 0)
                eg(1, 0)
                for g in range(1, ng2):
                    P.do("pe", sg(h0, 0, g), reads=[B.KT, B.QT],
                         writes=[(B.pb_q, 0), (B.pb_q, 1)])
                    P.do("pe", sg(h1, 1, g), reads=[B.KT, B.QT],
                         writes=[(B.pb_q, 2), (B.pb_q, 3)])
                    eg(0, g)
                    eg(1, g)
                    P.do("pe", cg(h0, 0, g - 1), reads=[B.Vaug, (B.ET[0], (g - 1) % 2)],
                         writes=[B.pb_c[0]])
                    P.do("pe", cg(h1, 1, g - 1), reads=[B.Vaug, (B.ET[1], (g - 1) % 2)],
                         writes=[B.pb_c[1]])
                P.do("pe", cg(h0, 0, ng2 - 1), reads=[B.Vaug, (B.ET[0], (ng2 - 1) % 2)],
                     writes=[B.pb_c[0]])
                P.do("pe", cg(h1, 1, ng2 - 1), reads=[B.Vaug, (B.ET[1], (ng2 - 1) % 2)],
                     writes=[B.pb_c[1]])
            normalize_head(H - 2)
            normalize_head(H - 1)
            for co in range(DC):
                pb = B.pb_q[:, (co % 4) * 512:(co % 4) * 512 + 512]
                qk = (B.pb_q, co % 4)
                P.do("pe", emit_linear(
                    P, lambda pb=pb: pb,
                    lambda k, co=co: B.Wb[:, k * 1024 + co * 128:k * 1024 + co * 128 + 128],
                    lambda k: B.ctxT[:, k * NT:(k + 1) * NT], DC, None),
                    reads=[(B.ctxT, hh) for hh in range(H)] + [B.Wb], writes=[qk])
                bias_s = 0.0 if rope else B.bias[:, 3 * DC + co:3 * DC + co + 1]
                brd = [] if rope else [B.bias]
                P.do("dve", lambda v, co=co, pb=pb, xf=xf, bias_s=bias_s:
                     v.scalar_tensor_tensor(
                         B.xo[:, co * NT:(co + 1) * NT], pb, bias_s,
                         xf[:, co * NT:(co + 1) * NT], ALU.add, ALU.add),
                     reads=[qk, xf] + brd, writes=[(B.xo, co)])
            emit_dma_tile(P, "sp_dma", B.xo, x_out, t, write=True, subkeys=range(8))
            if zprep is not None:
                emit_attn_zprep(P, C, B, t, i, zprep[0], zprep[1], zprep[2])


class L1FFNBufs:
    def __init__(self, P):
        self.wmacT = sb(P, "e_wmacT", [128, DC * 4], BF16)
        self.wmicT = sb(P, "e_wmicT", [128, DC * 8], BF16)
        self.nw = sb(P, "e_nw", [128, DC], F32)
        self.w1sb = sb(P, "e_w1", [128, 8 * 2048], BF16)
        self.w2sb = sb(P, "e_w2", [128, 16 * 1024], BF16)
        self.z = [sb(P, f"e_z{i}", [128, DC * NT], F32) for i in range(2)]
        self.zb = [sb(P, f"e_zb{i}", [128, DC * NT], BF16) for i in range(2)]
        self.zn = [sb(P, f"e_zn{i}", [128, DC * NT], BF16) for i in range(2)]
        self.y = [sb(P, f"e_y{i}", [128, DC * NT], BF16) for i in range(2)]
        self.hs = sb(P, "e_hs", [128, 16 * NT], BF16)
        self.xr = sb(P, "e_xr", [128, DC * NT], F32)
        self.aB = sb(P, "e_aB", [128, NT], BF16)
        self.bB2 = sb(P, "e_bB2", [128, NT], BF16)
        self.sm = sb(P, "e_sm", [128, 16], F32)
        self.mi = sb(P, "e_mi", [128, 32], F32)
        self.r1 = sb(P, "e_r1", [128, 16], F32)
        self.r2 = sb(P, "e_r2", [128, 16], F32)
        self.r3 = sb(P, "e_r3", [128, 16], F32)
        self.r4 = sb(P, "e_r4", [128, 16], F32)
        self.rs = sb(P, "e_rs", [128, 4 * 4], F32)
        self.rows = sb(P, "e_rows", [1, 6 * NT], F32)
        self.cw = sb(P, "e_cw", [128, 32], F32)
        self.cwT = [sb(P, f"e_cwT{i}", [8, NT], BF16) for i in range(2)]
        self.cwB = sb(P, "e_cwB", [128, NT], BF16)
        self.pb_h = [ps(P, f"e_pbh{i}") for i in range(2)]
        self.pb_y = [ps(P, f"e_pby{i}") for i in range(2)]
        self.pb_r = ps(P, "e_pbr")
        self.pb_bc = ps(P, "e_pbbc")
        self.pb_bc2 = ps(P, "e_pbbc2")
        self.pb_t = ps(P, "e_pbt")


def emit_stage_bf16(P, buf, src_ap, dst_ap, key):
    load_wT(P, buf, src_ap)
    w = dst_ap.shape[-1]
    P.do("sp_dma", lambda e: e.dma_start(out=dst_ap, in_=buf[:, 0:w]),
         reads=[buf], writes=[key])


def emit_weight_convert_l1(P, buf1, buf2, w1T_hbm, w2T_hbm, w1bf, w2bf):
    for e in range(8):
        b = buf1 if e % 2 == 0 else buf2
        for h in range(2):
            load_wT(P, b, w1T_hbm[e][:, h * 1024:(h + 1) * 1024])
            P.do("sp_dma", lambda en, e=e, b=b, h=h: en.dma_start(
                out=w1bf[e].rearrange("p (k f) -> p k f", f=2048)[:, :, h * 1024:(h + 1) * 1024],
                in_=b[:, 0:8192].rearrange("p (k f) -> p k f", f=1024)),
                reads=[b], writes=[(w1bf.tensor, e)])
            load_wT(P, b, w2T_hbm[e][h * 1024:(h + 1) * 1024, :])
            P.do("act_dma", lambda en, e=e, b=b, h=h: en.dma_start(
                out=w2bf[e].rearrange("p (k f) -> p k f", f=1024)[:, h * 8:(h + 1) * 8, :],
                in_=b[:, 0:8192].rearrange("p (k f) -> p k f", f=1024)),
                reads=[b], writes=[(w2bf.tensor, e)])


def emit_routing_l1(P, C, E, cwT):
    v = "dve"
    sm, r1, r2, r3, r4, rs, cw = E.sm, E.r1, E.r2, E.r3, E.r4, E.rs, E.cw

    def v4(buf):
        return buf[:].rearrange("p (tc e) -> p tc e", e=4)

    def v8(buf):
        return buf[:].rearrange("p (tc e) -> p tc e", e=8)

    def rsv(i):
        return rs[:, i * 4:(i + 1) * 4]

    def rsb(i, n=4):
        return rs[:, i * 4:(i + 1) * 4].unsqueeze(2).broadcast_to([128, 4, n])

    mac = E.pb_r[:, 0:16].rearrange("p (tc e) -> p tc e", e=4)
    iota4b = C.iota8[:, 0:4].unsqueeze(1).broadcast_to([128, 4, 4])
    P.do(v, lambda e: e.tensor_reduce(rsv(0), mac, axis=AX.X, op=ALU.max),
         reads=[E.pb_r], writes=[rs])
    P.do(v, lambda e: e.tensor_tensor(v4(sm), mac, rsb(0), ALU.subtract),
         reads=[E.pb_r, rs], writes=[sm])
    P.do("act", lambda s: s.activation(sm[:], sm[:], AF.Exp), reads=[sm], writes=[sm])
    P.do(v, lambda e: e.tensor_reduce(rsv(0), v4(sm), axis=AX.X, op=ALU.add),
         reads=[sm], writes=[rs])
    P.do(v, lambda e: e.reciprocal(rsv(1), rsv(0)), reads=[rs], writes=[rs])
    P.do(v, lambda e: e.tensor_tensor(v4(sm), v4(sm), rsb(1), ALU.mult),
         reads=[sm, rs], writes=[sm])
    P.do(v, lambda e: e.tensor_reduce(rsv(0), v4(sm), axis=AX.X, op=ALU.max),
         reads=[sm], writes=[rs])
    P.do(v, lambda e: e.tensor_tensor(v4(r1), v4(sm), rsb(0), ALU.is_ge),
         reads=[sm, rs], writes=[r1])
    P.do(v, lambda e: e.scalar_tensor_tensor(v4(r1), v4(r1), -999.0, iota4b,
                                             ALU.mult, ALU.add),
         reads=[r1, C.iota8], writes=[r1])
    P.do(v, lambda e: e.tensor_reduce(rsv(1), v4(r1), axis=AX.X, op=ALU.min),
         reads=[r1], writes=[rs])
    P.do(v, lambda e: e.tensor_scalar_add(rsv(1), rsv(1), 999.0), reads=[rs], writes=[rs])
    P.do(v, lambda e: e.tensor_tensor(v4(r1), iota4b, rsb(1), ALU.is_equal),
         reads=[C.iota8, rs], writes=[r1])
    P.do(v, lambda e: e.scalar_tensor_tensor(v4(r2), v4(r1), -1e30, v4(sm),
                                             ALU.mult, ALU.add),
         reads=[r1, sm], writes=[r2])
    P.do(v, lambda e: e.tensor_reduce(rsv(2), v4(r2), axis=AX.X, op=ALU.max),
         reads=[r2], writes=[rs])
    P.do(v, lambda e: e.tensor_tensor(v4(r2), v4(r2), rsb(2), ALU.is_ge),
         reads=[r2, rs], writes=[r2])
    P.do(v, lambda e: e.scalar_tensor_tensor(v4(r2), v4(r2), -999.0, iota4b,
                                             ALU.mult, ALU.add),
         reads=[r2, C.iota8], writes=[r2])
    P.do(v, lambda e: e.tensor_reduce(rsv(3), v4(r2), axis=AX.X, op=ALU.min),
         reads=[r2], writes=[rs])
    P.do(v, lambda e: e.tensor_scalar_add(rsv(3), rsv(3), 999.0), reads=[rs], writes=[rs])
    P.do(v, lambda e: e.tensor_tensor(v4(r2), iota4b, rsb(3), ALU.is_equal),
         reads=[C.iota8, rs], writes=[r2])
    P.do(v, lambda e: e.tensor_tensor(rsv(3), rsv(0), rsv(2), ALU.add),
         reads=[rs], writes=[rs])
    P.do(v, lambda e: e.tensor_scalar_add(rsv(3), rsv(3), 1e-9), reads=[rs], writes=[rs])
    P.do(v, lambda e: e.reciprocal(rsv(3), rsv(3)), reads=[rs], writes=[rs])
    P.do(v, lambda e: e.tensor_tensor(rsv(0), rsv(0), rsv(3), ALU.mult),
         reads=[rs], writes=[rs])
    P.do(v, lambda e: e.tensor_tensor(rsv(2), rsv(2), rsv(3), ALU.mult),
         reads=[rs], writes=[rs])
    P.do(v, lambda e: e.tensor_tensor(v4(r1), v4(r1), rsb(0), ALU.mult),
         reads=[r1, rs], writes=[r1])
    P.do(v, lambda e: e.tensor_tensor(v4(r2), v4(r2), rsb(2), ALU.mult),
         reads=[r2, rs], writes=[r2])
    P.do(v, lambda e: e.tensor_tensor(r1[:], r1[:], r2[:], ALU.add),
         reads=[r1, r2], writes=[r1])
    P.do("act", lambda s: s.copy(E.mi[:], E.pb_r[:, 16:48]), reads=[E.pb_r], writes=[E.mi])
    mic = E.mi[:].rearrange("p (tc e) -> p tc e", e=8)
    P.do(v, lambda e: e.tensor_tensor(v4(r2), mic[:, :, 0:8:2], mic[:, :, 1:8:2],
                                      ALU.subtract),
         reads=[E.mi], writes=[r2])
    P.do("act", lambda s: s.activation(r2[:], r2[:], AF.Sigmoid), reads=[r2], writes=[r2])
    P.do(v, lambda e: e.tensor_scalar(r3[:], r2[:], -1.0, 1.0, ALU.mult, ALU.add),
         reads=[r2], writes=[r3])
    P.do(v, lambda e: e.tensor_tensor(r4[:], r2[:], r3[:], ALU.max),
         reads=[r2, r3], writes=[r4])
    P.do(v, lambda e: e.tensor_tensor(r3[:], r2[:], r3[:], ALU.is_ge),
         reads=[r2, r3], writes=[r3])
    P.do(v, lambda e: e.tensor_scalar_add(r2[:], r4[:], 1e-9), reads=[r4], writes=[r2])
    P.do(v, lambda e: e.reciprocal(r2[:], r2[:]), reads=[r2], writes=[r2])
    P.do(v, lambda e: e.tensor_tensor(r4[:], r4[:], r2[:], ALU.mult),
         reads=[r4, r2], writes=[r4])
    P.do(v, lambda e: e.tensor_tensor(r4[:], r1[:], r4[:], ALU.mult),
         reads=[r1, r4], writes=[r4])
    P.do(v, lambda e: e.tensor_scalar(r2[:], r3[:], -1.0, 1.0, ALU.mult, ALU.add),
         reads=[r3], writes=[r2])
    P.do(v, lambda e: e.tensor_tensor(v8(cw)[:, :, 0::2], v4(r4), v4(r3), ALU.mult),
         reads=[r4, r3], writes=[cw])
    P.do(v, lambda e: e.tensor_tensor(v8(cw)[:, :, 1::2], v4(r4), v4(r2), ALU.mult),
         reads=[r4, r2], writes=[cw])
    for tc in range(4):
        P.do("pe", lambda t, tc=tc: t.transpose(E.pb_t[0:8, 0:128], cw[:, tc * 8:tc * 8 + 8],
                                                C.ident[:]),
             reads=[cw, C.ident], writes=[E.pb_t])
        P.do(v, lambda e, tc=tc: e.tensor_copy(cwT[0:8, tc * 128:(tc + 1) * 128],
                                               E.pb_t[0:8, 0:128]),
             reads=[E.pb_t], writes=[cwT])


def emit_phase_l1ffn_loop(P, C, E, loop, x_in, zT, znT, x_out, w1bf, w2bf,
                          tiles=range(NTILES)):
    tiles = list(tiles)

    def preamble_dma(t, s):
        emit_dma_tile(P, "pool_dma", E.z[s], zT, t)
        P.do("sp_dma", lambda e, t=t, s=s: e.dma_start(
            out=E.zn[s][:].rearrange("p (c n) -> p c n", n=NT),
            in_=znT.rearrange("(c p) n -> p c n", p=128)[:, :, t * NT:(t + 1) * NT]),
            reads=[(znT.tensor, t)], writes=[(E.zn[s], c) for c in range(DC)])

    def preamble_route(t, s):
        zn, cwT = E.zn[s], E.cwT[s]
        for tc in range(4):
            P.do("pe", emit_linear(
                P, lambda tc=tc: E.pb_r[:, tc * 4:(tc + 1) * 4],
                lambda k, tc=tc: zn[:, k * NT + tc * 128:k * NT + tc * 128 + 128],
                lambda k: E.wmacT[:, k * 4:(k + 1) * 4], DC, None),
                reads=[(zn, c2) for c2 in range(DC)] + [E.wmacT], writes=[E.pb_r])
            P.do("pe", emit_linear(
                P, lambda tc=tc: E.pb_r[:, 16 + tc * 8:16 + (tc + 1) * 8],
                lambda k, tc=tc: zn[:, k * NT + tc * 128:k * NT + tc * 128 + 128],
                lambda k: E.wmicT[:, k * 8:(k + 1) * 8], DC, None),
                reads=[(zn, c2) for c2 in range(DC)] + [E.wmicT], writes=[E.pb_r])
        emit_routing_l1(P, C, E, cwT)

    def experts_pair(pair, nxt_pair):
        w1 = E.w1sb
        for e in range(8):
            for wh in range(2):
                P.do("sp_dma", lambda en, e=e, wh=wh: en.dma_start(
                    out=w1[:, wh * 8192:(wh + 1) * 8192],
                    in_=w1bf[e][:, wh * 8192:(wh + 1) * 8192]),
                    reads=[(w1bf.tensor, e)], writes=[(w1, wh)])
            P.do("act_dma", lambda en, e=e: en.dma_start(out=E.w2sb[:], in_=w2bf[e]),
                 reads=[(w2bf.tensor, e)], writes=[E.w2sb])
            for pi, (t, s) in enumerate(pair):
                zn, cwT, y = E.zn[s], E.cwT[s], E.y[s]
                first = (e == 0)

                def bcast(e=e, cwT=cwT, pi=pi):
                    bc = E.pb_bc if (e + pi) % 2 == 0 else E.pb_bc2
                    P.do("pe", lambda t_: t_.matmul(
                        bc[:], C.sel8[:, e * 128:(e + 1) * 128],
                        cwT[:], start=True, stop=True),
                        reads=[cwT, C.sel8], writes=[bc])
                    P.do("act", lambda s_: s_.copy(E.cwB[:], bc[:]),
                         reads=[bc], writes=[E.cwB])

                def scale(fc):
                    P.do("dve", lambda v, fc=fc: v.tensor_tensor(
                        E.hs[:, fc * NT:(fc + 1) * NT], E.hs[:, fc * NT:(fc + 1) * NT],
                        E.cwB[:], ALU.mult),
                        reads=[(E.hs, fc), E.cwB], writes=[(E.hs, fc)])
                if not first:
                    bcast()
                for fc in range(16):
                    pb = E.pb_h[fc % 2]

                    def hmm(k0, k1, fc=fc, pb=pb, zn=zn):
                        def f(t_):
                            last = None
                            for k in range(k0, k1):
                                last = t_.matmul(
                                    pb[:], w1[:, k * 2048 + fc * 128:k * 2048 + fc * 128 + 128],
                                    zn[:, k * NT:(k + 1) * NT],
                                    start=(k == 0), stop=(k == DC - 1))
                            return last
                        return f
                    P.do("pe", hmm(0, 4),
                         reads=[(zn, c2) for c2 in range(4)] + [(w1, 0)], writes=[pb])
                    P.do("pe", hmm(4, 8),
                         reads=[(zn, c2) for c2 in range(4, 8)] + [(w1, 1)], writes=[pb])
                    if first and pi == 0 and fc == 0:
                        for t2, s2 in pair:
                            preamble_route(t2, s2)
                    if first and fc == 8:
                        bcast()
                    P.do("act", lambda s_, fc=fc, pb=pb: s_.activation(
                        E.hs[:, fc * NT:(fc + 1) * NT], pb[:], AF.Silu),
                        reads=[pb], writes=[(E.hs, fc)])
                    if not first or fc >= 8:
                        scale(fc)
                        if first and fc >= 8:
                            scale(fc - 8)
                if first:
                    pass
                for co in range(8):
                    pb = E.pb_y[co % 2]

                    def ymm(k0, k1, co=co, pb=pb):
                        def f(t_):
                            last = None
                            for k in range(k0, k1):
                                last = t_.matmul(
                                    pb[:],
                                    E.w2sb[:, k * 1024 + co * 128:k * 1024 + co * 128 + 128],
                                    E.hs[:, k * NT:(k + 1) * NT],
                                    start=(k == 0), stop=(k == 15))
                            return last
                        return f
                    P.do("pe", ymm(0, 14),
                         reads=[(E.hs, k) for k in range(14)] + [E.w2sb], writes=[pb])
                    P.do("pe", ymm(14, 16),
                         reads=[(E.hs, 14), (E.hs, 15), E.w2sb], writes=[pb])
                    if e == 0:
                        P.do("act", lambda s_, co=co, pb=pb, y=y: s_.copy(
                            y[:, co * NT:(co + 1) * NT], pb[:]),
                            reads=[pb], writes=[(y, co)])
                    else:
                        P.do("dve", lambda v, co=co, pb=pb, y=y: v.tensor_tensor(
                            y[:, co * NT:(co + 1) * NT], y[:, co * NT:(co + 1) * NT],
                            pb[:], ALU.add),
                            reads=[(y, co), pb], writes=[(y, co)])

    def finals(t, s):
        z = E.z[s]
        y = E.y[s]
        if loop == 0:
            zn = E.zn[s]
            zbk = y
            nt = NT
            rw = E.rows
            v2 = rw[0:1, (3 * s + 1) * nt:(3 * s + 1) * nt + nt]
            inv = rw[0:1, (3 * s + 2) * nt:(3 * s + 2) * nt + nt]
            P.do("dve", lambda v, y=y: v.tensor_tensor(z[:], z[:], y[:], ALU.add),
                 reads=[z] + [(z, c) for c in range(DC)] + [(y, co) for co in range(8)],
                 writes=[z] + [(z, c) for c in range(DC)])
            emit_dma_tile(P, "sp_dma", z, zT, t, write=True)
            P.do("dve", lambda v, zbk=zbk: v.tensor_tensor(zbk[:], z[:], z[:], ALU.mult),
                 reads=[z], writes=[(zbk, co) for co in range(8)])
            P.do("pe", emit_linear(P, lambda: E.pb_r[64:65, 0:nt], lambda k: C.onesc[:],
                                   lambda k, zbk=zbk: zbk[:, k * nt:(k + 1) * nt], DC, None),
                 reads=[(zbk, co) for co in range(8)] + [C.onesc], writes=[E.pb_r])
            P.do("dve", lambda v: v.tensor_scalar(v2, E.pb_r[64:65, 0:nt],
                                                  1.0 / D, 1e-6, ALU.mult, ALU.add),
                 reads=[E.pb_r], writes=[rw])
            P.do("act", lambda s_: s_.activation(inv, v2, AF.Ln), reads=[rw], writes=[rw])
            P.do("act", lambda s_: s_.activation(inv, inv, AF.Exp, scale=-0.5),
                 reads=[rw], writes=[rw])
            P.do("pe", lambda t_: t_.matmul(E.pb_bc[:, 0:nt], C.onesr[:], inv,
                                            start=True, stop=True),
                 reads=[rw, C.onesr], writes=[E.pb_bc])
            P.do("act", lambda s_: s_.copy(E.aB[:], E.pb_bc[:, 0:nt]),
                 reads=[E.pb_bc], writes=[E.aB])
            for c in range(DC):
                sl = slice(c * NT, (c + 1) * NT)
                P.do("dve", lambda v, sl=sl, zn=zn: v.tensor_tensor(
                    zn[:, sl], z[:, sl], E.aB[:], ALU.mult),
                    reads=[z, E.aB], writes=[(zn, c)])
                P.do("act", lambda s_, c=c, sl=sl, zn=zn: s_.activation(
                    zn[:, sl], zn[:, sl], AF.Identity, scale=E.nw[:, c:c + 1]),
                    reads=[(zn, c), E.nw], writes=[(zn, c)])
            P.do("sp_dma", lambda e, t=t, zn=zn: e.dma_start(
                out=znT.rearrange("(c p) n -> p c n", p=128)[:, :, t * NT:(t + 1) * NT],
                in_=zn[:].rearrange("p (c n) -> p c n", n=NT)),
                reads=[(zn, c) for c in range(DC)], writes=[(znT.tensor, t)])
        else:
            emit_dma_tile(P, "pool_dma", E.xr, x_in, t)
            P.do("dve", lambda v, y=y: v.tensor_tensor(z[:], z[:], y[:], ALU.add),
                 reads=[z] + [(z, c) for c in range(DC)] + [(y, co) for co in range(8)],
                 writes=[z] + [(z, c) for c in range(DC)])
            P.do("dve", lambda v: v.tensor_tensor(z[:], z[:], E.xr[:], ALU.add),
                 reads=[z, E.xr], writes=[z] + [(z, c) for c in range(DC)])
            emit_dma_tile(P, "sp_dma", z, x_out, t, write=True)

    assert len(tiles) % 2 == 0
    for i in range(0, len(tiles), 2):
        pair = [(tiles[i], 0), (tiles[i + 1], 1)]
        for t, s in pair:
            preamble_dma(t, s)
        experts_pair(pair, None)
        for t, s in pair:
            finals(t, s)

FUSED = True


def host_prep(inputs):
    g = {}
    ac = np.ascontiguousarray
    g["WqT0"], g["WkT0"], g["WvT0"], g["WoT0"] = (
        inputs["L0_attn_Wq"], inputs["L0_attn_Wk"],
        inputs["L0_attn_Wv"], inputs["L0_attn_Wo"])
    ab = np.zeros((128, 32), np.float32)
    for i, n in enumerate(("bq", "bk", "bv", "bo")):
        ab[:, i * 8:(i + 1) * 8] = inputs[f"L0_attn_{n}"].reshape(8, 128).T
    g["abias"] = ab
    g["gateT"] = ac(inputs["L0_ffn_gate"].T)
    g["corr4"] = np.tile(inputs["L0_ffn_corr_bias"], (128, 4)).astype(np.float32)
    g["iota32"] = np.tile(np.arange(8, dtype=np.float32), (128, 4))
    g["fc1T"] = ac(inputs["L0_ffn_fc1_w"].T)
    g["fc2T"] = ac(inputs["L0_ffn_fc2_w"].T)
    g["w1Tl0"] = ac(inputs["L0_ffn_w1"].transpose(0, 2, 1))
    g["w2Tl0"] = ac(inputs["L0_ffn_w2"].transpose(0, 2, 1))
    g["upT"] = ac(inputs["L0_ffn_up_w"].T)
    g["downT"] = ac(inputs["L0_ffn_down_w"].T)
    g["WqT1"] = ac(inputs["L1_attn_Wq"].T)
    g["WkT1"] = ac(inputs["L1_attn_Wk"].T)
    g["WvT1"] = ac(inputs["L1_attn_Wv"].T)
    g["WoT1"] = ac(inputs["L1_attn_Wo"].T)
    g["wmacT"] = ac(inputs["L1_ffn_w_macro"].T)
    g["wmicT"] = ac(inputs["L1_ffn_w_micro"].reshape(8, 1024).T)
    g["nw"] = ac(inputs["L1_ffn_norm_w"].reshape(8, 128).T)
    g["w1Tl1"] = ac(inputs["L1_ffn_w1"].transpose(0, 2, 1))
    g["w2Tl1"] = ac(inputs["L1_ffn_w2"].transpose(0, 2, 1))
    return g


_W_SPECS = {
    "WqT0": [1024, 1024], "WkT0": [1024, 1024], "WvT0": [1024, 1024],
    "WoT0": [1024, 1024], "abias": [128, 32],
    "gateT": [1024, 8], "corr4": [128, 32], "iota32": [128, 32],
    "fc1T": [1024, 512], "fc2T": [512, 1024],
    "w1Tl0": [8, 512, 512], "w2Tl0": [8, 512, 512],
    "upT": [1024, 4096], "downT": [4096, 1024],
    "WqT1": [1024, 1024], "WkT1": [1024, 1024], "WvT1": [1024, 1024],
    "WoT1": [1024, 1024],
    "wmacT": [1024, 4], "wmicT": [1024, 8], "nw": [128, 8],
    "w1Tl1": [8, 1024, 2048], "w2Tl1": [8, 2048, 1024],
}


class _Scoped:

    def __init__(self, P):
        self.P = P

    def __enter__(self):
        import contextlib
        self.outer = self.P.es
        self.inner = contextlib.ExitStack()
        self.inner.__enter__()
        self.P.es = self.inner
        return self

    def __exit__(self, *a):
        self.P.es = self.outer
        return self.inner.__exit__(*a)


def _decl_inputs(nc, names):
    hb = {}
    for n in names:
        hb[n] = nc.dram_tensor(n, _W_SPECS[n], F32, kind="ExternalInput")
    return hb


def _emit_attn_consts(P, nc, B, hb, rope):
    h_masks = nc.inline_tensor(attn_masks_np(), name=_uname("h_masks"))
    P.do("sp_dma", lambda e: e.dma_start(out=B.masks[:], in_=h_masks.ap()),
         writes=[B.masks])
    if rope:
        cosn, sinn, permn = rope_tables_np()
        hc = nc.inline_tensor(cosn, name="h_cos")
        hs = nc.inline_tensor(sinn, name="h_sin")
        hp = nc.inline_tensor(permn, name="h_perm")
        P.do("sp_dma", lambda e: e.dma_start(out=B.cosT[:], in_=hc.ap()), writes=[B.cosT])
        P.do("sp_dma", lambda e: e.dma_start(out=B.sinA[:], in_=hs.ap()), writes=[B.sinA])
        P.do("sp_dma", lambda e: e.dma_start(out=B.perm[:], in_=hp.ap()), writes=[B.perm])
    else:
        P.do("sp_dma", lambda e: e.dma_start(out=B.bias[:], in_=hb["abias"].ap()),
             writes=[B.bias])


def build_full(n_launch_phases=None):
    import contextlib
    progs = []

    def new_nc():
        return bass.Bass("TRN2", target_bir_lowering=False, debug=False)

    if FUSED:
        nc = new_nc()
        hb = _decl_inputs(nc, list(_W_SPECS))
        xin = nc.dram_tensor("xT", [1024, NTOK], F32, kind="ExternalInput")
        yout = nc.dram_tensor("yT", [1024, NTOK], F32, kind="ExternalOutput")
        xr = [nc.dram_tensor(f"xr{i}", [1024, NTOK], F32, kind="Internal")
              for i in range(5)]
        xnsp = nc.dram_tensor("xnsp", [1024, NTOK], BF16, kind="Internal")
        zT = nc.dram_tensor("zT", [1024, NTOK], F32, kind="Internal")
        znT = nc.dram_tensor("znT", [1024, NTOK], BF16, kind="Internal")
        w1bf = nc.dram_tensor("w1bf", [8, 128, 16384], BF16, kind="Internal")
        w2bf = nc.dram_tensor("w2bf", [8, 128, 16384], BF16, kind="Internal")
        upbf = nc.dram_tensor("upbf", [2, 128, 16384], BF16, kind="Internal")
        dnbf = nc.dram_tensor("dnbf", [2, 128, 16384], BF16, kind="Internal")
        w4bf = nc.dram_tensor("w4bf", [4, 128, 8192], BF16, kind="Internal")
        es = contextlib.ExitStack()
        with es:
            P = Prog(nc, es)
            C = Consts(P)
            with _Scoped(P):
                B = AttnBufs(P, False)
                _emit_attn_consts(P, nc, B, hb, False)
                emit_phase_attn(P, C, B, xin.ap(), xr[0].ap(), hb["WqT0"].ap(),
                                hb["WkT0"].ap(), hb["WvT0"].ap(), hb["WoT0"].ap())
                P.barrier()
            P.recycle_dma_sems()
            with _Scoped(P):
                B = L0FFNBufs(P)
                cb1 = sb(P, "cvt1", [128, 16384], BF16)
                cb2 = cb1
                load_l0ffn_weights(P, B, {"gateT": hb["gateT"], "fc1T": hb["fc1T"],
                                          "w1T": hb["w1Tl0"], "w2T": hb["w2Tl0"],
                                          "fc2T": hb["fc2T"], "corr4": hb["corr4"],
                                          "iota32": hb["iota32"]})
                emit_weight_convert_l1(P, cb1, cb2, hb["w1Tl1"].ap(), hb["w2Tl1"].ap(),
                                       w1bf.ap(), w2bf.ap())
                for half in range(2):
                    emit_stage_bf16(P, cb1, hb["upT"].ap()[:, half * 2048:(half + 1) * 2048],
                                    upbf.ap()[half], (upbf, half))
                    emit_stage_bf16(P, cb1, hb["downT"].ap()[half * 2048:(half + 1) * 2048, :],
                                    dnbf.ap()[half], (dnbf, half))
                for wi, wn in enumerate(("WqT1", "WkT1", "WvT1", "WoT1")):
                    emit_stage_bf16(P, cb1, hb[wn].ap(),
                                    w4bf.ap()[wi], (w4bf, wi))
                emit_phase_l0ffn_routed(P, C, B, xr[0].ap(), xr[1].ap(), xnsp.ap())
                P.barrier()
            P.recycle_dma_sems()
            for half in (0, 1):
                with _Scoped(P):
                    B = SharedBufs(P)
                    load_wT(P, B.upT, upbf.ap()[half])
                    load_wT(P, B.downT, dnbf.ap()[half])
                    emit_phase_l0ffn_shared(P, C, B, xr[1 + half].ap(),
                                            xr[2 + half].ap(), xnsp.ap(), half)
                    P.barrier()
                P.recycle_dma_sems()
            with _Scoped(P):
                B = AttnBufs(P, True)
                _emit_attn_consts(P, nc, B, hb, True)
                nw_d = sb(P, "d_nw", [128, DC], F32)
                P.do("sp_dma", lambda e: e.dma_start(out=nw_d[:], in_=hb["nw"].ap()),
                     writes=[nw_d])
                emit_phase_attn(P, C, B, xr[3].ap(), xr[4].ap(), w4bf.ap()[0],
                                w4bf.ap()[1], w4bf.ap()[2], w4bf.ap()[3],
                                zprep=(zT.ap(), znT.ap(), nw_d))
                P.barrier()
            P.recycle_dma_sems()
            with _Scoped(P):
                E = L1FFNBufs(P)
                load_wT(P, E.wmacT, hb["wmacT"].ap())
                load_wT(P, E.wmicT, hb["wmicT"].ap())
                P.do("sp_dma", lambda e: e.dma_start(out=E.nw[:], in_=hb["nw"].ap()),
                     writes=[E.nw])
                emit_phase_l1ffn_loop(P, C, E, 0, xr[4].ap(), zT.ap(), znT.ap(), None,
                                      w1bf.ap(), w2bf.ap())
                emit_phase_l1ffn_loop(P, C, E, 1, xr[4].ap(), zT.ap(), znT.ap(), yout.ap(),
                                      w1bf.ap(), w2bf.ap())
            P.finish()
        ins = list(_W_SPECS) + ["xT"]
        progs.append((nc, ins, ["yT"]))
        return progs

    def mk_io(nc, xname="xT", xdt=F32):
        xin = nc.dram_tensor(xname, [1024, NTOK], xdt, kind="ExternalInput")
        xo = nc.dram_tensor("xo", [1024, NTOK], F32, kind="ExternalOutput")
        return xin, xo

    nc = new_nc()
    hb = _decl_inputs(nc, ["WqT0", "WkT0", "WvT0", "WoT0", "abias"])
    xin, xo = mk_io(nc)
    es = contextlib.ExitStack()
    with es:
        P = Prog(nc, es)
        C = Consts(P)
        B = AttnBufs(P, False)
        _emit_attn_consts(P, nc, B, hb, False)
        emit_phase_attn(P, C, B, xin.ap(), xo.ap(), hb["WqT0"].ap(), hb["WkT0"].ap(),
                        hb["WvT0"].ap(), hb["WoT0"].ap())
        P.finish()
    progs.append((nc, ["WqT0", "WkT0", "WvT0", "WoT0", "abias", "xT"], ["xo"]))

    nc = new_nc()
    hb = _decl_inputs(nc, ["gateT", "fc1T", "w1Tl0", "w2Tl0", "fc2T", "corr4", "iota32"])
    xin, xo = mk_io(nc)
    xnsp = nc.dram_tensor("xnsp", [1024, NTOK], BF16, kind="ExternalOutput")
    es = contextlib.ExitStack()
    with es:
        P = Prog(nc, es)
        C = Consts(P)
        B = L0FFNBufs(P)
        load_l0ffn_weights(P, B, {"gateT": hb["gateT"], "fc1T": hb["fc1T"],
                                  "w1T": hb["w1Tl0"], "w2T": hb["w2Tl0"],
                                  "fc2T": hb["fc2T"], "corr4": hb["corr4"],
                                  "iota32": hb["iota32"]})
        emit_phase_l0ffn_routed(P, C, B, xin.ap(), xo.ap(), xnsp.ap())
        P.finish()
    progs.append((nc, ["gateT", "fc1T", "w1Tl0", "w2Tl0", "fc2T", "corr4", "iota32",
                       "xT"], ["xo", "xnsp"]))

    for half in (0, 1):
        nc = new_nc()
        hb = _decl_inputs(nc, ["upT", "downT"])
        xin, xo = mk_io(nc)
        xnsp = nc.dram_tensor("xnsp", [1024, NTOK], BF16, kind="ExternalInput")
        es = contextlib.ExitStack()
        with es:
            P = Prog(nc, es)
            C = Consts(P)
            B = SharedBufs(P)
            load_shared_weights(P, B, hb, half)
            emit_phase_l0ffn_shared(P, C, B, xin.ap(), xo.ap(), xnsp.ap(), half)
            P.finish()
        progs.append((nc, ["upT", "downT", "xT", "xnsp"], ["xo"]))

    nc = new_nc()
    hb = _decl_inputs(nc, ["WqT1", "WkT1", "WvT1", "WoT1"])
    xin, xo = mk_io(nc)
    es = contextlib.ExitStack()
    with es:
        P = Prog(nc, es)
        C = Consts(P)
        B = AttnBufs(P, True)
        _emit_attn_consts(P, nc, B, hb, True)
        emit_phase_attn(P, C, B, xin.ap(), xo.ap(), hb["WqT1"].ap(), hb["WkT1"].ap(),
                        hb["WvT1"].ap(), hb["WoT1"].ap())
        P.finish()
    progs.append((nc, ["WqT1", "WkT1", "WvT1", "WoT1", "xT"], ["xo"]))

    nc = new_nc()
    hb = _decl_inputs(nc, ["wmacT", "wmicT", "nw", "w1Tl1", "w2Tl1"])
    xin, xo = mk_io(nc)
    zT = nc.dram_tensor("zT", [1024, NTOK], F32, kind="Internal")
    w1bf = nc.dram_tensor("w1bf", [8, 128, 16384], BF16, kind="Internal")
    w2bf = nc.dram_tensor("w2bf", [8, 128, 16384], BF16, kind="Internal")
    es = contextlib.ExitStack()
    with es:
        P = Prog(nc, es)
        C = Consts(P)
        E = L1FFNBufs(P)
        load_wT(P, E.wmacT, hb["wmacT"].ap())
        load_wT(P, E.wmicT, hb["wmicT"].ap())
        P.do("sp_dma", lambda e: e.dma_start(out=E.nw[:], in_=hb["nw"].ap()),
             writes=[E.nw])
        emit_weight_convert_l1(P, E.w1sb, E.w2sb, hb["w1Tl1"].ap(), hb["w2Tl1"].ap(),
                               w1bf.ap(), w2bf.ap())
        znT2 = nc.dram_tensor("znT", [1024, NTOK], BF16, kind="Internal")
        emit_phase_l1ffn_loop(P, C, E, 0, xin.ap(), zT.ap(), znT2.ap(), None,
                              w1bf.ap(), w2bf.ap())
        emit_phase_l1ffn_loop(P, C, E, 1, xin.ap(), zT.ap(), znT2.ap(), xo.ap(),
                              w1bf.ap(), w2bf.ap())
        P.finish()
    progs.append((nc, ["wmacT", "wmicT", "nw", "w1Tl1", "w2Tl1", "xT"], ["xo"]))
    return progs


_CACHED = {"progs": None, "fused": None}


def kernel(**inputs):
    g = host_prep(inputs)
    x = inputs["L0_attn_x"]
    xT = [np.ascontiguousarray(x[4 * c:4 * c + 4].transpose(2, 0, 1).reshape(1024, NTOK))
          for c in range(8)]
    if _CACHED["progs"] is None or _CACHED["fused"] != FUSED:
        _CACHED["progs"] = build_full()
        _CACHED["fused"] = FUSED
    progs = _CACHED["progs"]
    state = {c: {"xT": xT[c]} for c in range(8)}
    for nc, ins, outs in progs:
        in_maps = []
        for c in range(8):
            m = {}
            for n in ins:
                if n in g:
                    m[n] = g[n]
                else:
                    m[n] = state[c][n]
            in_maps.append(m)
        res = run_bass_kernel_spmd(nc, in_maps, core_ids=list(range(8)))
        for c in range(8):
            r = res.results[c]
            if "yT" in r:
                state[c]["xT"] = r["yT"]
            elif "xo" in r:
                state[c]["xT"] = r["xo"]
            if "xnsp" in r:
                state[c]["xnsp"] = r["xnsp"]
    out = np.empty((32, 2048, 1024), dtype=np.float32)
    for c in range(8):
        out[4 * c:4 * c + 4] = np.asarray(state[c]["xT"], dtype=np.float32).reshape(
            1024, 4, 2048).transpose(1, 2, 0)
    return out
